# version 05
# baseline (speedup 1.0000x reference)
import contextlib
import os
import numpy as np
import ml_dtypes
import concourse.bass as bass
import concourse.mybir as mybir
from concourse.bass_utils import run_bass_kernel_spmd

F32 = mybir.dt.float32
BF16 = mybir.dt.bfloat16
FP8 = mybir.dt.float8e4
ALU = mybir.AluOpType
AF = mybir.ActivationFunctionType
AX = mybir.AxisListType

D = 2048
S = 2048
NH = 16
DK = 128
II = 8192
SEG = 1024
NSEG = II // SEG
TOK = 1024
NTB = TOK // 128
NKC = D // 128
NXT = S // 128
NSP = 16
EPS = 1e-5
RSD = float(1.0 / np.sqrt(np.float64(DK)))
INV448H = 0.5 / 448.0


class Sem:
    __slots__ = ("h", "n")

    def __init__(self, nc, es, name):
        self.h = es.enter_context(nc.semaphore(name))
        self.n = 0

    def inc(self, ins, k=1):
        ins.then_inc(self.h, k)
        self.n += k
        return ins

    def wait(self, eng, v):
        eng.wait_ge(self.h, v)


class Fence:
    __slots__ = ("h", "n")

    def __init__(self, nc, es, name):
        self.h = es.enter_context(nc.semaphore(name))
        self.n = 0

    def cut(self, e, ins):
        ins.then_inc(self.h, 1)
        self.n += 1
        e.wait_ge(self.h, self.n)
        return ins


class Stream:

    def __init__(self, nc, es, name, k=3):
        self.sems = [es.enter_context(nc.semaphore(f"{name}{j}"))
                     for j in range(k)]
        self.k = k
        self.i = 0

    @property
    def n(self):
        return 16 * self.i

    def dma(self, e, out, in_):
        i = self.i
        j = i % self.k
        if i >= self.k:
            e.wait_ge(self.sems[j], 16 * (i // self.k))
        ins = e.dma_start(out=out, in_=in_)
        ins.then_inc(self.sems[j], 16)
        self.i += 1
        return ins

    def wait(self, e, v):
        assert v % 16 == 0
        n = v // 16
        for idx in range(max(0, n - self.k), n):
            e.wait_ge(self.sems[idx % self.k], 16 * (idx // self.k + 1))


class B_:
    pass


def build_nc():
    STOP = int(os.environ.get("K_STOP", "99"))
    SUB = os.environ.get("K_SUB", "full")
    z = B_()
    nc = bass.Bass("TRN2", target_bir_lowering=False, debug=False)
    z.nc = nc
    inp = {}
    for nm, shp in [("xf", [S, D]), ("wqT", [D, D]), ("wkT", [D, D]),
                    ("wvT", [D, D]), ("woT", [D, D]), ("gwT", [D, II]),
                    ("uwT", [D, II]), ("dwT", [II, D]),
                    ("sint", [128, S]), ("cost", [128, S])]:
        inp[nm] = nc.dram_tensor(nm, shp, F32, kind="ExternalInput").ap()
    inp["trimask"] = nc.dram_tensor("trimask", [128, 128], BF16, kind="ExternalInput").ap()
    inp["parf"] = nc.dram_tensor("parf", [128, 1], F32, kind="ExternalInput").ap()
    inp["ident"] = nc.dram_tensor("ident", [128, 128], BF16, kind="ExternalInput").ap()
    out_own = nc.dram_tensor("out_own", [TOK, D], F32, kind="ExternalOutput").ap()
    d_k = nc.dram_tensor("d_k", [NH, 128, S], BF16, kind="Internal").ap()
    d_q = nc.dram_tensor("d_q", [NH, 128, TOK], BF16, kind="Internal").ap()
    d_v = nc.dram_tensor("d_v", [S, D], BF16, kind="Internal").ap()
    d_xn = nc.dram_tensor("d_xn", [NTB, 128, D], F32, kind="Internal").ap()
    d_xq = nc.dram_tensor("d_xq", [NTB, 128, NKC * 128], BF16, kind="Internal").ap()
    d_acc = nc.dram_tensor("d_acc", [NTB, 128, D], F32, kind="Internal").ap()

    es = contextlib.ExitStack()
    with es:
        A = es.enter_context(nc.sbuf_tensor("Ar", [128, 32768], BF16))
        Bf = es.enter_context(nc.sbuf_tensor("Br", [128, 24576], BF16))
        C = es.enter_context(nc.sbuf_tensor("Cr", [128, 4096], F32))
        W = es.enter_context(nc.sbuf_tensor("Wr", [128, 4096], F32))
        SM = es.enter_context(nc.sbuf_tensor("SMr", [128, 2048], F32))
        SCL = es.enter_context(nc.sbuf_tensor("SCLr", [128, 6144], F32))
        SC2 = es.enter_context(nc.sbuf_tensor("SC2r", [128, 4096], BF16))
        Q8 = es.enter_context(nc.sbuf_tensor("Q8r", [128, 2048], FP8))
        TRI = es.enter_context(nc.sbuf_tensor("TRIr", [128, 128], BF16))
        PARF = es.enter_context(nc.sbuf_tensor("PARFr", [128, 1], F32))
        ONE1 = es.enter_context(nc.sbuf_tensor("ONE1r", [128, 1], BF16))
        EPSC = es.enter_context(nc.sbuf_tensor("EPSCr", [128, 1], F32))
        ONER = es.enter_context(nc.sbuf_tensor("ONERr", [1, 128], BF16))
        IDN = es.enter_context(nc.sbuf_tensor("IDNr", [128, 128], BF16))
        P32 = es.enter_context(nc.psum_tensor("P32r", [128, 4096], F32))
        P16 = P32[:, 3584:4096].bitcast(BF16)

        s_x = Stream(nc, es, "sx")
        s_p = Stream(nc, es, "sp")
        c_s = Sem(nc, es, "c_s")
        c_t = Sem(nc, es, "c_t")
        v_a = Sem(nc, es, "v_a")
        v_b = Sem(nc, es, "v_b")
        v_c = Sem(nc, es, "v_c")
        f_v = Fence(nc, es, "f_v")

        blk = es.enter_context(nc.Block())

        def _stop_tail():
            dump = os.environ.get("K_DUMP", "")

            def t_sync(e):
                for sm in (c_s, c_t, v_a, v_b, v_c):
                    e.wait_ge(sm.h, sm.n)
                s_p.wait(e, s_p.n)
                s_x.wait(e, s_x.n)
                if dump == "xn":
                    for j in range(8):
                        s_x.dma(e, out=out_own[j * 128:(j + 1) * 128, :],
                                in_=d_xn[j])
                elif dump == "acc":
                    for j in range(8):
                        s_x.dma(e, out=out_own[j * 128:(j + 1) * 128, :],
                                in_=d_acc[j])
                elif dump == "q":
                    for j in range(8):
                        s_x.dma(e, out=out_own[j * 128:(j + 1) * 128, 0:512]
                                .bitcast(BF16), in_=d_q[2 * j])
                elif dump == "k":
                    for j in range(8):
                        s_x.dma(e, out=out_own[j * 128:(j + 1) * 128, 0:1024]
                                .bitcast(BF16), in_=d_k[2 * j])
                elif dump == "v":
                    for j in range(8):
                        s_x.dma(e, out=out_own[j * 128:(j + 1) * 128, 0:1024]
                                .bitcast(BF16), in_=d_v[j * 128:(j + 1) * 128, :])
                elif dump == "xq":
                    for j in range(8):
                        s_x.dma(e, out=out_own[j * 128:(j + 1) * 128, 0:1024]
                                .bitcast(BF16), in_=d_xq[j])
                else:
                    s_x.dma(e, out=out_own[0:128, :], in_=SM[:, 0:2048])
                s_x.wait(e, s_x.n)
            blk.sync(t_sync)

        def p0_sync(e):
            s_x.dma(e, out=C[:, 0:2048], in_=inp["sint"])
            s_x.dma(e, out=C[:, 2048:4096], in_=inp["cost"])
            s_x.dma(e, out=TRI[:], in_=inp["trimask"])
            s_x.dma(e, out=PARF[:], in_=inp["parf"])
            s_x.dma(e, out=IDN[:], in_=inp["ident"])
        blk.sync(p0_sync)

        def p0_vec(e):
            e.memset(ONE1[:], 1.0)
            e.memset(EPSC[:], EPS)
            v_a.inc(e.memset(ONER[:], 1.0))
        blk.vector(p0_vec)

        if STOP == 0:
            _stop_tail()
            return nc

        bx, bva, bcs, bvb, bct, bvc, bsp = (s_x.n, v_a.n, c_s.n, v_b.n,
                                            c_t.n, v_c.n, s_p.n)

        def p1_sync(e):
            for i in range(NXT):
                if i >= 2:
                    v_b.wait(e, bvb + (i - 1))
                    if SUB not in ("ln", "tp") and i - 2 < NTB:
                        s_p.wait(e, bsp + 16 * (i - 1))
                s_x.dma(e, out=W[:, (i % 2) * 2048:(i % 2) * 2048 + 2048],
                            in_=inp["xf"][i * 128:(i + 1) * 128, :])
        blk.sync(p1_sync)

        def p1_evac(e, G):
            i, g = G // 2, G % 2
            c_t.wait(e, bct + G + 1)
            ins = None
            for t in range(8):
                kc = g * 8 + t
                ins = e.tensor_copy(
                    out=A[:, kc * 2048 + i * 128:kc * 2048 + (i + 1) * 128],
                    in_=P16[:, t * 128:(t + 1) * 128])
            v_c.inc(ins)

        def p1_vec(e):
            for i in range(NXT + 1):
                if i < NXT:
                    xt = W[:, (i % 2) * 2048:(i % 2) * 2048 + 2048]
                    st = (i % 2) * 8
                    s_x.wait(e, bx + 16 * (i + 1))
                    if SUB != "ln" and i >= 2:
                        c_t.wait(e, bct + 2 * (i - 2) + 2)
                    f_v.cut(e, e.tensor_reduce(out=SCL[:, st:st + 1], in_=xt,
                                               axis=AX.X, op=ALU.add))
                    f_v.cut(e, e.tensor_scalar(out=SCL[:, st + 1:st + 2],
                                               in0=SCL[:, st:st + 1],
                                               scalar1=1.0 / 2048.0,
                                               scalar2=None, op0=ALU.mult))
                    e.tensor_scalar(out=xt, in0=xt, scalar1=SCL[:, st + 1:st + 2],
                                    scalar2=None, op0=ALU.subtract)
                    e.tensor_tensor(out=SM[:, 0:2048], in0=xt, in1=xt, op=ALU.mult)
                    v_a.inc(e.tensor_reduce(out=SCL[:, st + 2:st + 3],
                                            in_=SM[:, 0:2048], axis=AX.X, op=ALU.add))
                    c_s.wait(e, bcs + (i + 1))
                    e.tensor_scalar(out=xt, in0=xt, scalar1=SCL[:, st + 4:st + 5],
                                    scalar2=None, op0=ALU.mult)
                    bb = Bf[:, (i % 2) * 2048:(i % 2) * 2048 + 2048]
                    v_b.inc(e.tensor_copy(out=bb, in_=xt))
                if SUB != "ln" and i >= 1:
                    p1_evac(e, 2 * (i - 1))
                    p1_evac(e, 2 * (i - 1) + 1)
        blk.vector(p1_vec)

        if SUB == "ln" and STOP == 1:
            def p1_scal0(e):
                for i in range(NXT):
                    st = (i % 2) * 8
                    v_a.wait(e, bva + (i + 1))
                    e.activation(out=SCL[:, st + 3:st + 4], in_=SCL[:, st + 2:st + 3],
                                 func=AF.Ln, bias=EPSC[:, 0:1], scale=1.0 / 2048.0)
                    c_s.inc(e.activation(out=SCL[:, st + 4:st + 5],
                                         in_=SCL[:, st + 3:st + 4],
                                         func=AF.Exp, scale=-0.5))
            blk.scalar(p1_scal0)
            _stop_tail()
            return nc

        def p1_scal(e):
            for i in range(NXT):
                st = (i % 2) * 8
                v_a.wait(e, bva + (i + 1))
                e.activation(out=SCL[:, st + 3:st + 4], in_=SCL[:, st + 2:st + 3],
                             func=AF.Ln, bias=EPSC[:, 0:1], scale=1.0 / 2048.0)
                c_s.inc(e.activation(out=SCL[:, st + 4:st + 5], in_=SCL[:, st + 3:st + 4],
                                     func=AF.Exp, scale=-0.5))
        blk.scalar(p1_scal)

        def p1_tens(e):
            for i in range(NXT):
                bb = Bf[:, (i % 2) * 2048:(i % 2) * 2048 + 2048]
                v_b.wait(e, bvb + (i + 1))
                for g in range(2):
                    if 2 * i + g >= 1:
                        v_c.wait(e, bvc + 2 * i + g)
                    ins = None
                    for t in range(8):
                        kc = g * 8 + t
                        ins = e.transpose(out=P16[:, t * 128:(t + 1) * 128],
                                          in_=bb[:, kc * 128:(kc + 1) * 128],
                                          identity=IDN[:])
                    c_t.inc(ins)
        blk.tensor(p1_tens)

        def p1_gp(e):
            for i in range(NTB):
                v_b.wait(e, bvb + (i + 1))
                s_p.dma(e, out=d_xn[i],
                            in_=W[:, (i % 2) * 2048:(i % 2) * 2048 + 2048]
                            )
        blk.gpsimd(p1_gp)

        if SUB == "full":
            assert v_a.n == bva + NXT and c_s.n == bcs + NXT
            assert v_b.n == bvb + NXT and c_t.n == bct + 2 * NXT
            assert v_c.n == bvc + 2 * NXT and s_p.n == bsp + 16 * NTB
        if STOP == 1:
            _stop_tail()
            return nc

        for wnm, kind in [("wkT", "K"), ("wvT", "V"), ("wqT", "Q")]:
            for half in range(2):
                bw, bcs, bct, bvb, bsp = s_x.n, c_s.n, c_t.n, v_b.n, s_p.n

                def p2_sync(e, wnm=wnm, half=half, bcs=bcs,
                            fvb=v_b.n, fsp=s_p.n):
                    e.wait_ge(v_b.h, fvb)
                    s_p.wait(e, fsp)
                    e.wait_ge(c_s.h, bcs)
                    for k in range(NKC):
                        if k >= 2:
                            c_s.wait(e, bcs + (k - 1))
                        s_x.dma(e, 
                            out=W[:, (k % 2) * 1024:(k % 2) * 1024 + 1024],
                            in_=inp[wnm][k * 128:(k + 1) * 128,
                                         half * 1024:half * 1024 + 1024]
                            )
                blk.sync(p2_sync)

                def p2_scal(e, bw=bw, fct=c_t.n):
                    e.wait_ge(c_t.h, fct)
                    for k in range(NKC):
                        s_x.wait(e, bw + 16 * (k + 1))
                        c_s.inc(e.activation(
                            out=Bf[:, k * 1024:(k + 1) * 1024],
                            in_=W[:, (k % 2) * 1024:(k % 2) * 1024 + 1024],
                            func=AF.Copy))
                blk.scalar(p2_scal)

                if kind in ("K", "Q"):
                    ntc = (S if kind == "K" else TOK) // 512

                    def p2_tens(e, ntc=ntc, bcs=bcs, bvb=bvb):
                        e.wait_ge(v_b.h, bvb)
                        for hh in range(8):
                            if hh >= 1:
                                v_b.wait(e, bvb + hh)
                            ins = None
                            for k in range(NKC):
                                c_s.wait(e, bcs + (k + 1))
                                lhs = Bf[:, k * 1024 + hh * 128:k * 1024 + (hh + 1) * 128]
                                for t in range(ntc):
                                    rhs = A[:, k * 2048 + t * 512:k * 2048 + (t + 1) * 512]
                                    ins = e.matmul(out=P32[:, t * 512:(t + 1) * 512],
                                                   lhsT=lhs, rhs=rhs,
                                                   start=(k == 0), stop=(k == NKC - 1))
                            c_t.inc(ins)
                    blk.tensor(p2_tens)

                    def p2_vec(e, ntc=ntc, bct=bct, bsp=bsp):
                        sin = C[:, 0:2048]
                        cos = C[:, 2048:4096]
                        for hh in range(8):
                            c_t.wait(e, bct + hh + 1)
                            if hh >= 2:
                                s_p.wait(e, bsp + 16 * (hh - 1))
                            stg = Bf[:, 16384 + (hh % 2) * 2048:
                                     16384 + (hh % 2) * 2048 + ntc * 512]
                            ins = None
                            for t in range(ntc):
                                ps = P32[:, t * 512:(t + 1) * 512]
                                tok0 = t * 512
                                s1 = SM[:, 0:512]
                                t2 = SM[:, 512:1024]
                                e.tensor_tensor(out=s1[0:64, :], in0=ps[64:128, :],
                                                in1=sin[0:64, tok0:tok0 + 512], op=ALU.mult)
                                e.tensor_tensor(out=s1[64:128, :], in0=ps[0:64, :],
                                                in1=sin[64:128, tok0:tok0 + 512], op=ALU.mult)
                                e.tensor_tensor(out=t2, in0=ps,
                                                in1=cos[:, tok0:tok0 + 512], op=ALU.mult)
                                ins = e.tensor_tensor(out=stg[:, t * 512:(t + 1) * 512],
                                                      in0=t2, in1=s1, op=ALU.add)
                            v_b.inc(ins)
                    blk.vector(p2_vec)

                    def p2_gp(e, kind=kind, half=half, ntc=ntc, bvb=bvb):
                        dst = d_k if kind == "K" else d_q
                        for hh in range(8):
                            v_b.wait(e, bvb + hh + 1)
                            stg = Bf[:, 16384 + (hh % 2) * 2048:
                                     16384 + (hh % 2) * 2048 + ntc * 512]
                            s_p.dma(e, out=dst[half * 8 + hh], in_=stg
                                        )
                    blk.gpsimd(p2_gp)
                    nu = 8
                else:
                    def p2_tens(e, bcs=bcs, bvb=bvb):
                        e.wait_ge(v_b.h, bvb)
                        for tb in range(NXT):
                            if tb >= 2:
                                v_b.wait(e, bvb + tb - 1)
                            pc = (tb % 2) * 1024
                            ins = None
                            for k in range(NKC):
                                c_s.wait(e, bcs + (k + 1))
                                lhs = A[:, k * 2048 + tb * 128:k * 2048 + (tb + 1) * 128]
                                for t in range(2):
                                    rhs = Bf[:, k * 1024 + t * 512:k * 1024 + (t + 1) * 512]
                                    ins = e.matmul(
                                        out=P32[:, pc + t * 512:pc + (t + 1) * 512],
                                        lhsT=lhs, rhs=rhs,
                                        start=(k == 0), stop=(k == NKC - 1))
                            c_t.inc(ins)
                    blk.tensor(p2_tens)

                    def p2_scal2(e, bct=bct, bsp=bsp):
                        for tb in range(NXT):
                            c_t.wait(e, bct + tb + 1)
                            if tb >= 2:
                                s_p.wait(e, bsp + 16 * (tb - 1))
                            pc = (tb % 2) * 1024
                            stg = Bf[:, 20480 + (tb % 2) * 1024:
                                     20480 + (tb % 2) * 1024 + 1024]
                            v_b.inc(e.activation(out=stg, in_=P32[:, pc:pc + 1024],
                                                 func=AF.Copy))
                    blk.scalar(p2_scal2)

                    def p2_gp(e, half=half, bvb=bvb):
                        for tb in range(NXT):
                            v_b.wait(e, bvb + tb + 1)
                            stg = Bf[:, 20480 + (tb % 2) * 1024:
                                     20480 + (tb % 2) * 1024 + 1024]
                            s_p.dma(e, out=d_v[tb * 128:(tb + 1) * 128,
                                                half * 1024:half * 1024 + 1024],
                                        in_=stg)
                    blk.gpsimd(p2_gp)
                    nu = NXT

                assert c_t.n == bct + nu and v_b.n == bvb + nu
                assert s_p.n == bsp + 16 * nu

        if STOP == 2:
            _stop_tail()
            return nc

        bsr, bct, bcs, bva, bvb, bvc = s_x.n, c_t.n, c_s.n, v_a.n, v_b.n, v_c.n
        sp_qb0 = [sp % 8 for sp in range(NSP)]
        sp_nq = [8 - q for q in sp_qb0]

        ct = {}
        n = bct
        for h in range(NH):
            n += 1; ct[("rows", h)] = n
            n += 1; ct[("bbc", h)] = n
            for sp in range(NSP):
                n += 1; ct[("sc", h, sp)] = n
                n += 1; ct[("av", h, sp)] = n
            n += 1; ct[("rbc", h)] = n
        ct_end = n
        cs = {}
        n = bcs
        for h in range(NH):
            n += 1; cs[("sq", h)] = n
            n += 1; cs[("bnd", h)] = n
            for sp in range(NSP):
                n += 1; cs[("exp", h, sp)] = n
        cs_end = n
        va = {}
        n = bva
        for h in range(NH):
            n += 1; va[("mkk", h)] = n
            n += 1; va[("bnd", h)] = n
            for sp in range(NSP):
                n += 1; va[("t", h, sp)] = n
                n += 1; va[("avadd", h, sp)] = n
        va_end = n
        vb = {}
        n = bvb
        for h in range(NH):
            n += 1; vb[("brow", h)] = n
            for sp in range(NSP):
                n += 1; vb[("P", h, sp)] = n
        vb_end = n
        vc = {}
        n = bvc
        for h in range(NH):
            n += 1; vc[("rec", h)] = n
            n += 1; vc[("ao", h)] = n
        vc_end = n

        def kcol(h):
            return (h % 2) * 2048

        def vvcol(h):
            return 4096 + (h % 2) * 2048

        def qcol(h):
            return 8192 + (h % 2) * 1024

        AACC = SCL[:, 4096:5120]
        DACC = SCL[0:1, 5120:6144]

        def p3_sync(e, fsp=s_p.n, fct=c_t.n, fcs=c_s.n):
            s_p.wait(e, fsp)
            e.wait_ge(c_t.h, fct)
            e.wait_ge(c_s.h, fcs)
            for h in range(NH):
                if h >= 2:
                    c_t.wait(e, ct[("av", h - 2, NSP - 1)])
                    c_s.wait(e, cs[("sq", h - 1)])
                s_x.dma(e, out=Bf[:, kcol(h):kcol(h) + 2048], in_=d_k[h])
                s_x.dma(e, out=Bf[:, vvcol(h):vvcol(h) + 2048].rearrange(
                    "p (sb j) -> p sb j", j=128),
                    in_=d_v[:, h * 128:(h + 1) * 128].rearrange(
                        "(sb p) j -> p sb j", p=128))
                s_x.dma(e, out=Bf[:, qcol(h):qcol(h) + 1024], in_=d_q[h])
        blk.sync(p3_sync)

        def p3_scal(e):
            for h in range(NH):
                s_x.wait(e, bsr + 48 * (h + 1))
                e.activation(out=Bf[:, 12288:14336],
                             in_=Bf[:, kcol(h):kcol(h) + 2048], func=AF.Square)
                c_s.inc(e.activation(out=Bf[:, 14336:15360],
                                     in_=Bf[:, qcol(h):qcol(h) + 1024],
                                     func=AF.Square))
                v_a.wait(e, va[("mkk", h)])
                c_s.inc(e.activation(out=SCL[0:1, 3072:4096],
                                     in_=SCL[0:1, 2048:3072], func=AF.Sqrt))
                for sp in range(NSP):
                    nq = sp_nq[sp]
                    v_a.wait(e, va[("t", h, sp)])
                    if sp >= 2:
                        c_t.wait(e, ct[("av", h, sp - 2)])
                    elif h >= 1:
                        c_t.wait(e, ct[("av", h - 1, NSP - 2 + sp)])
                    tst = SM[:, (sp % 2) * 1024:(sp % 2) * 1024 + nq * 128]
                    pst = Bf[:, 10240 + (sp % 2) * 1024:
                             10240 + (sp % 2) * 1024 + nq * 128]
                    c_s.inc(e.activation(out=pst, in_=tst, func=AF.Exp))
        blk.scalar(p3_scal)

        def p3_tens(e, fvb=v_b.n, fvc=v_c.n):
            e.wait_ge(v_b.h, fvb)
            e.wait_ge(v_c.h, fvc)
            for h in range(NH):
                c_s.wait(e, cs[("sq", h)])
                if h >= 1:
                    v_c.wait(e, vc[("ao", h - 1)])
                ins = None
                for t in range(4):
                    ins = e.matmul(out=P32[0:1, t * 512:(t + 1) * 512], lhsT=ONE1[:],
                                   rhs=Bf[:, 12288 + t * 512:12288 + (t + 1) * 512],
                                   start=True, stop=True)
                for t in range(2):
                    ins = e.matmul(out=P32[0:1, 2048 + t * 512:2048 + (t + 1) * 512],
                                   lhsT=ONE1[:],
                                   rhs=Bf[:, 14336 + t * 512:14336 + (t + 1) * 512],
                                   start=True, stop=True)
                c_t.inc(ins)
                v_b.wait(e, vb[("brow", h)])
                ins = None
                for t in range(2):
                    ins = e.matmul(out=P32[:, t * 512:(t + 1) * 512], lhsT=ONER[:],
                                   rhs=Bf[0:1, 15360 + t * 512:15360 + (t + 1) * 512],
                                   start=True, stop=True)
                c_t.inc(ins)
                for sp in range(NSP):
                    q0, nq = sp_qb0[sp], sp_nq[sp]
                    if sp < 2:
                        v_a.wait(e, va[("bnd", h)])
                    else:
                        v_a.wait(e, va[("avadd", h, sp - 2)])
                    ksl = Bf[:, kcol(h) + sp * 128:kcol(h) + (sp + 1) * 128]
                    pcol = (sp % 2) * 1024
                    ins = None
                    nmm = (nq * 128 + 511) // 512
                    for t in range(nmm):
                        w0 = t * 512
                        w1 = min(nq * 128, w0 + 512)
                        rhs = Bf[:, qcol(h) + q0 * 128 + w0:qcol(h) + q0 * 128 + w1]
                        ins = e.matmul(out=P32[:, pcol + w0:pcol + w1],
                                       lhsT=ksl, rhs=rhs, start=True, stop=True)
                    c_t.inc(ins)
                    v_b.wait(e, vb[("P", h, sp)])
                    vsl = Bf[:, vvcol(h) + sp * 128:vvcol(h) + (sp + 1) * 128]
                    pb = Bf[:, 10240 + (sp % 2) * 1024:
                            10240 + (sp % 2) * 1024 + nq * 128]
                    vcol0 = 2048 + (sp % 2) * 1024
                    ins = None
                    for t in range(nmm):
                        w0 = t * 512
                        w1 = min(nq * 128, w0 + 512)
                        ins = e.matmul(out=P32[:, vcol0 + w0:vcol0 + w1],
                                       lhsT=vsl, rhs=pb[:, w0:w1],
                                       start=True, stop=True)
                    for t in range(nmm):
                        w0 = t * 512
                        w1 = min(nq * 128, w0 + 512)
                        ins = e.matmul(out=P32[0:1, pcol + w0:pcol + w1],
                                       lhsT=ONE1[:], rhs=pb[:, w0:w1],
                                       start=True, stop=True)
                    c_t.inc(ins)
                v_c.wait(e, vc[("rec", h)])
                ins = None
                for t in range(2):
                    ins = e.matmul(out=P32[:, t * 512:(t + 1) * 512], lhsT=ONER[:],
                                   rhs=Bf[0:1, 16384 + t * 512:16384 + (t + 1) * 512],
                                   start=True, stop=True)
                c_t.inc(ins)
        blk.tensor(p3_tens)

        def p3_vec(e):
            for h in range(NH):
                e.memset(AACC[:, :], 0.0)
                e.memset(DACC[:, :], 0.0)
                c_t.wait(e, ct[("rows", h)])
                f_v.cut(e, e.tensor_reduce(out=SCL[0:1, 1024:1025],
                                           in_=P32[0:1, 0:2048],
                                           axis=AX.X, op=ALU.max))
                v_a.inc(e.tensor_scalar(out=SCL[0:1, 2048:3072],
                                        in0=P32[0:1, 2048:3072],
                                        scalar1=SCL[0:1, 1024:1025],
                                        scalar2=None, op0=ALU.mult))
                c_s.wait(e, cs[("bnd", h)])
                v_b.inc(e.tensor_scalar(out=Bf[0:1, 15360:16384],
                                        in0=SCL[0:1, 3072:4096], scalar1=RSD,
                                        scalar2=-40.0, op0=ALU.mult,
                                        op1=ALU.add))
                c_t.wait(e, ct[("bbc", h)])
                v_a.inc(e.tensor_scalar(out=SCL[:, 0:1024], in0=P32[:, 0:1024],
                                        scalar1=-1.0, scalar2=None, op0=ALU.mult))

                def avadd(sp):
                    q0n, nqn = sp_qb0[sp], sp_nq[sp]
                    c_t.wait(e, ct[("av", h, sp)])
                    vcol0 = 2048 + (sp % 2) * 1024
                    pcol = (sp % 2) * 1024
                    e.scalar_tensor_tensor(
                        out=AACC[:, q0n * 128:1024],
                        in0=P32[:, vcol0:vcol0 + nqn * 128], scalar=1.0,
                        in1=AACC[:, q0n * 128:1024],
                        op0=ALU.mult, op1=ALU.add)
                    v_a.inc(e.scalar_tensor_tensor(
                        out=DACC[0:1, q0n * 128:1024],
                        in0=P32[0:1, pcol:pcol + nqn * 128], scalar=1.0,
                        in1=DACC[0:1, q0n * 128:1024],
                        op0=ALU.mult, op1=ALU.add))

                for sp in range(NSP):
                    q0, nq = sp_qb0[sp], sp_nq[sp]
                    c_t.wait(e, ct[("sc", h, sp)])
                    pcol = (sp % 2) * 1024
                    tst = SM[:, (sp % 2) * 1024:(sp % 2) * 1024 + nq * 128]
                    v_a.inc(e.scalar_tensor_tensor(
                        out=tst, in0=P32[:, pcol:pcol + nq * 128], scalar=RSD,
                        in1=SCL[:, q0 * 128:q0 * 128 + nq * 128],
                        op0=ALU.mult, op1=ALU.add))
                    c_s.wait(e, cs[("exp", h, sp)])
                    pb = Bf[:, 10240 + (sp % 2) * 1024:
                            10240 + (sp % 2) * 1024 + nq * 128]
                    if sp < 8:
                        ins = e.tensor_tensor(out=pb[:, 0:128], in0=pb[:, 0:128],
                                              in1=TRI[:], op=ALU.mult)
                    else:
                        ins = e.tensor_scalar(out=pb[:, 0:128], in0=pb[:, 0:128],
                                              scalar1=PARF[:, 0:1], scalar2=None,
                                              op0=ALU.mult)
                    v_b.inc(ins)
                    avadd(sp)
                f_v.cut(e, e.engine_nop())
                e.reciprocal(out=DACC[:, :], in_=DACC[:, :])
                v_c.inc(e.tensor_copy(out=Bf[0:1, 16384:17408], in_=DACC[:, :]))
                c_t.wait(e, ct[("rbc", h)])
                e.tensor_copy(out=SM[:, 0:1024], in_=P32[:, 0:1024])
                v_c.inc(e.tensor_tensor(out=A[:, h * 1024:(h + 1) * 1024],
                                        in0=AACC[:, :], in1=SM[:, 0:1024],
                                        op=ALU.mult))
        blk.vector(p3_vec)

        assert c_t.n == ct_end and c_s.n == cs_end
        assert v_a.n == va_end and v_b.n == vb_end and v_c.n == vc_end

        if STOP == 3:
            _stop_tail()
            return nc

        for half in range(2):
            bw, bcs, bct, bvb, bsp = s_x.n, c_s.n, c_t.n, v_b.n, s_p.n

            def p4_sync(e, half=half, bcs=bcs, fcs=c_s.n, fvc=v_c.n):
                e.wait_ge(c_s.h, fcs)
                e.wait_ge(v_c.h, fvc)
                for k in range(8):
                    kk = half * 8 + k
                    if k >= 2:
                        c_s.wait(e, bcs + (k - 1))
                    s_x.dma(e, out=W[:, (k % 2) * 2048:(k % 2) * 2048 + 2048],
                                in_=inp["woT"][kk * 128:(kk + 1) * 128, :]
                                )
            blk.sync(p4_sync)

            def p4_scal(e, bw=bw, fct=c_t.n):
                e.wait_ge(c_t.h, fct)
                for k in range(8):
                    s_x.wait(e, bw + 16 * (k + 1))
                    c_s.inc(e.activation(out=Bf[:, k * 2048:(k + 1) * 2048],
                                         in_=W[:, (k % 2) * 2048:(k % 2) * 2048 + 2048],
                                         func=AF.Copy))
            blk.scalar(p4_scal)

            def p4_tens(e, half=half, bcs=bcs, bvb=bvb, fvc=v_c.n,
                        fva=v_a.n):
                e.wait_ge(v_c.h, fvc)
                e.wait_ge(v_a.h, fva)
                e.wait_ge(v_b.h, bvb)
                for tb in range(NTB):
                    if tb >= 1:
                        v_b.wait(e, bvb + tb)
                    ins = None
                    for k in range(8):
                        c_s.wait(e, bcs + (k + 1))
                        hh = half * 8 + k
                        lhs = A[:, hh * 1024 + tb * 128:hh * 1024 + (tb + 1) * 128]
                        for t in range(4):
                            rhs = Bf[:, k * 2048 + t * 512:k * 2048 + (t + 1) * 512]
                            ins = e.matmul(out=P32[:, t * 512:(t + 1) * 512],
                                           lhsT=lhs, rhs=rhs,
                                           start=(k == 0), stop=(k == 7))
                    c_t.inc(ins)
            blk.tensor(p4_tens)

            def p4_gp(e, half=half, bvb=bvb, bsp=bsp):
                for tb in range(NTB):
                    if half == 1:
                        s_p.dma(e, out=C[:, (tb % 2) * 2048:(tb % 2) * 2048 + 2048],
                                    in_=d_acc[tb])
                    v_b.wait(e, bvb + tb + 1)
                    s_p.dma(e, out=d_acc[tb],
                                in_=C[:, (tb % 2) * 2048:(tb % 2) * 2048 + 2048]
                                )
            blk.gpsimd(p4_gp)

            def p4_vec(e, half=half, bct=bct, bsp=bsp):
                for tb in range(NTB):
                    c_t.wait(e, bct + tb + 1)
                    cst = C[:, (tb % 2) * 2048:(tb % 2) * 2048 + 2048]
                    if half == 0:
                        if tb >= 2:
                            s_p.wait(e, bsp + 16 * (tb - 1))
                        v_b.inc(e.tensor_copy(out=cst, in_=P32[:, 0:2048]))
                    else:
                        s_p.wait(e, bsp + 32 * tb + 16)
                        if tb >= 2:
                            s_p.wait(e, bsp + 32 * (tb - 1))
                        v_b.inc(e.scalar_tensor_tensor(
                            out=cst, in0=P32[:, 0:2048], scalar=1.0, in1=cst,
                            op0=ALU.mult, op1=ALU.add))
            blk.vector(p4_vec)

            assert c_t.n == bct + NTB and v_b.n == bvb + NTB

        if STOP == 4:
            _stop_tail()
            return nc

        bxn, bva, bct, bvc, bsp = s_x.n, v_a.n, c_t.n, v_c.n, s_p.n

        def p5p_sync(e, fcs=c_s.n):
            e.wait_ge(c_s.h, fcs)
            for tb in range(NTB):
                if tb >= 2:
                    v_a.wait(e, bva + (tb - 1))
                s_x.dma(e, out=W[:, (tb % 2) * 2048:(tb % 2) * 2048 + 2048],
                            in_=d_xn[tb])
        blk.sync(p5p_sync)

        def p5p_evac(e, G):
            tb, g = G // 2, G % 2
            c_t.wait(e, bct + G + 1)
            if tb >= 2 and g == 0:
                s_p.wait(e, bsp + 16 * (tb - 1))
            ins = None
            for t in range(8):
                kc = g * 8 + t
                ins = e.tensor_copy(
                    out=Bf[:, 16384 + (tb % 2) * 2048 + kc * 128:
                           16384 + (tb % 2) * 2048 + (kc + 1) * 128],
                    in_=P16[:, t * 128:(t + 1) * 128])
            v_c.inc(ins)

        def p5p_vec(e):
            for tb in range(NTB + 1):
              if tb < NTB:
                xt = W[:, (tb % 2) * 2048:(tb % 2) * 2048 + 2048]
                s_x.wait(e, bxn + 16 * (tb + 1))
                if tb >= 2:
                    c_t.wait(e, bct + 2 * (tb - 2) + 2)
                f_v.cut(e, e.tensor_reduce(
                    out=SCL[:, 4160:4224],
                    in_=xt.rearrange("p (j b) -> p b j", b=64),
                    axis=AX.X, op=ALU.max, apply_absolute_value=True))
                f_v.cut(e, e.tensor_scalar(out=SCL[:, 4224:4288],
                                           in0=SCL[:, 4160:4224],
                                           scalar1=1.0 / 448.0, scalar2=None,
                                           op0=ALU.mult))
                f_v.cut(e, e.reciprocal(out=SCL[:, 4288:4352],
                                        in_=SCL[:, 4224:4288]))
                f_v.cut(e, e.tensor_scalar(out=SCL[:, 4288:4352],
                                           in0=SCL[:, 4288:4352],
                                           scalar1=0.5, scalar2=None,
                                           op0=ALU.mult))
                f_v.cut(e, e.tensor_scalar(out=SCL[:, 4352:4416],
                                           in0=SCL[:, 4224:4288],
                                           scalar1=2.0, scalar2=None,
                                           op0=ALU.mult))
                quot = SM[:, 0:2048]
                e.tensor_tensor(
                    out=quot.rearrange("p (j b) -> p j b", b=64),
                    in0=xt.rearrange("p (j b) -> p j b", b=64),
                    in1=SCL[:, 4288:4352].rearrange(
                        "p (o b) -> p o b", o=1).broadcast_to([128, 32, 64]),
                    op=ALU.mult)
                e.tensor_copy(out=Q8[:, 0:2048], in_=quot)
                xqs = 20480 + (tb % 2) * 2048
                v_a.inc(e.tensor_tensor(
                    out=Bf[:, xqs:xqs + 2048].rearrange("p (j b) -> p j b", b=64),
                    in0=Q8[:, 0:2048].rearrange("p (j b) -> p j b", b=64),
                    in1=SCL[:, 4352:4416].rearrange(
                        "p (o b) -> p o b", o=1).broadcast_to([128, 32, 64]),
                    op=ALU.mult))
              if tb >= 1:
                p5p_evac(e, 2 * (tb - 1))
                p5p_evac(e, 2 * (tb - 1) + 1)
        blk.vector(p5p_vec)

        def p5p_tens(e, fvb=v_b.n):
            e.wait_ge(v_b.h, fvb)
            e.wait_ge(v_c.h, bvc)
            for tb in range(NTB):
                v_a.wait(e, bva + tb + 1)
                for g in range(2):
                    if 2 * tb + g >= 1:
                        v_c.wait(e, bvc + 2 * tb + g)
                    ins = None
                    xqs = 20480 + (tb % 2) * 2048
                    for t in range(8):
                        kc = g * 8 + t
                        ins = e.transpose(
                            out=P16[:, t * 128:(t + 1) * 128],
                            in_=Bf[:, xqs + kc * 128:xqs + (kc + 1) * 128],
                            identity=IDN[:])
                    c_t.inc(ins)
        blk.tensor(p5p_tens)

        def p5p_gp(e):
            for tb in range(NTB):
                v_c.wait(e, bvc + 2 * tb + 2)
                s_p.dma(e, out=d_xq[tb],
                            in_=Bf[:, 16384 + (tb % 2) * 2048:
                                   16384 + (tb % 2) * 2048 + 2048]
                            )
        blk.gpsimd(p5p_gp)

        assert v_a.n == bva + NTB and c_t.n == bct + 2 * NTB
        assert v_c.n == bvc + 2 * NTB and s_p.n == bsp + 16 * NTB

        if STOP == 5:
            _stop_tail()
            return nc

        for sg in range(NSEG):
            bw1, bva1, bvb1 = s_x.n, v_a.n, v_b.n

            def p5s_sync(e, sg=sg, bva1=bva1):
                e.wait_ge(v_a.h, bva1)
                for wi, wnm in enumerate(["gwT", "uwT"]):
                    for k in range(NKC):
                        u = wi * NKC + k
                        if u >= 2:
                            v_a.wait(e, bva1 + (u - 1))
                        s_x.dma(e, 
                            out=W[:, (u % 2) * 1024:(u % 2) * 1024 + 1024],
                            in_=inp[wnm][k * 128:(k + 1) * 128,
                                         sg * SEG:sg * SEG + SEG]
                            )
                for k in range(8):
                    u = 32 + k
                    v_a.wait(e, bva1 + (u if u == 32 else u - 1))
                    s_x.dma(e, 
                        out=W[:, (u % 2) * 2048:(u % 2) * 2048 + 2048],
                        in_=inp["dwT"][(sg * 8 + k) * 128:(sg * 8 + k + 1) * 128, :]
                        )
            blk.sync(p5s_sync)

            def p5s_vec(e, sg=sg, bw1=bw1, bcs1=c_s.n):
                for wi in range(2):
                    dst = SCL[0:64, wi * 1024:wi * 1024 + 1024]
                    for k in range(NKC):
                        u = wi * NKC + k
                        s_x.wait(e, bw1 + 16 * (u + 1))
                        t_ = W[:, (u % 2) * 1024:(u % 2) * 1024 + 1024]
                        e.tensor_tensor(out=SM[0:64, 0:1024], in0=t_[0:64, :],
                                        in1=t_[0:64, :], op=ALU.mult)
                        e.tensor_tensor(out=SM[0:64, 1024:2048], in0=t_[64:128, :],
                                        in1=t_[64:128, :], op=ALU.mult)
                        e.tensor_tensor(out=SM[0:64, 0:1024], in0=SM[0:64, 0:1024],
                                        in1=SM[0:64, 1024:2048], op=ALU.max)
                        if k == 0:
                            ins = e.tensor_copy(out=dst, in_=SM[0:64, 0:1024])
                        else:
                            ins = e.tensor_tensor(out=dst, in0=dst,
                                                  in1=SM[0:64, 0:1024], op=ALU.max)
                        v_a.inc(ins)
                for k in range(8):
                    u = 32 + k
                    s_x.wait(e, bw1 + 16 * (u + 1))
                    t_ = W[:, (u % 2) * 2048:(u % 2) * 2048 + 2048]
                    e.tensor_tensor(out=SM[0:64, 0:2048], in0=t_[0:64, :],
                                    in1=t_[0:64, :], op=ALU.mult)
                    e.tensor_tensor(out=SCL[0:64, 4096:6144], in0=t_[64:128, :],
                                    in1=t_[64:128, :], op=ALU.mult)
                    e.tensor_tensor(out=SM[0:64, 0:2048], in0=SM[0:64, 0:2048],
                                    in1=SCL[0:64, 4096:6144], op=ALU.max)
                    e.tensor_copy(out=SCL[0:32, 4096:6144], in_=SM[32:64, 0:2048])
                    e.tensor_tensor(out=SM[0:32, 0:2048], in0=SM[0:32, 0:2048],
                                    in1=SCL[0:32, 4096:6144], op=ALU.max)
                    dst = SCL[0:32, 2048:4096]
                    if k == 0:
                        ins = e.tensor_copy(out=dst, in_=SM[0:32, 0:2048])
                    else:
                        ins = e.tensor_tensor(out=dst, in0=dst,
                                              in1=SM[0:32, 0:2048], op=ALU.max)
                    v_a.inc(ins)
            blk.vector(p5s_vec)

            def p5s_scal(e, bva1=bva1):
                v_a.wait(e, bva1 + 40)
                e.activation(out=SCL[0:64, 0:1024], in_=SCL[0:64, 0:1024],
                             func=AF.Sqrt, scale=1.0 / (224.0 * 224.0))
                e.activation(out=SCL[0:64, 1024:2048], in_=SCL[0:64, 1024:2048],
                             func=AF.Sqrt, scale=1.0 / (224.0 * 224.0))
                c_s.inc(e.activation(out=SCL[0:32, 2048:4096],
                                     in_=SCL[0:32, 2048:4096],
                                     func=AF.Sqrt, scale=1.0 / (224.0 * 224.0)))
            blk.scalar(p5s_scal)

            def p5s_vec2(e, bcs1=c_s.n):
                e.wait_ge(c_s.h, bcs1)
                for wi in range(2):
                    sc_ = SCL[:, wi * 1024:wi * 1024 + 1024]
                    s2_ = SC2[:, wi * 1024:wi * 1024 + 1024]
                    e.tensor_copy(out=s2_[0:64, :], in_=sc_[0:64, :])
                    e.tensor_copy(out=s2_[64:128, :], in_=s2_[0:64, :])
                    e.reciprocal(out=sc_[0:64, :], in_=sc_[0:64, :])
                    e.tensor_copy(out=sc_[64:128, :], in_=sc_[0:64, :])
                dsc = SCL[:, 2048:4096]
                ds2 = SC2[:, 2048:4096]
                e.tensor_copy(out=ds2[0:32, :], in_=dsc[0:32, :])
                e.tensor_copy(out=ds2[32:64, :], in_=ds2[0:32, :])
                e.tensor_copy(out=ds2[64:128, :], in_=ds2[0:64, :])
                e.reciprocal(out=dsc[0:32, :], in_=dsc[0:32, :])
                e.tensor_copy(out=dsc[32:64, :], in_=dsc[0:32, :])
                v_b.inc(e.tensor_copy(out=dsc[64:128, :], in_=dsc[0:64, :]))
            blk.vector(p5s_vec2)

            bw2, bcs2, bva2 = s_x.n, c_s.n, v_a.n

            def p5q_sync(e, sg=sg, bva2=bva2, bvb1=bvb1, fct=c_t.n):
                e.wait_ge(c_t.h, fct)
                e.wait_ge(v_a.h, bva2)
                for wi, wnm in enumerate(["gwT", "uwT"]):
                    for k in range(NKC):
                        u = wi * NKC + k
                        if u == 0:
                            v_b.wait(e, bvb1 + 1)
                        if u >= 2:
                            v_a.wait(e, bva2 + 3 * (u - 1))
                        s_x.dma(e, 
                            out=W[:, (u % 2) * 1024:(u % 2) * 1024 + 1024],
                            in_=inp[wnm][k * 128:(k + 1) * 128,
                                         sg * SEG:sg * SEG + SEG]
                            )
                for k in range(8):
                    u = 32 + k
                    v_a.wait(e, bva2 + 3 * u)
                    s_x.dma(e, 
                        out=W[:, 0:2048],
                        in_=inp["dwT"][(sg * 8 + k) * 128:(sg * 8 + k + 1) * 128, :]
                        )
            blk.sync(p5q_sync)

            def p5q_scal(e, bva2=bva2):
                for u in range(40):
                    v_a.wait(e, bva2 + 3 * u + 1)
                    wdt = 1024 if u < 32 else 2048
                    c_s.inc(e.activation(out=Q8[:, 0:wdt], in_=SM[:, 0:wdt],
                                         func=AF.Copy))
            blk.scalar(p5q_scal)

            def p5q_vec(e, sg=sg, bw2=bw2, bcs2=bcs2):
                for u in range(40):
                    s_x.wait(e, bw2 + 16 * (u + 1))
                    if u < 32:
                        wi, k = divmod(u, NKC)
                        t_ = W[:, (u % 2) * 1024:(u % 2) * 1024 + 1024]
                        rc = SCL[:, wi * 1024:wi * 1024 + 1024]
                        s2 = SC2[:, wi * 1024:wi * 1024 + 1024]
                        dst = A[:, wi * 16384 + k * 1024:wi * 16384 + (k + 1) * 1024]
                        wdt = 1024
                    else:
                        k = u - 32
                        t_ = W[:, 0:2048]
                        rc = SCL[:, 2048:4096]
                        s2 = SC2[:, 2048:4096]
                        dst = Bf[:, k * 2048:(k + 1) * 2048]
                        wdt = 2048
                    e.tensor_tensor(out=SM[:, 0:wdt], in0=t_, in1=rc[:, 0:wdt],
                                    op=ALU.mult)
                    v_a.inc(e.engine_nop())
                    v_a.inc(e.engine_nop())
                    c_s.wait(e, bcs2 + (u + 1))
                    v_a.inc(e.tensor_tensor(out=dst, in0=Q8[:, 0:wdt],
                                            in1=s2[:, 0:wdt], op=ALU.mult))
            blk.vector(p5q_vec)
            assert v_a.n == bva2 + 120 and c_s.n == bcs2 + 40

            bsx3, bct3, bcs3, bva3, bvb3, bvc3, bsp3 = (s_x.n, c_t.n, c_s.n,
                                                        v_a.n, v_b.n, v_c.n,
                                                        s_p.n)
            CT = lambda tb, i: bct3 + 4 * tb + i + 1
            CS = lambda tb, i: bcs3 + 3 * tb + i + 1
            VA = lambda tb, i: bva3 + 3 * tb + i + 1
            VB = lambda tb: bvb3 + tb + 1
            VC = lambda tb, i: bvc3 + 2 * tb + i + 1

            def p5m_sync(e, bct3=bct3):
                for tb in range(NTB):
                    if tb >= 2:
                        c_t.wait(e, CT(tb - 2, 1))
                    s_x.dma(e, out=Bf[:, 18432 + (tb % 2) * 1024:
                                      18432 + (tb % 2) * 1024 + 1024],
                                in_=d_xq[tb][:, 0:1024])
                    s_x.dma(e, out=Bf[:, 20480 + (tb % 2) * 1024:
                                   20480 + (tb % 2) * 1024 + 1024],
                                in_=d_xq[tb][:, 1024:2048])
            blk.sync(p5m_sync)

            def xqsl(tb, k):
                if k < 8:
                    base = 18432 + (tb % 2) * 1024
                    return Bf[:, base + k * 128:base + (k + 1) * 128]
                base = 20480 + (tb % 2) * 1024
                return Bf[:, base + (k - 8) * 128:base + (k - 7) * 128]

            def p5m_tens(e, sg=sg, bsx3=bsx3, bva2=bva2, bvc3=bvc3,
                         fvb=v_b.n, fvc=v_c.n):
                e.wait_ge(v_b.h, fvb)
                e.wait_ge(v_c.h, fvc)
                for tb in range(NTB):
                    s_x.wait(e, bsx3 + 32 * (tb + 1))
                    for half in range(2):
                        g = 2 * tb + half
                        if g >= 1:
                            v_a.wait(e, VA((g - 1) // 2, (g - 1) % 2))
                        ins = None
                        for k in range(NKC):
                            v_a.wait(e, bva2 + 3 * (16 + k) + 3)
                            lhs = xqsl(tb, k)
                            gt = A[:, k * 1024 + half * 512:
                                   k * 1024 + half * 512 + 512]
                            ut = A[:, 16384 + k * 1024 + half * 512:
                                   16384 + k * 1024 + half * 512 + 512]
                            e.matmul(out=P32[:, 0:512], lhsT=lhs, rhs=gt,
                                     start=(k == 0), stop=(k == NKC - 1))
                            ins = e.matmul(out=P32[:, 512:1024], lhsT=lhs, rhs=ut,
                                           start=(k == 0), stop=(k == NKC - 1))
                        c_t.inc(ins)
                    v_b.wait(e, VB(tb))
                    if tb >= 1:
                        v_c.wait(e, VC(tb - 1, 0))
                    ins = None
                    for t in range(8):
                        ins = e.transpose(out=P16[:, t * 128:(t + 1) * 128],
                                          in_=Bf[:, 16384 + t * 128:
                                                 16384 + (t + 1) * 128],
                                          identity=IDN[:])
                    c_t.inc(ins)
                    v_c.wait(e, VC(tb, 0))
                    if tb >= 1:
                        v_c.wait(e, VC(tb - 1, 1))
                    ins = None
                    for kq in range(8):
                        v_a.wait(e, bva2 + 3 * (32 + kq) + 3)
                        lhs = Bf[:, 17408 + kq * 128:17408 + (kq + 1) * 128]
                        for t in range(4):
                            rhs = Bf[:, kq * 2048 + t * 512:kq * 2048 + (t + 1) * 512]
                            ins = e.matmul(out=P32[:, 1024 + t * 512:
                                                  1024 + (t + 1) * 512],
                                           lhsT=lhs, rhs=rhs,
                                           start=(kq == 0), stop=(kq == 7))
                    c_t.inc(ins)
            blk.tensor(p5m_tens)

            def p5m_scal(e, bct3=bct3, bva3=bva3, bvb3=bvb3):
                for tb in range(NTB):
                    for half in range(2):
                        c_t.wait(e, CT(tb, half))
                        if half == 1:
                            v_a.wait(e, VA(tb, 0))
                        c_s.inc(e.activation(out=W[:, 3072:3584],
                                             in_=P32[:, 0:512], func=AF.Silu))
                    v_a.wait(e, VA(tb, 2))
                    if tb >= 1:
                        v_b.wait(e, VB(tb - 1))
                    c_s.inc(e.activation(out=Q8[:, 0:1024], in_=SM[:, 0:1024],
                                         func=AF.Copy))
            blk.scalar(p5m_scal)

            def p5m_vec(e, bct3=bct3, bcs3=bcs3, bsp3=bsp3):
                for tb in range(NTB):
                    for half in range(2):
                        c_s.wait(e, CS(tb, half))
                        v_a.inc(e.tensor_tensor(
                            out=W[:, 2048 + half * 512:2048 + half * 512 + 512],
                            in0=P32[:, 512:1024], in1=W[:, 3072:3584],
                            op=ALU.mult))
                    ht = W[:, 2048:3072]
                    f_v.cut(e, e.tensor_reduce(
                        out=SCL[:, 4096:4128],
                        in_=ht.rearrange("p (j b) -> p b j", b=32),
                        axis=AX.X, op=ALU.max, apply_absolute_value=True))
                    f_v.cut(e, e.tensor_scalar(out=SCL[:, 4128:4160],
                                               in0=SCL[:, 4096:4128],
                                               scalar1=1.0 / 448.0,
                                               scalar2=None, op0=ALU.mult))
                    f_v.cut(e, e.reciprocal(out=SCL[:, 4160:4192],
                                            in_=SCL[:, 4128:4160]))
                    f_v.cut(e, e.tensor_scalar(out=SCL[:, 4160:4192],
                                               in0=SCL[:, 4160:4192],
                                               scalar1=0.5, scalar2=None,
                                               op0=ALU.mult))
                    f_v.cut(e, e.tensor_scalar(out=SCL[:, 4192:4224],
                                               in0=SCL[:, 4128:4160],
                                               scalar1=2.0, scalar2=None,
                                               op0=ALU.mult))
                    e.tensor_tensor(
                        out=SM[:, 0:1024].rearrange("p (j b) -> p j b", b=32),
                        in0=ht.rearrange("p (j b) -> p j b", b=32),
                        in1=SCL[:, 4160:4192].rearrange(
                            "p (o b) -> p o b", o=1).broadcast_to([128, 32, 32]),
                        op=ALU.mult)
                    v_a.inc(e.engine_nop())
                    c_s.wait(e, CS(tb, 2))
                    v_b.inc(e.tensor_tensor(
                        out=Bf[:, 16384:17408].rearrange("p (j b) -> p j b", b=32),
                        in0=Q8[:, 0:1024].rearrange("p (j b) -> p j b", b=32),
                        in1=SCL[:, 4192:4224].rearrange(
                            "p (o b) -> p o b", o=1).broadcast_to([128, 32, 32]),
                        op=ALU.mult))
                    c_t.wait(e, CT(tb, 2))
                    ins = None
                    for t in range(8):
                        ins = e.tensor_copy(out=Bf[:, 17408 + t * 128:
                                                  17408 + (t + 1) * 128],
                                            in_=P16[:, t * 128:(t + 1) * 128])
                    v_c.inc(ins)
                    c_t.wait(e, CT(tb, 3))
                    s_p.wait(e, bsp3 + 32 * tb + 16)
                    cst = C[:, (tb % 2) * 2048:(tb % 2) * 2048 + 2048]
                    v_c.inc(e.scalar_tensor_tensor(
                        out=cst, in0=P32[:, 1024:3072], scalar=1.0, in1=cst,
                        op0=ALU.mult, op1=ALU.add))
            blk.vector(p5m_vec)

            def p5m_gp(e, sg=sg, bvc3=bvc3, bsp3=bsp3):
                for tb in range(NTB):
                    if tb >= 2:
                        v_c.wait(e, VC(tb - 2, 1))
                    s_p.dma(e, out=C[:, (tb % 2) * 2048:(tb % 2) * 2048 + 2048],
                                in_=d_acc[tb])
                    v_c.wait(e, VC(tb, 1))
                    s_p.dma(e, out=d_acc[tb],
                                in_=C[:, (tb % 2) * 2048:(tb % 2) * 2048 + 2048]
                                )
            blk.gpsimd(p5m_gp)

            assert c_t.n == bct3 + 4 * NTB and c_s.n == bcs3 + 3 * NTB
            assert v_a.n == bva3 + 3 * NTB and v_b.n == bvb3 + NTB
            assert v_c.n == bvc3 + 2 * NTB and s_p.n == bsp3 + 32 * NTB

        if STOP == 6:
            _stop_tail()
            return nc

        bx6, bsp6, bva6 = s_x.n, s_p.n, v_a.n
        p6_order = []
        for tb in range(NTB):
            p6_order.append(("L", tb))
            if tb >= 2:
                p6_order.append(("S", tb - 2))
        p6_order += [("S", NTB - 2), ("S", NTB - 1)]
        p6_cum = {}
        cnt6 = 0
        for it6 in p6_order:
            cnt6 += 16
            p6_cum[it6] = bx6 + cnt6

        def p6_sync(e, fva=v_a.n):
            e.wait_ge(v_a.h, fva)
            for kind, tb in p6_order:
                if kind == "L":
                    if tb >= 2:
                        v_a.wait(e, bva6 + (tb - 1))
                    s_x.dma(e, out=W[:, (tb % 2) * 2048:(tb % 2) * 2048 + 2048],
                                in_=inp["xf"][tb * 128:(tb + 1) * 128, :]
                                )
                else:
                    v_a.wait(e, bva6 + (tb + 1))
                    s_x.dma(e, out=out_own[tb * 128:(tb + 1) * 128, :],
                                in_=SM[:, 0:2048])
            s_x.wait(e, s_x.n)
        blk.sync(p6_sync)

        def p6_gp(e, fvc=v_c.n):
            e.wait_ge(v_c.h, fvc)
            for tb in range(NTB):
                if tb >= 2:
                    v_a.wait(e, bva6 + (tb - 1))
                s_p.dma(e, out=C[:, (tb % 2) * 2048:(tb % 2) * 2048 + 2048],
                            in_=d_acc[tb])
        blk.gpsimd(p6_gp)

        def p6_vec(e):
            for tb in range(NTB):
                s_x.wait(e, p6_cum[("L", tb)])
                s_p.wait(e, bsp6 + 16 * (tb + 1))
                if tb >= 1:
                    s_x.wait(e, p6_cum[("S", tb - 1)])
                cst = C[:, (tb % 2) * 2048:(tb % 2) * 2048 + 2048]
                xst = W[:, (tb % 2) * 2048:(tb % 2) * 2048 + 2048]
                v_a.inc(e.tensor_tensor(
                    out=SM[:, 0:2048].rearrange("p (b j) -> p b j", j=32),
                    in0=cst.rearrange("p (j b) -> p b j", b=64),
                    in1=xst.rearrange("p (j b) -> p b j", b=64),
                    op=ALU.add))
        blk.vector(p6_vec)

    return nc


def _perms():
    permH = np.empty(D, np.int64)
    for j in range(32):
        for b in range(64):
            permH[j * 64 + b] = 32 * b + j
    rho = np.empty(D, np.int64)
    for h in range(NH):
        for k in range(DK):
            rho[h * DK + k] = h * DK + (2 * k if k < 64 else 2 * (k - 64) + 1)
    permI = np.empty(II, np.int64)
    for sg in range(NSEG):
        for j in range(32):
            for b in range(32):
                permI[sg * SEG + j * 32 + b] = sg * SEG + 32 * b + j
    return permH, rho, permI


def kernel(L0_attn_x, L0_attn_wq, L0_attn_wk, L0_attn_wv, L0_attn_wo,
           L0_ffn_gate_w, L0_ffn_up_w, L0_ffn_down_w):
    x = np.ascontiguousarray(np.asarray(L0_attn_x, np.float32))
    wq = np.asarray(L0_attn_wq, np.float32)
    wk = np.asarray(L0_attn_wk, np.float32)
    wv = np.asarray(L0_attn_wv, np.float32)
    wo = np.asarray(L0_attn_wo, np.float32)
    gw = np.asarray(L0_ffn_gate_w, np.float32)
    uw = np.asarray(L0_ffn_up_w, np.float32)
    dw = np.asarray(L0_ffn_down_w, np.float32)

    permH, rho, permI = _perms()
    wqT = np.ascontiguousarray(wq[rho][:, permH].T)
    wkT = np.ascontiguousarray(wk[rho][:, permH].T)
    wvT = np.ascontiguousarray(wv[:, permH].T)
    woT = np.ascontiguousarray(wo[permH].T)
    gwT = np.ascontiguousarray(gw[permI][:, permH].T)
    uwT = np.ascontiguousarray(uw[permI][:, permH].T)
    dwT = np.ascontiguousarray(dw[permH][:, permI].T)
    tri = np.ascontiguousarray(np.triu(np.ones((128, 128)))).astype(
        ml_dtypes.bfloat16)
    ident = np.eye(128).astype(ml_dtypes.bfloat16)
    inv = (np.float32(10000.0) **
           (-(np.arange(0, DK, 2).astype(np.float32) / np.float32(DK))))

    nc = build_nc()
    in_maps = []
    for c in range(8):
        b, par = divmod(c, 2)
        ordbl = [2 * j + par for j in range(8)] + [2 * j + (1 - par)
                                                  for j in range(8)]
        idx = np.concatenate([np.arange(g * 128, (g + 1) * 128) for g in ordbl])
        xf = np.ascontiguousarray(x[b][idx][:, permH])
        pos = idx.astype(np.float32)
        ang = inv[:, None].astype(np.float32) * pos[None, :].astype(np.float32)
        sn = np.sin(ang).astype(np.float32)
        cs_ = np.cos(ang).astype(np.float32)
        sint = np.ascontiguousarray(np.concatenate([-sn, sn], 0))
        cost = np.ascontiguousarray(np.concatenate([cs_, cs_], 0))
        parf = np.full((128, 1), float(par), np.float32)
        in_maps.append({
            "xf": xf, "wqT": wqT, "wkT": wkT, "wvT": wvT, "woT": woT,
            "gwT": gwT, "uwT": uwT, "dwT": dwT, "sint": sint, "cost": cost,
            "trimask": tri, "parf": parf, "ident": ident,
        })

    res = run_bass_kernel_spmd(nc, in_maps, core_ids=list(range(8)))
    out = np.empty((4, S, D), np.float32)
    for c in range(8):
        b, par = divmod(c, 2)
        o = res.results[c]["out_own"]
        for j in range(8):
            out[b][(2 * j + par) * 128:(2 * j + par + 1) * 128] = \
                o[j * 128:(j + 1) * 128]
    return out


# version 07
# speedup vs baseline: 1.2040x; 1.2040x over previous
import contextlib
import os
import numpy as np
import ml_dtypes
import concourse.bass as bass
import concourse.mybir as mybir
from concourse.bass_utils import run_bass_kernel_spmd

F32 = mybir.dt.float32
BF16 = mybir.dt.bfloat16
FP8 = mybir.dt.float8e4
ALU = mybir.AluOpType
AF = mybir.ActivationFunctionType
AX = mybir.AxisListType

D = 2048
S = 2048
NH = 16
DK = 128
II = 8192
SEG = 1024
NSEG = II // SEG
TOK = 1024
NTB = TOK // 128
NKC = D // 128
NXT = S // 128
NSP = 16
EPS = 1e-5
RSD = float(1.0 / np.sqrt(np.float64(DK)))
INV448H = 0.5 / 448.0


class Sem:
    __slots__ = ("h", "n")

    def __init__(self, nc, es, name):
        self.h = es.enter_context(nc.semaphore(name))
        self.n = 0

    def inc(self, ins, k=1):
        ins.then_inc(self.h, k)
        self.n += k
        return ins

    def wait(self, eng, v):
        eng.wait_ge(self.h, v)


class Fence:
    __slots__ = ("h", "n")

    def __init__(self, nc, es, name):
        self.h = es.enter_context(nc.semaphore(name))
        self.n = 0

    def cut(self, e, ins):
        ins.then_inc(self.h, 1)
        self.n += 1
        e.wait_ge(self.h, self.n)
        return ins


class Stream:

    def __init__(self, nc, es, name, k=3):
        self.sems = [es.enter_context(nc.semaphore(f"{name}{j}"))
                     for j in range(k)]
        self.k = k
        self.i = 0

    @property
    def n(self):
        return 16 * self.i

    def dma(self, e, out, in_):
        i = self.i
        j = i % self.k
        if i >= self.k:
            e.wait_ge(self.sems[j], 16 * (i // self.k))
        ins = e.dma_start(out=out, in_=in_)
        ins.then_inc(self.sems[j], 16)
        self.i += 1
        return ins

    def wait(self, e, v):
        assert v % 16 == 0
        n = v // 16
        for idx in range(max(0, n - self.k), n):
            e.wait_ge(self.sems[idx % self.k], 16 * (idx // self.k + 1))


class B_:
    pass


def build_nc():
    STOP = int(os.environ.get("K_STOP", "99"))
    SUB = os.environ.get("K_SUB", "full")
    z = B_()
    nc = bass.Bass("TRN2", target_bir_lowering=False, debug=False)
    z.nc = nc
    inp = {}
    for nm, shp in [("xf", [S, D]), ("wqT", [D, D]), ("wkT", [D, D]),
                    ("wvT", [D, D]), ("woT", [D, D]), ("gwT", [D, II]),
                    ("uwT", [D, II]), ("dwT", [II, D]),
                    ("sint", [128, S]), ("cost", [128, S])]:
        inp[nm] = nc.dram_tensor(nm, shp, F32, kind="ExternalInput").ap()
    inp["trimask"] = nc.dram_tensor("trimask", [128, 128], BF16, kind="ExternalInput").ap()
    inp["parf"] = nc.dram_tensor("parf", [128, 1], F32, kind="ExternalInput").ap()
    inp["ident"] = nc.dram_tensor("ident", [128, 128], BF16, kind="ExternalInput").ap()
    out_own = nc.dram_tensor("out_own", [TOK, D], F32, kind="ExternalOutput").ap()
    d_k = nc.dram_tensor("d_k", [NH, 128, S], BF16, kind="Internal").ap()
    d_q = nc.dram_tensor("d_q", [NH, 128, TOK], BF16, kind="Internal").ap()
    d_v = nc.dram_tensor("d_v", [S, D], BF16, kind="Internal").ap()
    d_xn = nc.dram_tensor("d_xn", [NTB, 128, D], F32, kind="Internal").ap()
    d_xq = nc.dram_tensor("d_xq", [NTB, 128, NKC * 128], BF16, kind="Internal").ap()
    d_acc = nc.dram_tensor("d_acc", [NTB, 128, D], F32, kind="Internal").ap()

    es = contextlib.ExitStack()
    with es:
        A = es.enter_context(nc.sbuf_tensor("Ar", [128, 32768], BF16))
        Bf = es.enter_context(nc.sbuf_tensor("Br", [128, 24576], BF16))
        C = es.enter_context(nc.sbuf_tensor("Cr", [128, 4096], F32))
        W = es.enter_context(nc.sbuf_tensor("Wr", [128, 4096], F32))
        SM = es.enter_context(nc.sbuf_tensor("SMr", [128, 2048], F32))
        SCL = es.enter_context(nc.sbuf_tensor("SCLr", [128, 8192], F32))
        GS = es.enter_context(nc.sbuf_tensor("GSr", [128, 512], F32))
        Q8 = es.enter_context(nc.sbuf_tensor("Q8r", [128, 2048], FP8))
        TRI = es.enter_context(nc.sbuf_tensor("TRIr", [128, 128], BF16))
        PARF = es.enter_context(nc.sbuf_tensor("PARFr", [128, 1], F32))
        ONE1 = es.enter_context(nc.sbuf_tensor("ONE1r", [128, 1], BF16))
        EPSC = es.enter_context(nc.sbuf_tensor("EPSCr", [128, 1], F32))
        ONER = es.enter_context(nc.sbuf_tensor("ONERr", [1, 128], BF16))
        IDN = es.enter_context(nc.sbuf_tensor("IDNr", [128, 128], BF16))
        P32 = es.enter_context(nc.psum_tensor("P32r", [128, 4096], F32))
        P16 = P32[:, 3584:4096].bitcast(BF16)

        s_x = Stream(nc, es, "sx")
        s_p = Stream(nc, es, "sp")
        c_s = Sem(nc, es, "c_s")
        c_t = Sem(nc, es, "c_t")
        v_a = Sem(nc, es, "v_a")
        v_b = Sem(nc, es, "v_b")
        v_c = Sem(nc, es, "v_c")
        f_v = Fence(nc, es, "f_v")

        blk = es.enter_context(nc.Block())

        def _stop_tail():
            dump = os.environ.get("K_DUMP", "")

            def t_sync(e):
                for sm in (c_s, c_t, v_a, v_b, v_c):
                    e.wait_ge(sm.h, sm.n)
                s_p.wait(e, s_p.n)
                s_x.wait(e, s_x.n)
                if dump == "xn":
                    for j in range(8):
                        s_x.dma(e, out=out_own[j * 128:(j + 1) * 128, :],
                                in_=d_xn[j])
                elif dump == "acc":
                    for j in range(8):
                        s_x.dma(e, out=out_own[j * 128:(j + 1) * 128, :],
                                in_=d_acc[j])
                elif dump == "q":
                    for j in range(8):
                        s_x.dma(e, out=out_own[j * 128:(j + 1) * 128, 0:512]
                                .bitcast(BF16), in_=d_q[2 * j])
                elif dump == "k":
                    for j in range(8):
                        s_x.dma(e, out=out_own[j * 128:(j + 1) * 128, 0:1024]
                                .bitcast(BF16), in_=d_k[2 * j])
                elif dump == "v":
                    for j in range(8):
                        s_x.dma(e, out=out_own[j * 128:(j + 1) * 128, 0:1024]
                                .bitcast(BF16), in_=d_v[j * 128:(j + 1) * 128, :])
                elif dump == "xq":
                    for j in range(8):
                        s_x.dma(e, out=out_own[j * 128:(j + 1) * 128, 0:1024]
                                .bitcast(BF16), in_=d_xq[j])
                else:
                    s_x.dma(e, out=out_own[0:128, :], in_=SM[:, 0:2048])
                s_x.wait(e, s_x.n)
            blk.sync(t_sync)

        def p0_sync(e):
            s_x.dma(e, out=C[:, 0:2048], in_=inp["sint"])
            s_x.dma(e, out=C[:, 2048:4096], in_=inp["cost"])
            s_x.dma(e, out=TRI[:], in_=inp["trimask"])
            s_x.dma(e, out=PARF[:], in_=inp["parf"])
            s_x.dma(e, out=IDN[:], in_=inp["ident"])
        blk.sync(p0_sync)

        def p0_vec(e):
            e.memset(ONE1[:], 1.0)
            e.memset(EPSC[:], EPS)
            v_a.inc(e.memset(ONER[:], 1.0))
        blk.vector(p0_vec)

        if STOP == 0:
            _stop_tail()
            return nc

        bx, bva, bcs, bvb, bct, bvc, bsp = (s_x.n, v_a.n, c_s.n, v_b.n,
                                            c_t.n, v_c.n, s_p.n)

        def p1_sync(e):
            for i in range(NXT):
                if i >= 2:
                    v_b.wait(e, bvb + (i - 1))
                    if SUB not in ("ln", "tp") and i - 2 < NTB:
                        s_p.wait(e, bsp + 16 * (i - 1))
                s_x.dma(e, out=W[:, (i % 2) * 2048:(i % 2) * 2048 + 2048],
                            in_=inp["xf"][i * 128:(i + 1) * 128, :])
        blk.sync(p1_sync)

        def p1_evac(e, G):
            i, g = G // 2, G % 2
            c_t.wait(e, bct + G + 1)
            ins = None
            for t in range(8):
                kc = g * 8 + t
                ins = e.tensor_copy(
                    out=A[:, kc * 2048 + i * 128:kc * 2048 + (i + 1) * 128],
                    in_=P16[:, t * 128:(t + 1) * 128])
            v_c.inc(ins)

        def p1_vec(e):
            for i in range(NXT + 1):
                if i < NXT:
                    xt = W[:, (i % 2) * 2048:(i % 2) * 2048 + 2048]
                    st = (i % 2) * 8
                    s_x.wait(e, bx + 16 * (i + 1))
                    if SUB != "ln" and i >= 2:
                        c_t.wait(e, bct + 2 * (i - 2) + 2)
                    f_v.cut(e, e.tensor_reduce(out=SCL[:, st:st + 1], in_=xt,
                                               axis=AX.X, op=ALU.add))
                    f_v.cut(e, e.tensor_scalar(out=SCL[:, st + 1:st + 2],
                                               in0=SCL[:, st:st + 1],
                                               scalar1=1.0 / 2048.0,
                                               scalar2=None, op0=ALU.mult))
                    e.tensor_scalar(out=xt, in0=xt, scalar1=SCL[:, st + 1:st + 2],
                                    scalar2=None, op0=ALU.subtract)
                    e.tensor_tensor(out=SM[:, 0:2048], in0=xt, in1=xt, op=ALU.mult)
                    v_a.inc(e.tensor_reduce(out=SCL[:, st + 2:st + 3],
                                            in_=SM[:, 0:2048], axis=AX.X, op=ALU.add))
                    c_s.wait(e, bcs + (i + 1))
                    e.tensor_scalar(out=xt, in0=xt, scalar1=SCL[:, st + 4:st + 5],
                                    scalar2=None, op0=ALU.mult)
                    bb = Bf[:, (i % 2) * 2048:(i % 2) * 2048 + 2048]
                    v_b.inc(e.tensor_copy(out=bb, in_=xt))
                if SUB != "ln" and i >= 1:
                    p1_evac(e, 2 * (i - 1))
                    p1_evac(e, 2 * (i - 1) + 1)
        blk.vector(p1_vec)

        if SUB == "ln" and STOP == 1:
            def p1_scal0(e):
                for i in range(NXT):
                    st = (i % 2) * 8
                    v_a.wait(e, bva + (i + 1))
                    e.activation(out=SCL[:, st + 3:st + 4], in_=SCL[:, st + 2:st + 3],
                                 func=AF.Ln, bias=EPSC[:, 0:1], scale=1.0 / 2048.0)
                    c_s.inc(e.activation(out=SCL[:, st + 4:st + 5],
                                         in_=SCL[:, st + 3:st + 4],
                                         func=AF.Exp, scale=-0.5))
            blk.scalar(p1_scal0)
            _stop_tail()
            return nc

        def p1_scal(e):
            for i in range(NXT):
                st = (i % 2) * 8
                v_a.wait(e, bva + (i + 1))
                e.activation(out=SCL[:, st + 3:st + 4], in_=SCL[:, st + 2:st + 3],
                             func=AF.Ln, bias=EPSC[:, 0:1], scale=1.0 / 2048.0)
                c_s.inc(e.activation(out=SCL[:, st + 4:st + 5], in_=SCL[:, st + 3:st + 4],
                                     func=AF.Exp, scale=-0.5))
        blk.scalar(p1_scal)

        def p1_tens(e):
            for i in range(NXT):
                bb = Bf[:, (i % 2) * 2048:(i % 2) * 2048 + 2048]
                v_b.wait(e, bvb + (i + 1))
                for g in range(2):
                    if 2 * i + g >= 1:
                        v_c.wait(e, bvc + 2 * i + g)
                    ins = None
                    for t in range(8):
                        kc = g * 8 + t
                        ins = e.transpose(out=P16[:, t * 128:(t + 1) * 128],
                                          in_=bb[:, kc * 128:(kc + 1) * 128],
                                          identity=IDN[:])
                    c_t.inc(ins)
        blk.tensor(p1_tens)

        def p1_gp(e):
            for i in range(NTB):
                v_b.wait(e, bvb + (i + 1))
                s_p.dma(e, out=d_xn[i],
                            in_=W[:, (i % 2) * 2048:(i % 2) * 2048 + 2048]
                            )
        blk.gpsimd(p1_gp)

        if SUB == "full":
            assert v_a.n == bva + NXT and c_s.n == bcs + NXT
            assert v_b.n == bvb + NXT and c_t.n == bct + 2 * NXT
            assert v_c.n == bvc + 2 * NXT and s_p.n == bsp + 16 * NTB
        if STOP == 1:
            _stop_tail()
            return nc

        for wnm, kind in [("wkT", "K"), ("wvT", "V"), ("wqT", "Q")]:
            for half in range(2):
                bw, bcs, bct, bvb, bsp = s_x.n, c_s.n, c_t.n, v_b.n, s_p.n

                def p2_sync(e, wnm=wnm, half=half, bcs=bcs,
                            fvb=v_b.n, fsp=s_p.n):
                    e.wait_ge(v_b.h, fvb)
                    s_p.wait(e, fsp)
                    e.wait_ge(c_s.h, bcs)
                    for k in range(NKC):
                        if k >= 2:
                            c_s.wait(e, bcs + (k - 1))
                        s_x.dma(e, 
                            out=W[:, (k % 2) * 1024:(k % 2) * 1024 + 1024],
                            in_=inp[wnm][k * 128:(k + 1) * 128,
                                         half * 1024:half * 1024 + 1024]
                            )
                blk.sync(p2_sync)

                def p2_scal(e, bw=bw, fct=c_t.n):
                    e.wait_ge(c_t.h, fct)
                    for k in range(NKC):
                        s_x.wait(e, bw + 16 * (k + 1))
                        c_s.inc(e.activation(
                            out=Bf[:, k * 1024:(k + 1) * 1024],
                            in_=W[:, (k % 2) * 1024:(k % 2) * 1024 + 1024],
                            func=AF.Copy))
                blk.scalar(p2_scal)

                if kind in ("K", "Q"):
                    ntc = (S if kind == "K" else TOK) // 512

                    def p2_tens(e, ntc=ntc, bcs=bcs, bvb=bvb):
                        e.wait_ge(v_b.h, bvb)
                        for hh in range(8):
                            if hh >= 1:
                                v_b.wait(e, bvb + hh)
                            ins = None
                            for k in range(NKC):
                                c_s.wait(e, bcs + (k + 1))
                                lhs = Bf[:, k * 1024 + hh * 128:k * 1024 + (hh + 1) * 128]
                                for t in range(ntc):
                                    rhs = A[:, k * 2048 + t * 512:k * 2048 + (t + 1) * 512]
                                    ins = e.matmul(out=P32[:, t * 512:(t + 1) * 512],
                                                   lhsT=lhs, rhs=rhs,
                                                   start=(k == 0), stop=(k == NKC - 1))
                            c_t.inc(ins)
                    blk.tensor(p2_tens)

                    def p2_vec(e, ntc=ntc, bct=bct, bsp=bsp):
                        sin = C[:, 0:2048]
                        cos = C[:, 2048:4096]
                        for hh in range(8):
                            c_t.wait(e, bct + hh + 1)
                            if hh >= 2:
                                s_p.wait(e, bsp + 16 * (hh - 1))
                            stg = Bf[:, 16384 + (hh % 2) * 2048:
                                     16384 + (hh % 2) * 2048 + ntc * 512]
                            ins = None
                            for t in range(ntc):
                                ps = P32[:, t * 512:(t + 1) * 512]
                                tok0 = t * 512
                                s1 = SM[:, 0:512]
                                t2 = SM[:, 512:1024]
                                e.tensor_tensor(out=s1[0:64, :], in0=ps[64:128, :],
                                                in1=sin[0:64, tok0:tok0 + 512], op=ALU.mult)
                                e.tensor_tensor(out=s1[64:128, :], in0=ps[0:64, :],
                                                in1=sin[64:128, tok0:tok0 + 512], op=ALU.mult)
                                e.tensor_tensor(out=t2, in0=ps,
                                                in1=cos[:, tok0:tok0 + 512], op=ALU.mult)
                                ins = e.tensor_tensor(out=stg[:, t * 512:(t + 1) * 512],
                                                      in0=t2, in1=s1, op=ALU.add)
                            v_b.inc(ins)
                    blk.vector(p2_vec)

                    def p2_gp(e, kind=kind, half=half, ntc=ntc, bvb=bvb):
                        dst = d_k if kind == "K" else d_q
                        for hh in range(8):
                            v_b.wait(e, bvb + hh + 1)
                            stg = Bf[:, 16384 + (hh % 2) * 2048:
                                     16384 + (hh % 2) * 2048 + ntc * 512]
                            s_p.dma(e, out=dst[half * 8 + hh], in_=stg
                                        )
                    blk.gpsimd(p2_gp)
                    nu = 8
                else:
                    def p2_tens(e, bcs=bcs, bvb=bvb):
                        e.wait_ge(v_b.h, bvb)
                        for tb in range(NXT):
                            if tb >= 2:
                                v_b.wait(e, bvb + tb - 1)
                            pc = (tb % 2) * 1024
                            ins = None
                            for k in range(NKC):
                                c_s.wait(e, bcs + (k + 1))
                                lhs = A[:, k * 2048 + tb * 128:k * 2048 + (tb + 1) * 128]
                                for t in range(2):
                                    rhs = Bf[:, k * 1024 + t * 512:k * 1024 + (t + 1) * 512]
                                    ins = e.matmul(
                                        out=P32[:, pc + t * 512:pc + (t + 1) * 512],
                                        lhsT=lhs, rhs=rhs,
                                        start=(k == 0), stop=(k == NKC - 1))
                            c_t.inc(ins)
                    blk.tensor(p2_tens)

                    def p2_scal2(e, bct=bct, bsp=bsp):
                        for tb in range(NXT):
                            c_t.wait(e, bct + tb + 1)
                            if tb >= 2:
                                s_p.wait(e, bsp + 16 * (tb - 1))
                            pc = (tb % 2) * 1024
                            stg = Bf[:, 20480 + (tb % 2) * 1024:
                                     20480 + (tb % 2) * 1024 + 1024]
                            v_b.inc(e.activation(out=stg, in_=P32[:, pc:pc + 1024],
                                                 func=AF.Copy))
                    blk.scalar(p2_scal2)

                    def p2_gp(e, half=half, bvb=bvb):
                        for tb in range(NXT):
                            v_b.wait(e, bvb + tb + 1)
                            stg = Bf[:, 20480 + (tb % 2) * 1024:
                                     20480 + (tb % 2) * 1024 + 1024]
                            s_p.dma(e, out=d_v[tb * 128:(tb + 1) * 128,
                                                half * 1024:half * 1024 + 1024],
                                        in_=stg)
                    blk.gpsimd(p2_gp)
                    nu = NXT

                assert c_t.n == bct + nu and v_b.n == bvb + nu
                assert s_p.n == bsp + 16 * nu

        if STOP == 2:
            _stop_tail()
            return nc

        bsr, bct, bcs, bva, bvb, bvc = s_x.n, c_t.n, c_s.n, v_a.n, v_b.n, v_c.n
        sp_qb0 = [sp % 8 for sp in range(NSP)]
        sp_nq = [8 - q for q in sp_qb0]

        ct = {}
        n = bct
        for h in range(NH):
            n += 1; ct[("rows", h)] = n
            n += 1; ct[("bbc", h)] = n
            for sp in range(NSP):
                n += 1; ct[("sc", h, sp)] = n
                n += 1; ct[("av", h, sp)] = n
            n += 1; ct[("rbc", h)] = n
        ct_end = n
        cs = {}
        n = bcs
        for h in range(NH):
            n += 1; cs[("sq", h)] = n
            n += 1; cs[("bnd", h)] = n
            for sp in range(NSP):
                n += 1; cs[("exp", h, sp)] = n
        cs_end = n
        va = {}
        n = bva
        for h in range(NH):
            n += 1; va[("mkk", h)] = n
            n += 1; va[("bnd", h)] = n
            for sp in range(NSP):
                n += 1; va[("t", h, sp)] = n
                n += 1; va[("avadd", h, sp)] = n
        va_end = n
        vb = {}
        n = bvb
        for h in range(NH):
            n += 1; vb[("brow", h)] = n
            for sp in range(NSP):
                n += 1; vb[("P", h, sp)] = n
        vb_end = n
        vc = {}
        n = bvc
        for h in range(NH):
            n += 1; vc[("rec", h)] = n
            n += 1; vc[("ao", h)] = n
        vc_end = n

        def kcol(h):
            return (h % 2) * 2048

        def vvcol(h):
            return 4096 + (h % 2) * 2048

        def qcol(h):
            return 8192 + (h % 2) * 1024

        AACC = SCL[:, 4096:5120]
        DACC = SCL[0:1, 5120:6144]

        def p3_sync(e, fsp=s_p.n, fct=c_t.n, fcs=c_s.n):
            s_p.wait(e, fsp)
            e.wait_ge(c_t.h, fct)
            e.wait_ge(c_s.h, fcs)
            for h in range(NH):
                if h >= 2:
                    c_t.wait(e, ct[("av", h - 2, NSP - 1)])
                    c_s.wait(e, cs[("sq", h - 1)])
                s_x.dma(e, out=Bf[:, kcol(h):kcol(h) + 2048], in_=d_k[h])
                s_x.dma(e, out=Bf[:, vvcol(h):vvcol(h) + 2048].rearrange(
                    "p (sb j) -> p sb j", j=128),
                    in_=d_v[:, h * 128:(h + 1) * 128].rearrange(
                        "(sb p) j -> p sb j", p=128))
                s_x.dma(e, out=Bf[:, qcol(h):qcol(h) + 1024], in_=d_q[h])
        blk.sync(p3_sync)

        def p3_scal(e):
            for h in range(NH):
                s_x.wait(e, bsr + 48 * (h + 1))
                e.activation(out=Bf[:, 12288:14336],
                             in_=Bf[:, kcol(h):kcol(h) + 2048], func=AF.Square)
                c_s.inc(e.activation(out=Bf[:, 14336:15360],
                                     in_=Bf[:, qcol(h):qcol(h) + 1024],
                                     func=AF.Square))
                v_a.wait(e, va[("mkk", h)])
                c_s.inc(e.activation(out=SCL[0:1, 3072:4096],
                                     in_=SCL[0:1, 2048:3072], func=AF.Sqrt))
                for sp in range(NSP):
                    nq = sp_nq[sp]
                    v_a.wait(e, va[("t", h, sp)])
                    if sp >= 2:
                        c_t.wait(e, ct[("av", h, sp - 2)])
                    elif h >= 1:
                        c_t.wait(e, ct[("av", h - 1, NSP - 2 + sp)])
                    tst = SM[:, (sp % 2) * 1024:(sp % 2) * 1024 + nq * 128]
                    pst = Bf[:, 10240 + (sp % 2) * 1024:
                             10240 + (sp % 2) * 1024 + nq * 128]
                    c_s.inc(e.activation(out=pst, in_=tst, func=AF.Exp))
        blk.scalar(p3_scal)

        def p3_tens(e, fvb=v_b.n, fvc=v_c.n):
            e.wait_ge(v_b.h, fvb)
            e.wait_ge(v_c.h, fvc)
            for h in range(NH):
                c_s.wait(e, cs[("sq", h)])
                if h >= 1:
                    v_c.wait(e, vc[("ao", h - 1)])
                ins = None
                for t in range(4):
                    ins = e.matmul(out=P32[0:1, t * 512:(t + 1) * 512], lhsT=ONE1[:],
                                   rhs=Bf[:, 12288 + t * 512:12288 + (t + 1) * 512],
                                   start=True, stop=True)
                for t in range(2):
                    ins = e.matmul(out=P32[0:1, 2048 + t * 512:2048 + (t + 1) * 512],
                                   lhsT=ONE1[:],
                                   rhs=Bf[:, 14336 + t * 512:14336 + (t + 1) * 512],
                                   start=True, stop=True)
                c_t.inc(ins)
                v_b.wait(e, vb[("brow", h)])
                ins = None
                for t in range(2):
                    ins = e.matmul(out=P32[:, t * 512:(t + 1) * 512], lhsT=ONER[:],
                                   rhs=Bf[0:1, 15360 + t * 512:15360 + (t + 1) * 512],
                                   start=True, stop=True)
                c_t.inc(ins)
                for sp in range(NSP):
                    q0, nq = sp_qb0[sp], sp_nq[sp]
                    if sp < 2:
                        v_a.wait(e, va[("bnd", h)])
                    else:
                        v_a.wait(e, va[("avadd", h, sp - 2)])
                    ksl = Bf[:, kcol(h) + sp * 128:kcol(h) + (sp + 1) * 128]
                    pcol = (sp % 2) * 1024
                    ins = None
                    nmm = (nq * 128 + 511) // 512
                    for t in range(nmm):
                        w0 = t * 512
                        w1 = min(nq * 128, w0 + 512)
                        rhs = Bf[:, qcol(h) + q0 * 128 + w0:qcol(h) + q0 * 128 + w1]
                        ins = e.matmul(out=P32[:, pcol + w0:pcol + w1],
                                       lhsT=ksl, rhs=rhs, start=True, stop=True)
                    c_t.inc(ins)
                    v_b.wait(e, vb[("P", h, sp)])
                    vsl = Bf[:, vvcol(h) + sp * 128:vvcol(h) + (sp + 1) * 128]
                    pb = Bf[:, 10240 + (sp % 2) * 1024:
                            10240 + (sp % 2) * 1024 + nq * 128]
                    vcol0 = 2048 + (sp % 2) * 1024
                    ins = None
                    for t in range(nmm):
                        w0 = t * 512
                        w1 = min(nq * 128, w0 + 512)
                        ins = e.matmul(out=P32[:, vcol0 + w0:vcol0 + w1],
                                       lhsT=vsl, rhs=pb[:, w0:w1],
                                       start=True, stop=True)
                    for t in range(nmm):
                        w0 = t * 512
                        w1 = min(nq * 128, w0 + 512)
                        ins = e.matmul(out=P32[0:1, pcol + w0:pcol + w1],
                                       lhsT=ONE1[:], rhs=pb[:, w0:w1],
                                       start=True, stop=True)
                    c_t.inc(ins)
                v_c.wait(e, vc[("rec", h)])
                ins = None
                for t in range(2):
                    ins = e.matmul(out=P32[:, t * 512:(t + 1) * 512], lhsT=ONER[:],
                                   rhs=Bf[0:1, 16384 + t * 512:16384 + (t + 1) * 512],
                                   start=True, stop=True)
                c_t.inc(ins)
        blk.tensor(p3_tens)

        def p3_vec(e):
            for h in range(NH):
                e.memset(AACC[:, :], 0.0)
                e.memset(DACC[:, :], 0.0)
                c_t.wait(e, ct[("rows", h)])
                f_v.cut(e, e.tensor_reduce(out=SCL[0:1, 1024:1025],
                                           in_=P32[0:1, 0:2048],
                                           axis=AX.X, op=ALU.max))
                v_a.inc(e.tensor_scalar(out=SCL[0:1, 2048:3072],
                                        in0=P32[0:1, 2048:3072],
                                        scalar1=SCL[0:1, 1024:1025],
                                        scalar2=None, op0=ALU.mult))
                c_s.wait(e, cs[("bnd", h)])
                v_b.inc(e.tensor_scalar(out=Bf[0:1, 15360:16384],
                                        in0=SCL[0:1, 3072:4096], scalar1=RSD,
                                        scalar2=-40.0, op0=ALU.mult,
                                        op1=ALU.add))
                c_t.wait(e, ct[("bbc", h)])
                v_a.inc(e.tensor_scalar(out=SCL[:, 0:1024], in0=P32[:, 0:1024],
                                        scalar1=-1.0, scalar2=None, op0=ALU.mult))

                def avadd(sp):
                    q0n, nqn = sp_qb0[sp], sp_nq[sp]
                    c_t.wait(e, ct[("av", h, sp)])
                    vcol0 = 2048 + (sp % 2) * 1024
                    pcol = (sp % 2) * 1024
                    e.scalar_tensor_tensor(
                        out=AACC[:, q0n * 128:1024],
                        in0=P32[:, vcol0:vcol0 + nqn * 128], scalar=1.0,
                        in1=AACC[:, q0n * 128:1024],
                        op0=ALU.mult, op1=ALU.add)
                    v_a.inc(e.scalar_tensor_tensor(
                        out=DACC[0:1, q0n * 128:1024],
                        in0=P32[0:1, pcol:pcol + nqn * 128], scalar=1.0,
                        in1=DACC[0:1, q0n * 128:1024],
                        op0=ALU.mult, op1=ALU.add))

                for sp in range(NSP):
                    q0, nq = sp_qb0[sp], sp_nq[sp]
                    c_t.wait(e, ct[("sc", h, sp)])
                    pcol = (sp % 2) * 1024
                    tst = SM[:, (sp % 2) * 1024:(sp % 2) * 1024 + nq * 128]
                    v_a.inc(e.scalar_tensor_tensor(
                        out=tst, in0=P32[:, pcol:pcol + nq * 128], scalar=RSD,
                        in1=SCL[:, q0 * 128:q0 * 128 + nq * 128],
                        op0=ALU.mult, op1=ALU.add))
                    c_s.wait(e, cs[("exp", h, sp)])
                    pb = Bf[:, 10240 + (sp % 2) * 1024:
                            10240 + (sp % 2) * 1024 + nq * 128]
                    if sp < 8:
                        ins = e.tensor_tensor(out=pb[:, 0:128], in0=pb[:, 0:128],
                                              in1=TRI[:], op=ALU.mult)
                    else:
                        ins = e.tensor_scalar(out=pb[:, 0:128], in0=pb[:, 0:128],
                                              scalar1=PARF[:, 0:1], scalar2=None,
                                              op0=ALU.mult)
                    v_b.inc(ins)
                    avadd(sp)
                f_v.cut(e, e.engine_nop())
                e.reciprocal(out=DACC[:, :], in_=DACC[:, :])
                v_c.inc(e.tensor_copy(out=Bf[0:1, 16384:17408], in_=DACC[:, :]))
                c_t.wait(e, ct[("rbc", h)])
                e.tensor_copy(out=SM[:, 0:1024], in_=P32[:, 0:1024])
                v_c.inc(e.tensor_tensor(out=A[:, h * 1024:(h + 1) * 1024],
                                        in0=AACC[:, :], in1=SM[:, 0:1024],
                                        op=ALU.mult))
        blk.vector(p3_vec)

        assert c_t.n == ct_end and c_s.n == cs_end
        assert v_a.n == va_end and v_b.n == vb_end and v_c.n == vc_end

        if STOP == 3:
            _stop_tail()
            return nc

        for half in range(2):
            bw, bcs, bct, bvb, bsp = s_x.n, c_s.n, c_t.n, v_b.n, s_p.n

            def p4_sync(e, half=half, bcs=bcs, fcs=c_s.n, fvc=v_c.n):
                e.wait_ge(c_s.h, fcs)
                e.wait_ge(v_c.h, fvc)
                for k in range(8):
                    kk = half * 8 + k
                    if k >= 2:
                        c_s.wait(e, bcs + (k - 1))
                    s_x.dma(e, out=W[:, (k % 2) * 2048:(k % 2) * 2048 + 2048],
                                in_=inp["woT"][kk * 128:(kk + 1) * 128, :]
                                )
            blk.sync(p4_sync)

            def p4_scal(e, bw=bw, fct=c_t.n):
                e.wait_ge(c_t.h, fct)
                for k in range(8):
                    s_x.wait(e, bw + 16 * (k + 1))
                    c_s.inc(e.activation(out=Bf[:, k * 2048:(k + 1) * 2048],
                                         in_=W[:, (k % 2) * 2048:(k % 2) * 2048 + 2048],
                                         func=AF.Copy))
            blk.scalar(p4_scal)

            def p4_tens(e, half=half, bcs=bcs, bvb=bvb, fvc=v_c.n,
                        fva=v_a.n):
                e.wait_ge(v_c.h, fvc)
                e.wait_ge(v_a.h, fva)
                e.wait_ge(v_b.h, bvb)
                for tb in range(NTB):
                    if tb >= 1:
                        v_b.wait(e, bvb + tb)
                    ins = None
                    for k in range(8):
                        c_s.wait(e, bcs + (k + 1))
                        hh = half * 8 + k
                        lhs = A[:, hh * 1024 + tb * 128:hh * 1024 + (tb + 1) * 128]
                        for t in range(4):
                            rhs = Bf[:, k * 2048 + t * 512:k * 2048 + (t + 1) * 512]
                            ins = e.matmul(out=P32[:, t * 512:(t + 1) * 512],
                                           lhsT=lhs, rhs=rhs,
                                           start=(k == 0), stop=(k == 7))
                    c_t.inc(ins)
            blk.tensor(p4_tens)

            def p4_gp(e, half=half, bvb=bvb, bsp=bsp):
                for tb in range(NTB):
                    if half == 1:
                        s_p.dma(e, out=C[:, (tb % 2) * 2048:(tb % 2) * 2048 + 2048],
                                    in_=d_acc[tb])
                    v_b.wait(e, bvb + tb + 1)
                    s_p.dma(e, out=d_acc[tb],
                                in_=C[:, (tb % 2) * 2048:(tb % 2) * 2048 + 2048]
                                )
            blk.gpsimd(p4_gp)

            def p4_vec(e, half=half, bct=bct, bsp=bsp):
                for tb in range(NTB):
                    c_t.wait(e, bct + tb + 1)
                    cst = C[:, (tb % 2) * 2048:(tb % 2) * 2048 + 2048]
                    if half == 0:
                        if tb >= 2:
                            s_p.wait(e, bsp + 16 * (tb - 1))
                        v_b.inc(e.tensor_copy(out=cst, in_=P32[:, 0:2048]))
                    else:
                        s_p.wait(e, bsp + 32 * tb + 16)
                        if tb >= 2:
                            s_p.wait(e, bsp + 32 * (tb - 1))
                        v_b.inc(e.scalar_tensor_tensor(
                            out=cst, in0=P32[:, 0:2048], scalar=1.0, in1=cst,
                            op0=ALU.mult, op1=ALU.add))
            blk.vector(p4_vec)

            assert c_t.n == bct + NTB and v_b.n == bvb + NTB

        if STOP == 4:
            _stop_tail()
            return nc

        bxn, bva, bct, bvc, bsp = s_x.n, v_a.n, c_t.n, v_c.n, s_p.n

        def p5p_sync(e, fcs=c_s.n):
            e.wait_ge(c_s.h, fcs)
            for tb in range(NTB):
                if tb >= 2:
                    v_a.wait(e, bva + (tb - 1))
                s_x.dma(e, out=W[:, (tb % 2) * 2048:(tb % 2) * 2048 + 2048],
                            in_=d_xn[tb])
        blk.sync(p5p_sync)

        def p5p_evac(e, G):
            tb, g = G // 2, G % 2
            c_t.wait(e, bct + G + 1)
            if tb >= 2 and g == 0:
                s_p.wait(e, bsp + 16 * (tb - 1))
            ins = None
            for t in range(8):
                kc = g * 8 + t
                ins = e.tensor_copy(
                    out=Bf[:, 16384 + (tb % 2) * 2048 + kc * 128:
                           16384 + (tb % 2) * 2048 + (kc + 1) * 128],
                    in_=P16[:, t * 128:(t + 1) * 128])
            v_c.inc(ins)

        def p5p_vec(e):
            for tb in range(NTB + 1):
              if tb < NTB:
                xt = W[:, (tb % 2) * 2048:(tb % 2) * 2048 + 2048]
                s_x.wait(e, bxn + 16 * (tb + 1))
                if tb >= 2:
                    c_t.wait(e, bct + 2 * (tb - 2) + 2)
                f_v.cut(e, e.tensor_reduce(
                    out=SCL[:, 4160:4224],
                    in_=xt.rearrange("p (j b) -> p b j", b=64),
                    axis=AX.X, op=ALU.max, apply_absolute_value=True))
                e.reciprocal(out=SCL[:, 4224:4288], in_=SCL[:, 4160:4224])
                f_v.cut(e, e.tensor_scalar(out=SCL[:, 4288:4352],
                                           in0=SCL[:, 4160:4224],
                                           scalar1=1.0 / 224.0, scalar2=None,
                                           op0=ALU.mult))
                quot = SM[:, 0:2048]
                e.scalar_tensor_tensor(
                    out=quot.rearrange("p (j b) -> p j b", b=64),
                    in0=xt.rearrange("p (j b) -> p j b", b=64),
                    scalar=224.0, op0=ALU.mult,
                    in1=SCL[:, 4224:4288].rearrange(
                        "p (o b) -> p o b", o=1).broadcast_to([128, 32, 64]),
                    op1=ALU.mult)
                e.tensor_copy(out=Q8[:, 0:2048], in_=quot)
                xqs = 20480 + (tb % 2) * 2048
                v_a.inc(e.tensor_tensor(
                    out=Bf[:, xqs:xqs + 2048].rearrange("p (j b) -> p j b", b=64),
                    in0=Q8[:, 0:2048].rearrange("p (j b) -> p j b", b=64),
                    in1=SCL[:, 4288:4352].rearrange(
                        "p (o b) -> p o b", o=1).broadcast_to([128, 32, 64]),
                    op=ALU.mult))
              if tb >= 1:
                p5p_evac(e, 2 * (tb - 1))
                p5p_evac(e, 2 * (tb - 1) + 1)
        blk.vector(p5p_vec)

        def p5p_tens(e, fvb=v_b.n):
            e.wait_ge(v_b.h, fvb)
            e.wait_ge(v_c.h, bvc)
            for tb in range(NTB):
                v_a.wait(e, bva + tb + 1)
                for g in range(2):
                    if 2 * tb + g >= 1:
                        v_c.wait(e, bvc + 2 * tb + g)
                    ins = None
                    xqs = 20480 + (tb % 2) * 2048
                    for t in range(8):
                        kc = g * 8 + t
                        ins = e.transpose(
                            out=P16[:, t * 128:(t + 1) * 128],
                            in_=Bf[:, xqs + kc * 128:xqs + (kc + 1) * 128],
                            identity=IDN[:])
                    c_t.inc(ins)
        blk.tensor(p5p_tens)

        def p5p_gp(e):
            for tb in range(NTB):
                v_c.wait(e, bvc + 2 * tb + 2)
                s_p.dma(e, out=d_xq[tb],
                            in_=Bf[:, 16384 + (tb % 2) * 2048:
                                   16384 + (tb % 2) * 2048 + 2048]
                            )
        blk.gpsimd(p5p_gp)

        assert v_a.n == bva + NTB and c_t.n == bct + 2 * NTB
        assert v_c.n == bvc + 2 * NTB and s_p.n == bsp + 16 * NTB

        if STOP == 5:
            _stop_tail()
            return nc

        def pm_wslot(u):
            if u < 32:
                return W[:, (u % 4) * 1024:(u % 4) * 1024 + 1024]
            return W[:, (u % 2) * 2048:(u % 2) * 2048 + 2048]

        def pm_macc(u):
            if u < 16:
                return SCL[:, 0:1024]
            if u < 32:
                return SCL[:, 1024:2048]
            return SCL[:, 2048:4096]

        def pm_tile(e, u, bw):
            s_x.wait(e, bw + 16 * (u + 1))
            t_ = pm_wslot(u)
            m = pm_macc(u)
            e.scalar_tensor_tensor(out=m, in0=t_, scalar=1.0, in1=m,
                                   op0=ALU.mult, op1=ALU.max)
            v_a.inc(e.scalar_tensor_tensor(out=m, in0=t_, scalar=-1.0, in1=m,
                                           op0=ALU.mult, op1=ALU.max))

        def pm_memsets(e):
            e.memset(SCL[:, 0:1024], 0.0)
            e.memset(SCL[:, 1024:2048], 0.0)
            e.memset(SCL[:, 2048:4096], 0.0)

        def pm_folds(e):
            for wi in range(2):
                mm = SCL[:, wi * 1024:wi * 1024 + 1024]
                rc = SCL[:, 4096 + wi * 1024:4096 + wi * 1024 + 1024]
                e.tensor_copy(out=SM[0:64, 0:1024], in_=mm[64:128, :])
                e.tensor_tensor(out=mm[0:64, :], in0=SM[0:64, 0:1024],
                                in1=mm[0:64, :], op=ALU.max)
                e.reciprocal(out=rc[0:64, :], in_=mm[0:64, :])
                e.tensor_scalar(out=mm[0:64, :], in0=mm[0:64, :],
                                scalar1=1.0 / 224.0, scalar2=None,
                                op0=ALU.mult)
                e.tensor_copy(out=mm[64:128, :], in_=mm[0:64, :])
                e.tensor_copy(out=rc[64:128, :], in_=rc[0:64, :])
            mm = SCL[:, 2048:4096]
            rc = SCL[:, 6144:8192]
            e.tensor_copy(out=SM[0:64, 0:2048], in_=mm[64:128, :])
            e.tensor_tensor(out=mm[0:64, :], in0=SM[0:64, 0:2048],
                            in1=mm[0:64, :], op=ALU.max)
            e.tensor_copy(out=SM[0:32, 0:2048], in_=mm[32:64, :])
            e.tensor_tensor(out=mm[0:32, :], in0=SM[0:32, 0:2048],
                            in1=mm[0:32, :], op=ALU.max)
            e.reciprocal(out=rc[0:32, :], in_=mm[0:32, :])
            e.tensor_scalar(out=mm[0:32, :], in0=mm[0:32, :],
                            scalar1=1.0 / 224.0, scalar2=None, op0=ALU.mult)
            e.tensor_copy(out=mm[32:64, :], in_=mm[0:32, :])
            e.tensor_copy(out=mm[64:128, :], in_=mm[0:64, :])
            e.tensor_copy(out=rc[32:64, :], in_=rc[0:32, :])
            e.tensor_copy(out=rc[64:128, :], in_=rc[0:64, :])

        def pm_dma(e, u, sg):
            if u < 32:
                wi, k = divmod(u, 16)
                wnm = "gwT" if wi == 0 else "uwT"
                s_x.dma(e, out=pm_wslot(u),
                        in_=inp[wnm][k * 128:(k + 1) * 128,
                                     sg * SEG:sg * SEG + SEG])
            else:
                k = u - 32
                s_x.dma(e, out=pm_wslot(u),
                        in_=inp["dwT"][(sg * 8 + k) * 128:
                                       (sg * 8 + k + 1) * 128, :])

        V0s = v_a.n
        assert V0s == bva + NTB
        mc_cur = lambda u, V0s=V0s: V0s + u + 1
        bw1_0 = s_x.n

        def p5s_sync0(e, V0s=V0s, mc=mc_cur):
            for u in range(40):
                if u < 2:
                    v_a.wait(e, V0s - 1)
                elif u < 4:
                    v_a.wait(e, V0s)
                elif u < 32:
                    v_a.wait(e, mc(u - 4))
                elif u == 32:
                    v_a.wait(e, mc(29))
                elif u == 33:
                    v_a.wait(e, mc(31))
                else:
                    v_a.wait(e, mc(u - 2))
                pm_dma(e, u, 0)
        blk.sync(p5s_sync0)

        def p5s_vec0(e, bw=bw1_0):
            pm_memsets(e)
            for u in range(40):
                pm_tile(e, u, bw)
            pm_folds(e)
        blk.vector(p5s_vec0)
        assert v_a.n == V0s + 40

        if STOP == 50:
            _stop_tail()
            return nc

        ct_guA = c_t.n
        ct_dnB = c_t.n
        for sg in range(NSEG):
            bva2, bcs2, bw2 = v_a.n, c_s.n, s_x.n
            mc_p = mc_cur

            def p5q_sync(e, sg=sg, bva2=bva2, mc_p=mc_p):
                for u in range(40):
                    if u < 2:
                        v_a.wait(e, mc_p(38))
                    elif u < 4:
                        v_a.wait(e, mc_p(39))
                    elif u < 32:
                        v_a.wait(e, bva2 + 2 * (u - 4) + 1)
                    elif u <= 33:
                        v_a.wait(e, bva2 + 63)
                    else:
                        v_a.wait(e, bva2 + 2 * (u - 2) + 1)
                    pm_dma(e, u, sg)
            blk.sync(p5q_sync)

            if sg < NSEG - 1:
                bw1n = s_x.n
                V3 = bva2 + 80
                mc_next = lambda u, V3=V3: V3 + 3 * (u // 5 + 1) + u + 1

                def p5s_syncN(e, sg=sg, bva2=bva2, mc=mc_next):
                    for u in range(40):
                        if u < 2:
                            v_a.wait(e, bva2 + 77)
                        elif u < 4:
                            v_a.wait(e, bva2 + 79)
                        elif u < 32:
                            v_a.wait(e, mc(u - 4))
                        elif u == 32:
                            v_a.wait(e, mc(29))
                        elif u == 33:
                            v_a.wait(e, mc(31))
                        else:
                            v_a.wait(e, mc(u - 2))
                        pm_dma(e, u, sg + 1)
                blk.sync(p5s_syncN)
            else:
                bw1n = None
                mc_next = None

            def p5q_vec(e, sg=sg, bw2=bw2, bcs2=bcs2, ct_guA=ct_guA,
                        ct_dnB=ct_dnB):
                for u in range(40):
                    s_x.wait(e, bw2 + 16 * (u + 1))
                    t_ = pm_wslot(u)
                    if u < 32:
                        wi, k = divmod(u, 16)
                        rec = SCL[:, 4096 + wi * 1024:4096 + wi * 1024 + 1024]
                        s2 = SCL[:, wi * 1024:wi * 1024 + 1024]
                        sm = SM[:, (u % 2) * 1024:(u % 2) * 1024 + 1024]
                        q8 = Q8[:, (u % 2) * 1024:(u % 2) * 1024 + 1024]
                        dst = A[:, wi * 16384 + k * 1024:
                                wi * 16384 + (k + 1) * 1024]
                    else:
                        k = u - 32
                        rec = SCL[:, 6144:8192]
                        s2 = SCL[:, 2048:4096]
                        sm = SM[:, 0:2048]
                        q8 = Q8[:, 0:2048]
                        dst = Bf[:, k * 2048:(k + 1) * 2048]
                    if u == 0:
                        c_t.wait(e, ct_guA)
                        c_s.wait(e, bcs2)
                    if u == 32:
                        c_t.wait(e, ct_dnB)
                    if u >= 2:
                        if u < 32:
                            c_s.wait(e, bcs2 + u - 1)
                        elif u == 32:
                            c_s.wait(e, bcs2 + 32)
                        else:
                            c_s.wait(e, bcs2 + u)
                    v_a.inc(e.scalar_tensor_tensor(out=sm, in0=t_, scalar=224.0,
                                                   in1=rec, op0=ALU.mult,
                                                   op1=ALU.mult))
                    c_s.wait(e, bcs2 + u + 1)
                    v_a.inc(e.tensor_tensor(out=dst, in0=q8, in1=s2,
                                            op=ALU.mult))
                if sg < NSEG - 1:
                    pm_memsets(e)
            blk.vector(p5q_vec)

            def p5q_scal(e, bva2=bva2):
                for u in range(40):
                    v_a.wait(e, bva2 + 2 * u + 1)
                    if u < 32:
                        sm = SM[:, (u % 2) * 1024:(u % 2) * 1024 + 1024]
                        q8 = Q8[:, (u % 2) * 1024:(u % 2) * 1024 + 1024]
                    else:
                        sm = SM[:, 0:2048]
                        q8 = Q8[:, 0:2048]
                    c_s.inc(e.activation(out=q8, in_=sm, func=AF.Copy))
            blk.scalar(p5q_scal)
            assert v_a.n == bva2 + 80 and c_s.n == bcs2 + 40

            bct3, bcs3, bva3, bvb3, bvc3, bsp3 = (c_t.n, c_s.n, v_a.n,
                                                  v_b.n, v_c.n, s_p.n)
            vstep = 8 if sg < NSEG - 1 else 3
            CT = lambda tb, i, b=bct3: b + 4 * tb + i + 1
            CS = lambda tb, i, b=bcs3: b + 3 * tb + i + 1
            VA = lambda tb, i, b=bva3, vstep=vstep: b + vstep * tb + i + 1
            VB = lambda tb, b=bvb3: b + tb + 1
            VC = lambda tb, i, b=bvc3: b + 2 * tb + i + 1
            XQC = [2, 4, 6, 10, 14, 18, 22, 26]
            RLC = [7, 11, 15, 19, 23, 27, 29, 31]

            def p5m_gp(e, CT=CT, VC=VC):
                for tb in range(2):
                    s_p.dma(e, out=Bf[:, 18432 + (tb % 2) * 1024:
                                      18432 + (tb % 2) * 1024 + 1024],
                            in_=d_xq[tb][:, 0:1024])
                    s_p.dma(e, out=Bf[:, 20480 + (tb % 2) * 1024:
                                      20480 + (tb % 2) * 1024 + 1024],
                            in_=d_xq[tb][:, 1024:2048])
                for tb in range(NTB):
                    if tb + 2 < NTB:
                        t2 = tb + 2
                        c_t.wait(e, CT(tb, 1))
                        s_p.dma(e, out=Bf[:, 18432 + (t2 % 2) * 1024:
                                          18432 + (t2 % 2) * 1024 + 1024],
                                in_=d_xq[t2][:, 0:1024])
                        s_p.dma(e, out=Bf[:, 20480 + (t2 % 2) * 1024:
                                          20480 + (t2 % 2) * 1024 + 1024],
                                in_=d_xq[t2][:, 1024:2048])
                    if tb >= 2:
                        v_c.wait(e, VC(tb - 2, 1))
                    s_p.dma(e, out=C[:, (tb % 2) * 2048:(tb % 2) * 2048 + 2048],
                            in_=d_acc[tb])
                    v_c.wait(e, VC(tb, 1))
                    s_p.dma(e, out=d_acc[tb],
                            in_=C[:, (tb % 2) * 2048:(tb % 2) * 2048 + 2048])
            blk.gpsimd(p5m_gp)

            def xqsl(tb, k):
                if k < 8:
                    base = 18432 + (tb % 2) * 1024
                    return Bf[:, base + k * 128:base + (k + 1) * 128]
                base = 20480 + (tb % 2) * 1024
                return Bf[:, base + (k - 8) * 128:base + (k - 7) * 128]

            def p5m_tens(e, bva2=bva2, bsp3=bsp3, XQC=XQC, VA=VA, VB=VB,
                         VC=VC, fvb=v_b.n, fvc=v_c.n):
                e.wait_ge(v_b.h, fvb)
                e.wait_ge(v_c.h, fvc)
                for tb in range(NTB):
                    s_p.wait(e, bsp3 + 16 * XQC[tb])
                    for half in range(2):
                        g = 2 * tb + half
                        if g >= 1:
                            v_a.wait(e, VA((g - 1) // 2, (g - 1) % 2))
                        ins = None
                        for k in range(NKC):
                            v_a.wait(e, bva2 + 2 * (16 + k) + 2)
                            lhs = xqsl(tb, k)
                            gt = A[:, k * 1024 + half * 512:
                                   k * 1024 + half * 512 + 512]
                            ut = A[:, 16384 + k * 1024 + half * 512:
                                   16384 + k * 1024 + half * 512 + 512]
                            e.matmul(out=P32[:, 0:512], lhsT=lhs, rhs=gt,
                                     start=(k == 0), stop=(k == NKC - 1))
                            ins = e.matmul(out=P32[:, 512:1024], lhsT=lhs,
                                           rhs=ut, start=(k == 0),
                                           stop=(k == NKC - 1))
                        c_t.inc(ins)
                    v_b.wait(e, VB(tb))
                    if tb >= 1:
                        v_c.wait(e, VC(tb - 1, 0))
                    ins = None
                    for t in range(8):
                        ins = e.transpose(out=P16[:, t * 128:(t + 1) * 128],
                                          in_=Bf[:, 16384 + t * 128:
                                                 16384 + (t + 1) * 128],
                                          identity=IDN[:])
                    c_t.inc(ins)
                    v_c.wait(e, VC(tb, 0))
                    if tb >= 1:
                        v_c.wait(e, VC(tb - 1, 1))
                    ins = None
                    for kq in range(8):
                        v_a.wait(e, bva2 + 2 * (32 + kq) + 2)
                        lhs = Bf[:, 17408 + kq * 128:17408 + (kq + 1) * 128]
                        for t in range(4):
                            rhs = Bf[:, kq * 2048 + t * 512:
                                     kq * 2048 + (t + 1) * 512]
                            ins = e.matmul(out=P32[:, 1024 + t * 512:
                                                  1024 + (t + 1) * 512],
                                           lhsT=lhs, rhs=rhs,
                                           start=(kq == 0), stop=(kq == 7))
                    c_t.inc(ins)
            blk.tensor(p5m_tens)

            def p5m_scal(e, CT=CT, CS=CS, VA=VA, VB=VB):
                for tb in range(NTB):
                    for half in range(2):
                        c_t.wait(e, CT(tb, half))
                        if half == 1:
                            v_a.wait(e, VA(tb, 0))
                        c_s.inc(e.activation(out=GS[:, 0:512],
                                             in_=P32[:, 0:512], func=AF.Silu))
                    v_a.wait(e, VA(tb, 2))
                    if tb >= 1:
                        v_b.wait(e, VB(tb - 1))
                    c_s.inc(e.activation(out=Q8[:, 0:1024], in_=SM[:, 0:1024],
                                         func=AF.Copy))
            blk.scalar(p5m_scal)

            def p5m_vec(e, sg=sg, CT=CT, CS=CS, VB=VB, VC=VC, bsp3=bsp3,
                        bw1n=bw1n, RLC=RLC):
                for tb in range(NTB):
                    for half in range(2):
                        c_s.wait(e, CS(tb, half))
                        v_a.inc(e.tensor_tensor(
                            out=SM[:, 1024 + half * 512:1536 + half * 512],
                            in0=P32[:, 512:1024], in1=GS[:, 0:512],
                            op=ALU.mult))
                    ht = SM[:, 1024:2048]
                    f_v.cut(e, e.tensor_reduce(
                        out=SCL[:, 4096:4128],
                        in_=ht.rearrange("p (j b) -> p b j", b=32),
                        axis=AX.X, op=ALU.max, apply_absolute_value=True))
                    e.reciprocal(out=SCL[:, 4128:4160], in_=SCL[:, 4096:4128])
                    f_v.cut(e, e.tensor_scalar(out=SCL[:, 4160:4192],
                                               in0=SCL[:, 4096:4128],
                                               scalar1=1.0 / 224.0,
                                               scalar2=None, op0=ALU.mult))
                    v_a.inc(e.scalar_tensor_tensor(
                        out=SM[:, 0:1024].rearrange("p (j b) -> p j b", b=32),
                        in0=ht.rearrange("p (j b) -> p j b", b=32),
                        scalar=224.0, op0=ALU.mult,
                        in1=SCL[:, 4128:4160].rearrange(
                            "p (o b) -> p o b", o=1).broadcast_to([128, 32, 32]),
                        op1=ALU.mult))
                    c_s.wait(e, CS(tb, 2))
                    v_b.inc(e.tensor_tensor(
                        out=Bf[:, 16384:17408].rearrange("p (j b) -> p j b", b=32),
                        in0=Q8[:, 0:1024].rearrange("p (j b) -> p j b", b=32),
                        in1=SCL[:, 4160:4192].rearrange(
                            "p (o b) -> p o b", o=1).broadcast_to([128, 32, 32]),
                        op=ALU.mult))
                    c_t.wait(e, CT(tb, 2))
                    ins = None
                    for t in range(8):
                        ins = e.tensor_copy(out=Bf[:, 17408 + t * 128:
                                                  17408 + (t + 1) * 128],
                                            in_=P16[:, t * 128:(t + 1) * 128])
                    v_c.inc(ins)
                    c_t.wait(e, CT(tb, 3))
                    s_p.wait(e, bsp3 + 16 * RLC[tb])
                    cst = C[:, (tb % 2) * 2048:(tb % 2) * 2048 + 2048]
                    v_c.inc(e.scalar_tensor_tensor(
                        out=cst, in0=P32[:, 1024:3072], scalar=1.0, in1=cst,
                        op0=ALU.mult, op1=ALU.add))
                    if sg < NSEG - 1:
                        for j in range(5):
                            pm_tile(e, 5 * tb + j, bw1n)
                if sg < NSEG - 1:
                    c_s.wait(e, CS(NTB - 1, 2))
                    pm_folds(e)
            blk.vector(p5m_vec)

            assert c_t.n == bct3 + 4 * NTB and c_s.n == bcs3 + 3 * NTB
            assert v_a.n == bva3 + vstep * NTB and v_b.n == bvb3 + NTB
            assert v_c.n == bvc3 + 2 * NTB and s_p.n == bsp3 + 16 * 32
            ct_guA = bct3 + 4 * (NTB - 1) + 2
            ct_dnB = bct3 + 4 * NTB
            mc_cur = mc_next
            if STOP == 60 + sg:
                _stop_tail()
                return nc

        if STOP == 6:
            _stop_tail()
            return nc

        bx6, bsp6, bva6 = s_x.n, s_p.n, v_a.n
        p6_order = []
        for tb in range(NTB):
            p6_order.append(("L", tb))
            if tb >= 2:
                p6_order.append(("S", tb - 2))
        p6_order += [("S", NTB - 2), ("S", NTB - 1)]
        p6_cum = {}
        cnt6 = 0
        for it6 in p6_order:
            cnt6 += 16
            p6_cum[it6] = bx6 + cnt6

        def p6_sync(e, fva=v_a.n):
            e.wait_ge(v_a.h, fva)
            for kind, tb in p6_order:
                if kind == "L":
                    if tb >= 2:
                        v_a.wait(e, bva6 + (tb - 1))
                    s_x.dma(e, out=W[:, (tb % 2) * 2048:(tb % 2) * 2048 + 2048],
                                in_=inp["xf"][tb * 128:(tb + 1) * 128, :]
                                )
                else:
                    v_a.wait(e, bva6 + (tb + 1))
                    s_x.dma(e, out=out_own[tb * 128:(tb + 1) * 128, :],
                                in_=SM[:, 0:2048])
            s_x.wait(e, s_x.n)
        blk.sync(p6_sync)

        def p6_gp(e, fvc=v_c.n):
            e.wait_ge(v_c.h, fvc)
            for tb in range(NTB):
                if tb >= 2:
                    v_a.wait(e, bva6 + (tb - 1))
                s_p.dma(e, out=C[:, (tb % 2) * 2048:(tb % 2) * 2048 + 2048],
                            in_=d_acc[tb])
        blk.gpsimd(p6_gp)

        def p6_vec(e):
            for tb in range(NTB):
                s_x.wait(e, p6_cum[("L", tb)])
                s_p.wait(e, bsp6 + 16 * (tb + 1))
                if tb >= 1:
                    s_x.wait(e, p6_cum[("S", tb - 1)])
                cst = C[:, (tb % 2) * 2048:(tb % 2) * 2048 + 2048]
                xst = W[:, (tb % 2) * 2048:(tb % 2) * 2048 + 2048]
                v_a.inc(e.tensor_tensor(
                    out=SM[:, 0:2048].rearrange("p (b j) -> p b j", j=32),
                    in0=cst.rearrange("p (j b) -> p b j", b=64),
                    in1=xst.rearrange("p (j b) -> p b j", b=64),
                    op=ALU.add))
        blk.vector(p6_vec)

    return nc


def _perms():
    permH = np.empty(D, np.int64)
    for j in range(32):
        for b in range(64):
            permH[j * 64 + b] = 32 * b + j
    rho = np.empty(D, np.int64)
    for h in range(NH):
        for k in range(DK):
            rho[h * DK + k] = h * DK + (2 * k if k < 64 else 2 * (k - 64) + 1)
    permI = np.empty(II, np.int64)
    for sg in range(NSEG):
        for j in range(32):
            for b in range(32):
                permI[sg * SEG + j * 32 + b] = sg * SEG + 32 * b + j
    return permH, rho, permI


def kernel(L0_attn_x, L0_attn_wq, L0_attn_wk, L0_attn_wv, L0_attn_wo,
           L0_ffn_gate_w, L0_ffn_up_w, L0_ffn_down_w):
    x = np.ascontiguousarray(np.asarray(L0_attn_x, np.float32))
    wq = np.asarray(L0_attn_wq, np.float32)
    wk = np.asarray(L0_attn_wk, np.float32)
    wv = np.asarray(L0_attn_wv, np.float32)
    wo = np.asarray(L0_attn_wo, np.float32)
    gw = np.asarray(L0_ffn_gate_w, np.float32)
    uw = np.asarray(L0_ffn_up_w, np.float32)
    dw = np.asarray(L0_ffn_down_w, np.float32)

    permH, rho, permI = _perms()
    wqT = np.ascontiguousarray(wq[rho][:, permH].T)
    wkT = np.ascontiguousarray(wk[rho][:, permH].T)
    wvT = np.ascontiguousarray(wv[:, permH].T)
    woT = np.ascontiguousarray(wo[permH].T)
    gwT = np.ascontiguousarray(gw[permI][:, permH].T)
    uwT = np.ascontiguousarray(uw[permI][:, permH].T)
    dwT = np.ascontiguousarray(dw[permH][:, permI].T)
    tri = np.ascontiguousarray(np.triu(np.ones((128, 128)))).astype(
        ml_dtypes.bfloat16)
    ident = np.eye(128).astype(ml_dtypes.bfloat16)
    inv = (np.float32(10000.0) **
           (-(np.arange(0, DK, 2).astype(np.float32) / np.float32(DK))))

    nc = build_nc()
    in_maps = []
    for c in range(8):
        b, par = divmod(c, 2)
        ordbl = [2 * j + par for j in range(8)] + [2 * j + (1 - par)
                                                  for j in range(8)]
        idx = np.concatenate([np.arange(g * 128, (g + 1) * 128) for g in ordbl])
        xf = np.ascontiguousarray(x[b][idx][:, permH])
        pos = idx.astype(np.float32)
        ang = inv[:, None].astype(np.float32) * pos[None, :].astype(np.float32)
        sn = np.sin(ang).astype(np.float32)
        cs_ = np.cos(ang).astype(np.float32)
        sint = np.ascontiguousarray(np.concatenate([-sn, sn], 0))
        cost = np.ascontiguousarray(np.concatenate([cs_, cs_], 0))
        parf = np.full((128, 1), float(par), np.float32)
        in_maps.append({
            "xf": xf, "wqT": wqT, "wkT": wkT, "wvT": wvT, "woT": woT,
            "gwT": gwT, "uwT": uwT, "dwT": dwT, "sint": sint, "cost": cost,
            "trimask": tri, "parf": parf, "ident": ident,
        })

    res = run_bass_kernel_spmd(nc, in_maps, core_ids=list(range(8)))
    out = np.empty((4, S, D), np.float32)
    for c in range(8):
        b, par = divmod(c, 2)
        o = res.results[c]["out_own"]
        for j in range(8):
            out[b][(2 * j + par) * 128:(2 * j + par + 1) * 128] = \
                o[j * 128:(j + 1) * 128]
    return out


# version 09
# speedup vs baseline: 1.2101x; 1.0051x over previous
import contextlib
import os
import numpy as np
import ml_dtypes
import concourse.bass as bass
import concourse.mybir as mybir
from concourse.bass_utils import run_bass_kernel_spmd

F32 = mybir.dt.float32
BF16 = mybir.dt.bfloat16
FP8 = mybir.dt.float8e4
ALU = mybir.AluOpType
AF = mybir.ActivationFunctionType
AX = mybir.AxisListType

D = 2048
S = 2048
NH = 16
DK = 128
II = 8192
SEG = 1024
NSEG = II // SEG
TOK = 1024
NTB = TOK // 128
NKC = D // 128
NXT = S // 128
NSP = 16
EPS = 1e-5
RSD = float(1.0 / np.sqrt(np.float64(DK)))
INV448H = 0.5 / 448.0


class Sem:
    __slots__ = ("h", "n")

    def __init__(self, nc, es, name):
        self.h = es.enter_context(nc.semaphore(name))
        self.n = 0

    def inc(self, ins, k=1):
        ins.then_inc(self.h, k)
        self.n += k
        return ins

    def wait(self, eng, v):
        eng.wait_ge(self.h, v)


class Fence:
    __slots__ = ("h", "n")

    def __init__(self, nc, es, name):
        self.h = es.enter_context(nc.semaphore(name))
        self.n = 0

    def cut(self, e, ins):
        ins.then_inc(self.h, 1)
        self.n += 1
        e.wait_ge(self.h, self.n)
        return ins


class Stream:

    def __init__(self, nc, es, name, k=3):
        self.sems = [es.enter_context(nc.semaphore(f"{name}{j}"))
                     for j in range(k)]
        self.k = k
        self.i = 0

    @property
    def n(self):
        return 16 * self.i

    def dma(self, e, out, in_):
        i = self.i
        j = i % self.k
        if i >= self.k:
            e.wait_ge(self.sems[j], 16 * (i // self.k))
        ins = e.dma_start(out=out, in_=in_)
        ins.then_inc(self.sems[j], 16)
        self.i += 1
        return ins

    def wait(self, e, v):
        assert v % 16 == 0
        n = v // 16
        for idx in range(max(0, n - self.k), n):
            e.wait_ge(self.sems[idx % self.k], 16 * (idx // self.k + 1))


class B_:
    pass


def build_nc():
    STOP = int(os.environ.get("K_STOP", "99"))
    SUB = os.environ.get("K_SUB", "full")
    z = B_()
    nc = bass.Bass("TRN2", target_bir_lowering=False, debug=False)
    z.nc = nc
    inp = {}
    for nm, shp in [("xf", [S, D]), ("wqT", [D, D]), ("wkT", [D, D]),
                    ("wvT", [D, D]), ("woT", [D, D]), ("gwT", [D, II]),
                    ("uwT", [D, II]), ("dwT", [II, D]),
                    ("sint", [128, S]), ("cost", [128, S])]:
        inp[nm] = nc.dram_tensor(nm, shp, F32, kind="ExternalInput").ap()
    inp["trimask"] = nc.dram_tensor("trimask", [128, 128], BF16, kind="ExternalInput").ap()
    inp["parf"] = nc.dram_tensor("parf", [128, 1], F32, kind="ExternalInput").ap()
    inp["ident"] = nc.dram_tensor("ident", [128, 128], BF16, kind="ExternalInput").ap()
    out_own = nc.dram_tensor("out_own", [TOK, D], F32, kind="ExternalOutput").ap()
    d_k = nc.dram_tensor("d_k", [NH, 128, S], BF16, kind="Internal").ap()
    d_q = nc.dram_tensor("d_q", [NH, 128, TOK], BF16, kind="Internal").ap()
    d_v = nc.dram_tensor("d_v", [S, D], BF16, kind="Internal").ap()
    d_xn = nc.dram_tensor("d_xn", [NTB, 128, D], F32, kind="Internal").ap()
    d_xq = nc.dram_tensor("d_xq", [NTB, 128, NKC * 128], BF16, kind="Internal").ap()
    d_acc = nc.dram_tensor("d_acc", [NTB, 128, D], F32, kind="Internal").ap()

    es = contextlib.ExitStack()
    with es:
        A = es.enter_context(nc.sbuf_tensor("Ar", [128, 32768], BF16))
        Bf = es.enter_context(nc.sbuf_tensor("Br", [128, 24576], BF16))
        C = es.enter_context(nc.sbuf_tensor("Cr", [128, 4096], F32))
        W = es.enter_context(nc.sbuf_tensor("Wr", [128, 4096], F32))
        SM = es.enter_context(nc.sbuf_tensor("SMr", [128, 2048], F32))
        SCL = es.enter_context(nc.sbuf_tensor("SCLr", [128, 8192], F32))
        GS = es.enter_context(nc.sbuf_tensor("GSr", [128, 512], F32))
        Q8 = es.enter_context(nc.sbuf_tensor("Q8r", [128, 2048], FP8))
        TRI = es.enter_context(nc.sbuf_tensor("TRIr", [128, 128], BF16))
        PARF = es.enter_context(nc.sbuf_tensor("PARFr", [128, 1], F32))
        ONE1 = es.enter_context(nc.sbuf_tensor("ONE1r", [128, 1], BF16))
        EPSC = es.enter_context(nc.sbuf_tensor("EPSCr", [128, 1], F32))
        ONER = es.enter_context(nc.sbuf_tensor("ONERr", [1, 128], BF16))
        IDN = es.enter_context(nc.sbuf_tensor("IDNr", [128, 128], BF16))
        P32 = es.enter_context(nc.psum_tensor("P32r", [128, 4096], F32))
        P16 = P32[:, 3584:4096].bitcast(BF16)

        s_x = Stream(nc, es, "sx")
        s_p = Stream(nc, es, "sp")
        c_s = Sem(nc, es, "c_s")
        c_t = Sem(nc, es, "c_t")
        v_a = Sem(nc, es, "v_a")
        v_b = Sem(nc, es, "v_b")
        v_c = Sem(nc, es, "v_c")
        f_v = Fence(nc, es, "f_v")

        blk = es.enter_context(nc.Block())

        def _stop_tail():
            dump = os.environ.get("K_DUMP", "")

            def t_sync(e):
                for sm in (c_s, c_t, v_a, v_b, v_c):
                    e.wait_ge(sm.h, sm.n)
                s_p.wait(e, s_p.n)
                s_x.wait(e, s_x.n)
                if dump == "xn":
                    for j in range(8):
                        s_x.dma(e, out=out_own[j * 128:(j + 1) * 128, :],
                                in_=d_xn[j])
                elif dump == "acc":
                    for j in range(8):
                        s_x.dma(e, out=out_own[j * 128:(j + 1) * 128, :],
                                in_=d_acc[j])
                elif dump == "q":
                    for j in range(8):
                        s_x.dma(e, out=out_own[j * 128:(j + 1) * 128, 0:512]
                                .bitcast(BF16), in_=d_q[2 * j])
                elif dump == "k":
                    for j in range(8):
                        s_x.dma(e, out=out_own[j * 128:(j + 1) * 128, 0:1024]
                                .bitcast(BF16), in_=d_k[2 * j])
                elif dump == "v":
                    for j in range(8):
                        s_x.dma(e, out=out_own[j * 128:(j + 1) * 128, 0:1024]
                                .bitcast(BF16), in_=d_v[j * 128:(j + 1) * 128, :])
                elif dump == "xq":
                    for j in range(8):
                        s_x.dma(e, out=out_own[j * 128:(j + 1) * 128, 0:1024]
                                .bitcast(BF16), in_=d_xq[j])
                else:
                    s_x.dma(e, out=out_own[0:128, :], in_=SM[:, 0:2048])
                s_x.wait(e, s_x.n)
            blk.sync(t_sync)

        def p0_sync(e):
            s_x.dma(e, out=C[:, 0:2048], in_=inp["sint"])
            s_x.dma(e, out=C[:, 2048:4096], in_=inp["cost"])
            s_x.dma(e, out=TRI[:], in_=inp["trimask"])
            s_x.dma(e, out=PARF[:], in_=inp["parf"])
            s_x.dma(e, out=IDN[:], in_=inp["ident"])
        blk.sync(p0_sync)

        def p0_vec(e):
            e.memset(ONE1[:], 1.0)
            e.memset(EPSC[:], EPS)
            v_a.inc(e.memset(ONER[:], 1.0))
        blk.vector(p0_vec)

        if STOP == 0:
            _stop_tail()
            return nc

        bx, bva, bcs, bvb, bct, bvc, bsp = (s_x.n, v_a.n, c_s.n, v_b.n,
                                            c_t.n, v_c.n, s_p.n)

        def p1_sync(e):
            for i in range(NXT):
                if i >= 2:
                    v_b.wait(e, bvb + (i - 1))
                    if SUB not in ("ln", "tp") and i - 2 < NTB:
                        s_p.wait(e, bsp + 16 * (i - 1))
                s_x.dma(e, out=W[:, (i % 2) * 2048:(i % 2) * 2048 + 2048],
                            in_=inp["xf"][i * 128:(i + 1) * 128, :])
        blk.sync(p1_sync)

        def p1_evac(e, G):
            i, g = G // 2, G % 2
            c_t.wait(e, bct + G + 1)
            ins = None
            for t in range(8):
                kc = g * 8 + t
                ins = e.tensor_copy(
                    out=A[:, kc * 2048 + i * 128:kc * 2048 + (i + 1) * 128],
                    in_=P16[:, t * 128:(t + 1) * 128])
            v_c.inc(ins)

        def p1_vec(e):
            for i in range(NXT + 1):
                if i < NXT:
                    xt = W[:, (i % 2) * 2048:(i % 2) * 2048 + 2048]
                    st = (i % 2) * 8
                    s_x.wait(e, bx + 16 * (i + 1))
                    if SUB != "ln" and i >= 2:
                        c_t.wait(e, bct + 2 * (i - 2) + 2)
                    f_v.cut(e, e.tensor_reduce(out=SCL[:, st:st + 1], in_=xt,
                                               axis=AX.X, op=ALU.add))
                    f_v.cut(e, e.tensor_scalar(out=SCL[:, st + 1:st + 2],
                                               in0=SCL[:, st:st + 1],
                                               scalar1=1.0 / 2048.0,
                                               scalar2=None, op0=ALU.mult))
                    e.tensor_scalar(out=xt, in0=xt, scalar1=SCL[:, st + 1:st + 2],
                                    scalar2=None, op0=ALU.subtract)
                    e.tensor_tensor(out=SM[:, 0:2048], in0=xt, in1=xt, op=ALU.mult)
                    v_a.inc(e.tensor_reduce(out=SCL[:, st + 2:st + 3],
                                            in_=SM[:, 0:2048], axis=AX.X, op=ALU.add))
                    c_s.wait(e, bcs + (i + 1))
                    e.tensor_scalar(out=xt, in0=xt, scalar1=SCL[:, st + 4:st + 5],
                                    scalar2=None, op0=ALU.mult)
                    bb = Bf[:, (i % 2) * 2048:(i % 2) * 2048 + 2048]
                    v_b.inc(e.tensor_copy(out=bb, in_=xt))
                if SUB != "ln" and i >= 1:
                    p1_evac(e, 2 * (i - 1))
                    p1_evac(e, 2 * (i - 1) + 1)
        blk.vector(p1_vec)

        if SUB == "ln" and STOP == 1:
            def p1_scal0(e):
                for i in range(NXT):
                    st = (i % 2) * 8
                    v_a.wait(e, bva + (i + 1))
                    e.activation(out=SCL[:, st + 3:st + 4], in_=SCL[:, st + 2:st + 3],
                                 func=AF.Ln, bias=EPSC[:, 0:1], scale=1.0 / 2048.0)
                    c_s.inc(e.activation(out=SCL[:, st + 4:st + 5],
                                         in_=SCL[:, st + 3:st + 4],
                                         func=AF.Exp, scale=-0.5))
            blk.scalar(p1_scal0)
            _stop_tail()
            return nc

        def p1_scal(e):
            for i in range(NXT):
                st = (i % 2) * 8
                v_a.wait(e, bva + (i + 1))
                e.activation(out=SCL[:, st + 3:st + 4], in_=SCL[:, st + 2:st + 3],
                             func=AF.Ln, bias=EPSC[:, 0:1], scale=1.0 / 2048.0)
                c_s.inc(e.activation(out=SCL[:, st + 4:st + 5], in_=SCL[:, st + 3:st + 4],
                                     func=AF.Exp, scale=-0.5))
        blk.scalar(p1_scal)

        def p1_tens(e):
            for i in range(NXT):
                bb = Bf[:, (i % 2) * 2048:(i % 2) * 2048 + 2048]
                v_b.wait(e, bvb + (i + 1))
                for g in range(2):
                    if 2 * i + g >= 1:
                        v_c.wait(e, bvc + 2 * i + g)
                    ins = None
                    for t in range(8):
                        kc = g * 8 + t
                        ins = e.transpose(out=P16[:, t * 128:(t + 1) * 128],
                                          in_=bb[:, kc * 128:(kc + 1) * 128],
                                          identity=IDN[:])
                    c_t.inc(ins)
        blk.tensor(p1_tens)

        def p1_gp(e):
            for i in range(NTB):
                v_b.wait(e, bvb + (i + 1))
                s_p.dma(e, out=d_xn[i],
                            in_=W[:, (i % 2) * 2048:(i % 2) * 2048 + 2048]
                            )
        blk.gpsimd(p1_gp)

        if SUB == "full":
            assert v_a.n == bva + NXT and c_s.n == bcs + NXT
            assert v_b.n == bvb + NXT and c_t.n == bct + 2 * NXT
            assert v_c.n == bvc + 2 * NXT and s_p.n == bsp + 16 * NTB
        if STOP == 1:
            _stop_tail()
            return nc

        for wnm, kind in [("wkT", "K"), ("wvT", "V"), ("wqT", "Q")]:
            for half in range(2):
                bw, bcs, bct, bvb, bsp = s_x.n, c_s.n, c_t.n, v_b.n, s_p.n

                def p2_sync(e, wnm=wnm, half=half, bcs=bcs,
                            fvb=v_b.n, fsp=s_p.n):
                    e.wait_ge(v_b.h, fvb)
                    s_p.wait(e, fsp)
                    e.wait_ge(c_s.h, bcs)
                    for k in range(NKC):
                        if k >= 2:
                            c_s.wait(e, bcs + (k - 1))
                        s_x.dma(e, 
                            out=W[:, (k % 2) * 1024:(k % 2) * 1024 + 1024],
                            in_=inp[wnm][k * 128:(k + 1) * 128,
                                         half * 1024:half * 1024 + 1024]
                            )
                blk.sync(p2_sync)

                def p2_scal(e, bw=bw, fct=c_t.n):
                    e.wait_ge(c_t.h, fct)
                    for k in range(NKC):
                        s_x.wait(e, bw + 16 * (k + 1))
                        c_s.inc(e.activation(
                            out=Bf[:, k * 1024:(k + 1) * 1024],
                            in_=W[:, (k % 2) * 1024:(k % 2) * 1024 + 1024],
                            func=AF.Copy))
                blk.scalar(p2_scal)

                if kind in ("K", "Q"):
                    ntc = (S if kind == "K" else TOK) // 512

                    def p2_tens(e, ntc=ntc, bcs=bcs, bvb=bvb):
                        e.wait_ge(v_b.h, bvb)
                        for hh in range(8):
                            if hh >= 1:
                                v_b.wait(e, bvb + hh)
                            ins = None
                            for k in range(NKC):
                                c_s.wait(e, bcs + (k + 1))
                                lhs = Bf[:, k * 1024 + hh * 128:k * 1024 + (hh + 1) * 128]
                                for t in range(ntc):
                                    rhs = A[:, k * 2048 + t * 512:k * 2048 + (t + 1) * 512]
                                    ins = e.matmul(out=P32[:, t * 512:(t + 1) * 512],
                                                   lhsT=lhs, rhs=rhs,
                                                   start=(k == 0), stop=(k == NKC - 1))
                            c_t.inc(ins)
                    blk.tensor(p2_tens)

                    def p2_vec(e, ntc=ntc, bct=bct, bsp=bsp):
                        sin = C[:, 0:2048]
                        cos = C[:, 2048:4096]
                        for hh in range(8):
                            c_t.wait(e, bct + hh + 1)
                            if hh >= 2:
                                s_p.wait(e, bsp + 16 * (hh - 1))
                            stg = Bf[:, 16384 + (hh % 2) * 2048:
                                     16384 + (hh % 2) * 2048 + ntc * 512]
                            ins = None
                            for t in range(ntc):
                                ps = P32[:, t * 512:(t + 1) * 512]
                                tok0 = t * 512
                                s1 = SM[:, 0:512]
                                t2 = SM[:, 512:1024]
                                e.tensor_tensor(out=s1[0:64, :], in0=ps[64:128, :],
                                                in1=sin[0:64, tok0:tok0 + 512], op=ALU.mult)
                                e.tensor_tensor(out=s1[64:128, :], in0=ps[0:64, :],
                                                in1=sin[64:128, tok0:tok0 + 512], op=ALU.mult)
                                e.tensor_tensor(out=t2, in0=ps,
                                                in1=cos[:, tok0:tok0 + 512], op=ALU.mult)
                                ins = e.tensor_tensor(out=stg[:, t * 512:(t + 1) * 512],
                                                      in0=t2, in1=s1, op=ALU.add)
                            v_b.inc(ins)
                    blk.vector(p2_vec)

                    def p2_gp(e, kind=kind, half=half, ntc=ntc, bvb=bvb):
                        dst = d_k if kind == "K" else d_q
                        for hh in range(8):
                            v_b.wait(e, bvb + hh + 1)
                            stg = Bf[:, 16384 + (hh % 2) * 2048:
                                     16384 + (hh % 2) * 2048 + ntc * 512]
                            s_p.dma(e, out=dst[half * 8 + hh], in_=stg
                                        )
                    blk.gpsimd(p2_gp)
                    nu = 8
                else:
                    def p2_tens(e, bcs=bcs, bvb=bvb):
                        e.wait_ge(v_b.h, bvb)
                        for tb in range(NXT):
                            if tb >= 2:
                                v_b.wait(e, bvb + tb - 1)
                            pc = (tb % 2) * 1024
                            ins = None
                            for k in range(NKC):
                                c_s.wait(e, bcs + (k + 1))
                                lhs = A[:, k * 2048 + tb * 128:k * 2048 + (tb + 1) * 128]
                                for t in range(2):
                                    rhs = Bf[:, k * 1024 + t * 512:k * 1024 + (t + 1) * 512]
                                    ins = e.matmul(
                                        out=P32[:, pc + t * 512:pc + (t + 1) * 512],
                                        lhsT=lhs, rhs=rhs,
                                        start=(k == 0), stop=(k == NKC - 1))
                            c_t.inc(ins)
                    blk.tensor(p2_tens)

                    def p2_scal2(e, bct=bct, bsp=bsp):
                        for tb in range(NXT):
                            c_t.wait(e, bct + tb + 1)
                            if tb >= 2:
                                s_p.wait(e, bsp + 16 * (tb - 1))
                            pc = (tb % 2) * 1024
                            stg = Bf[:, 20480 + (tb % 2) * 1024:
                                     20480 + (tb % 2) * 1024 + 1024]
                            v_b.inc(e.activation(out=stg, in_=P32[:, pc:pc + 1024],
                                                 func=AF.Copy))
                    blk.scalar(p2_scal2)

                    def p2_gp(e, half=half, bvb=bvb):
                        for tb in range(NXT):
                            v_b.wait(e, bvb + tb + 1)
                            stg = Bf[:, 20480 + (tb % 2) * 1024:
                                     20480 + (tb % 2) * 1024 + 1024]
                            s_p.dma(e, out=d_v[tb * 128:(tb + 1) * 128,
                                                half * 1024:half * 1024 + 1024],
                                        in_=stg)
                    blk.gpsimd(p2_gp)
                    nu = NXT

                assert c_t.n == bct + nu and v_b.n == bvb + nu
                assert s_p.n == bsp + 16 * nu

        if STOP == 2:
            _stop_tail()
            return nc

        bsr, bct, bcs, bva, bvb, bvc = s_x.n, c_t.n, c_s.n, v_a.n, v_b.n, v_c.n
        sp_qb0 = [sp % 8 for sp in range(NSP)]
        sp_nq = [8 - q for q in sp_qb0]

        ct = {}
        n = bct
        for h in range(NH):
            n += 1; ct[("rows", h)] = n
            n += 1; ct[("bbc", h)] = n
            for sp in range(NSP):
                n += 1; ct[("sc", h, sp)] = n
                n += 1; ct[("av", h, sp)] = n
            n += 1; ct[("rbc", h)] = n
        ct_end = n
        cs = {}
        n = bcs
        for h in range(NH):
            n += 1; cs[("sq", h)] = n
            n += 1; cs[("bnd", h)] = n
            for sp in range(NSP):
                n += 1; cs[("exp", h, sp)] = n
        cs_end = n
        va = {}
        n = bva
        for h in range(NH):
            n += 1; va[("mkk", h)] = n
            n += 1; va[("bnd", h)] = n
            for sp in range(NSP):
                n += 1; va[("t", h, sp)] = n
                n += 1; va[("avadd", h, sp)] = n
        va_end = n
        vb = {}
        n = bvb
        for h in range(NH):
            n += 1; vb[("brow", h)] = n
            for sp in range(NSP):
                n += 1; vb[("P", h, sp)] = n
        vb_end = n
        vc = {}
        n = bvc
        for h in range(NH):
            n += 1; vc[("rec", h)] = n
            n += 1; vc[("ao", h)] = n
        vc_end = n

        def kcol(h):
            return (h % 2) * 2048

        def vvcol(h):
            return 4096 + (h % 2) * 2048

        def qcol(h):
            return 8192 + (h % 2) * 1024

        AACC = SCL[:, 4096:5120]
        DACC = SCL[0:1, 5120:6144]

        def p3_sync(e, fsp=s_p.n, fct=c_t.n, fcs=c_s.n):
            s_p.wait(e, fsp)
            e.wait_ge(c_t.h, fct)
            e.wait_ge(c_s.h, fcs)
            for h in range(NH):
                if h >= 2:
                    c_t.wait(e, ct[("av", h - 2, NSP - 1)])
                    c_s.wait(e, cs[("sq", h - 1)])
                s_x.dma(e, out=Bf[:, kcol(h):kcol(h) + 2048], in_=d_k[h])
                s_x.dma(e, out=Bf[:, vvcol(h):vvcol(h) + 2048].rearrange(
                    "p (sb j) -> p sb j", j=128),
                    in_=d_v[:, h * 128:(h + 1) * 128].rearrange(
                        "(sb p) j -> p sb j", p=128))
                s_x.dma(e, out=Bf[:, qcol(h):qcol(h) + 1024], in_=d_q[h])
        blk.sync(p3_sync)

        def p3_scal(e):
            for h in range(NH):
                s_x.wait(e, bsr + 48 * (h + 1))
                e.activation(out=Bf[:, 12288:14336],
                             in_=Bf[:, kcol(h):kcol(h) + 2048], func=AF.Square)
                c_s.inc(e.activation(out=Bf[:, 14336:15360],
                                     in_=Bf[:, qcol(h):qcol(h) + 1024],
                                     func=AF.Square))
                v_a.wait(e, va[("mkk", h)])
                c_s.inc(e.activation(out=SCL[0:1, 3072:4096],
                                     in_=SCL[0:1, 2048:3072], func=AF.Sqrt))
                for sp in range(NSP):
                    nq = sp_nq[sp]
                    v_a.wait(e, va[("t", h, sp)])
                    if sp >= 2:
                        c_t.wait(e, ct[("av", h, sp - 2)])
                    elif h >= 1:
                        c_t.wait(e, ct[("av", h - 1, NSP - 2 + sp)])
                    tst = SM[:, (sp % 2) * 1024:(sp % 2) * 1024 + nq * 128]
                    pst = Bf[:, 10240 + (sp % 2) * 1024:
                             10240 + (sp % 2) * 1024 + nq * 128]
                    c_s.inc(e.activation(out=pst, in_=tst, func=AF.Exp))
        blk.scalar(p3_scal)

        def p3_tens(e, fvb=v_b.n, fvc=v_c.n):
            e.wait_ge(v_b.h, fvb)
            e.wait_ge(v_c.h, fvc)
            for h in range(NH):
                c_s.wait(e, cs[("sq", h)])
                if h >= 1:
                    v_c.wait(e, vc[("ao", h - 1)])
                ins = None
                for t in range(4):
                    ins = e.matmul(out=P32[0:1, t * 512:(t + 1) * 512], lhsT=ONE1[:],
                                   rhs=Bf[:, 12288 + t * 512:12288 + (t + 1) * 512],
                                   start=True, stop=True)
                for t in range(2):
                    ins = e.matmul(out=P32[0:1, 2048 + t * 512:2048 + (t + 1) * 512],
                                   lhsT=ONE1[:],
                                   rhs=Bf[:, 14336 + t * 512:14336 + (t + 1) * 512],
                                   start=True, stop=True)
                c_t.inc(ins)
                v_b.wait(e, vb[("brow", h)])
                ins = None
                for t in range(2):
                    ins = e.matmul(out=P32[:, t * 512:(t + 1) * 512], lhsT=ONER[:],
                                   rhs=Bf[0:1, 15360 + t * 512:15360 + (t + 1) * 512],
                                   start=True, stop=True)
                c_t.inc(ins)
                for sp in range(NSP):
                    q0, nq = sp_qb0[sp], sp_nq[sp]
                    if sp < 2:
                        v_a.wait(e, va[("bnd", h)])
                    else:
                        v_a.wait(e, va[("avadd", h, sp - 2)])
                    ksl = Bf[:, kcol(h) + sp * 128:kcol(h) + (sp + 1) * 128]
                    pcol = (sp % 2) * 1024
                    ins = None
                    nmm = (nq * 128 + 511) // 512
                    for t in range(nmm):
                        w0 = t * 512
                        w1 = min(nq * 128, w0 + 512)
                        rhs = Bf[:, qcol(h) + q0 * 128 + w0:qcol(h) + q0 * 128 + w1]
                        ins = e.matmul(out=P32[:, pcol + w0:pcol + w1],
                                       lhsT=ksl, rhs=rhs, start=True, stop=True)
                    c_t.inc(ins)
                    v_b.wait(e, vb[("P", h, sp)])
                    vsl = Bf[:, vvcol(h) + sp * 128:vvcol(h) + (sp + 1) * 128]
                    pb = Bf[:, 10240 + (sp % 2) * 1024:
                            10240 + (sp % 2) * 1024 + nq * 128]
                    vcol0 = 2048 + (sp % 2) * 1024
                    ins = None
                    for t in range(nmm):
                        w0 = t * 512
                        w1 = min(nq * 128, w0 + 512)
                        ins = e.matmul(out=P32[:, vcol0 + w0:vcol0 + w1],
                                       lhsT=vsl, rhs=pb[:, w0:w1],
                                       start=True, stop=True)
                    for t in range(nmm):
                        w0 = t * 512
                        w1 = min(nq * 128, w0 + 512)
                        ins = e.matmul(out=P32[0:1, pcol + w0:pcol + w1],
                                       lhsT=ONE1[:], rhs=pb[:, w0:w1],
                                       start=True, stop=True)
                    c_t.inc(ins)
                v_c.wait(e, vc[("rec", h)])
                ins = None
                for t in range(2):
                    ins = e.matmul(out=P32[:, t * 512:(t + 1) * 512], lhsT=ONER[:],
                                   rhs=Bf[0:1, 16384 + t * 512:16384 + (t + 1) * 512],
                                   start=True, stop=True)
                c_t.inc(ins)
        blk.tensor(p3_tens)

        def p3_vec(e):
            for h in range(NH):
                e.memset(AACC[:, :], 0.0)
                e.memset(DACC[:, :], 0.0)
                c_t.wait(e, ct[("rows", h)])
                f_v.cut(e, e.tensor_reduce(out=SCL[0:1, 1024:1025],
                                           in_=P32[0:1, 0:2048],
                                           axis=AX.X, op=ALU.max))
                v_a.inc(e.tensor_scalar(out=SCL[0:1, 2048:3072],
                                        in0=P32[0:1, 2048:3072],
                                        scalar1=SCL[0:1, 1024:1025],
                                        scalar2=None, op0=ALU.mult))
                c_s.wait(e, cs[("bnd", h)])
                v_b.inc(e.tensor_scalar(out=Bf[0:1, 15360:16384],
                                        in0=SCL[0:1, 3072:4096], scalar1=RSD,
                                        scalar2=-40.0, op0=ALU.mult,
                                        op1=ALU.add))
                c_t.wait(e, ct[("bbc", h)])
                v_a.inc(e.tensor_scalar(out=SCL[:, 0:1024], in0=P32[:, 0:1024],
                                        scalar1=-1.0, scalar2=None, op0=ALU.mult))

                def avadd(sp):
                    q0n, nqn = sp_qb0[sp], sp_nq[sp]
                    c_t.wait(e, ct[("av", h, sp)])
                    vcol0 = 2048 + (sp % 2) * 1024
                    pcol = (sp % 2) * 1024
                    e.scalar_tensor_tensor(
                        out=AACC[:, q0n * 128:1024],
                        in0=P32[:, vcol0:vcol0 + nqn * 128], scalar=1.0,
                        in1=AACC[:, q0n * 128:1024],
                        op0=ALU.mult, op1=ALU.add)
                    v_a.inc(e.scalar_tensor_tensor(
                        out=DACC[0:1, q0n * 128:1024],
                        in0=P32[0:1, pcol:pcol + nqn * 128], scalar=1.0,
                        in1=DACC[0:1, q0n * 128:1024],
                        op0=ALU.mult, op1=ALU.add))

                for sp in range(NSP):
                    q0, nq = sp_qb0[sp], sp_nq[sp]
                    c_t.wait(e, ct[("sc", h, sp)])
                    pcol = (sp % 2) * 1024
                    tst = SM[:, (sp % 2) * 1024:(sp % 2) * 1024 + nq * 128]
                    v_a.inc(e.scalar_tensor_tensor(
                        out=tst, in0=P32[:, pcol:pcol + nq * 128], scalar=RSD,
                        in1=SCL[:, q0 * 128:q0 * 128 + nq * 128],
                        op0=ALU.mult, op1=ALU.add))
                    c_s.wait(e, cs[("exp", h, sp)])
                    pb = Bf[:, 10240 + (sp % 2) * 1024:
                            10240 + (sp % 2) * 1024 + nq * 128]
                    if sp < 8:
                        ins = e.tensor_tensor(out=pb[:, 0:128], in0=pb[:, 0:128],
                                              in1=TRI[:], op=ALU.mult)
                    else:
                        ins = e.tensor_scalar(out=pb[:, 0:128], in0=pb[:, 0:128],
                                              scalar1=PARF[:, 0:1], scalar2=None,
                                              op0=ALU.mult)
                    v_b.inc(ins)
                    avadd(sp)
                f_v.cut(e, e.engine_nop())
                e.reciprocal(out=DACC[:, :], in_=DACC[:, :])
                v_c.inc(e.tensor_copy(out=Bf[0:1, 16384:17408], in_=DACC[:, :]))
                c_t.wait(e, ct[("rbc", h)])
                e.tensor_copy(out=SM[:, 0:1024], in_=P32[:, 0:1024])
                v_c.inc(e.tensor_tensor(out=A[:, h * 1024:(h + 1) * 1024],
                                        in0=AACC[:, :], in1=SM[:, 0:1024],
                                        op=ALU.mult))
        blk.vector(p3_vec)

        assert c_t.n == ct_end and c_s.n == cs_end
        assert v_a.n == va_end and v_b.n == vb_end and v_c.n == vc_end

        if STOP == 3:
            _stop_tail()
            return nc

        for half in range(2):
            bw, bcs, bct, bvb, bsp = s_x.n, c_s.n, c_t.n, v_b.n, s_p.n

            def p4_sync(e, half=half, bcs=bcs, fcs=c_s.n, fvc=v_c.n):
                e.wait_ge(c_s.h, fcs)
                e.wait_ge(v_c.h, fvc)
                for k in range(8):
                    kk = half * 8 + k
                    if k >= 2:
                        c_s.wait(e, bcs + (k - 1))
                    s_x.dma(e, out=W[:, (k % 2) * 2048:(k % 2) * 2048 + 2048],
                                in_=inp["woT"][kk * 128:(kk + 1) * 128, :]
                                )
            blk.sync(p4_sync)

            def p4_scal(e, bw=bw, fct=c_t.n):
                e.wait_ge(c_t.h, fct)
                for k in range(8):
                    s_x.wait(e, bw + 16 * (k + 1))
                    c_s.inc(e.activation(out=Bf[:, k * 2048:(k + 1) * 2048],
                                         in_=W[:, (k % 2) * 2048:(k % 2) * 2048 + 2048],
                                         func=AF.Copy))
            blk.scalar(p4_scal)

            def p4_tens(e, half=half, bcs=bcs, bvb=bvb, fvc=v_c.n,
                        fva=v_a.n):
                e.wait_ge(v_c.h, fvc)
                e.wait_ge(v_a.h, fva)
                e.wait_ge(v_b.h, bvb)
                for tb in range(NTB):
                    if tb >= 1:
                        v_b.wait(e, bvb + tb)
                    ins = None
                    for k in range(8):
                        c_s.wait(e, bcs + (k + 1))
                        hh = half * 8 + k
                        lhs = A[:, hh * 1024 + tb * 128:hh * 1024 + (tb + 1) * 128]
                        for t in range(4):
                            rhs = Bf[:, k * 2048 + t * 512:k * 2048 + (t + 1) * 512]
                            ins = e.matmul(out=P32[:, t * 512:(t + 1) * 512],
                                           lhsT=lhs, rhs=rhs,
                                           start=(k == 0), stop=(k == 7))
                    c_t.inc(ins)
            blk.tensor(p4_tens)

            def p4_gp(e, half=half, bvb=bvb, bsp=bsp):
                for tb in range(NTB):
                    if half == 1:
                        s_p.dma(e, out=C[:, (tb % 2) * 2048:(tb % 2) * 2048 + 2048],
                                    in_=d_acc[tb])
                    v_b.wait(e, bvb + tb + 1)
                    s_p.dma(e, out=d_acc[tb],
                                in_=C[:, (tb % 2) * 2048:(tb % 2) * 2048 + 2048]
                                )
            blk.gpsimd(p4_gp)

            def p4_vec(e, half=half, bct=bct, bsp=bsp):
                for tb in range(NTB):
                    c_t.wait(e, bct + tb + 1)
                    cst = C[:, (tb % 2) * 2048:(tb % 2) * 2048 + 2048]
                    if half == 0:
                        if tb >= 2:
                            s_p.wait(e, bsp + 16 * (tb - 1))
                        v_b.inc(e.tensor_copy(out=cst, in_=P32[:, 0:2048]))
                    else:
                        s_p.wait(e, bsp + 32 * tb + 16)
                        if tb >= 2:
                            s_p.wait(e, bsp + 32 * (tb - 1))
                        v_b.inc(e.scalar_tensor_tensor(
                            out=cst, in0=P32[:, 0:2048], scalar=1.0, in1=cst,
                            op0=ALU.mult, op1=ALU.add))
            blk.vector(p4_vec)

            assert c_t.n == bct + NTB and v_b.n == bvb + NTB

        if STOP == 4:
            _stop_tail()
            return nc

        bxn, bva, bct, bvc, bsp = s_x.n, v_a.n, c_t.n, v_c.n, s_p.n

        def p5p_sync(e, fcs=c_s.n):
            e.wait_ge(c_s.h, fcs)
            for tb in range(NTB):
                if tb >= 2:
                    v_a.wait(e, bva + (tb - 1))
                s_x.dma(e, out=W[:, (tb % 2) * 2048:(tb % 2) * 2048 + 2048],
                            in_=d_xn[tb])
        blk.sync(p5p_sync)

        def p5p_evac(e, G):
            tb, g = G // 2, G % 2
            c_t.wait(e, bct + G + 1)
            if tb >= 2 and g == 0:
                s_p.wait(e, bsp + 16 * (tb - 1))
            ins = None
            for t in range(8):
                kc = g * 8 + t
                ins = e.tensor_copy(
                    out=Bf[:, 16384 + (tb % 2) * 2048 + kc * 128:
                           16384 + (tb % 2) * 2048 + (kc + 1) * 128],
                    in_=P16[:, t * 128:(t + 1) * 128])
            v_c.inc(ins)

        def p5p_vec(e):
            for tb in range(NTB + 1):
              if tb < NTB:
                xt = W[:, (tb % 2) * 2048:(tb % 2) * 2048 + 2048]
                s_x.wait(e, bxn + 16 * (tb + 1))
                if tb >= 2:
                    c_t.wait(e, bct + 2 * (tb - 2) + 2)
                f_v.cut(e, e.tensor_reduce(
                    out=SCL[:, 4160:4224],
                    in_=xt.rearrange("p (j b) -> p b j", b=64),
                    axis=AX.X, op=ALU.max, apply_absolute_value=True))
                e.reciprocal(out=SCL[:, 4224:4288], in_=SCL[:, 4160:4224])
                f_v.cut(e, e.tensor_scalar(out=SCL[:, 4288:4352],
                                           in0=SCL[:, 4160:4224],
                                           scalar1=1.0 / 224.0, scalar2=None,
                                           op0=ALU.mult))
                quot = SM[:, 0:2048]
                e.scalar_tensor_tensor(
                    out=quot.rearrange("p (j b) -> p j b", b=64),
                    in0=xt.rearrange("p (j b) -> p j b", b=64),
                    scalar=224.0, op0=ALU.mult,
                    in1=SCL[:, 4224:4288].rearrange(
                        "p (o b) -> p o b", o=1).broadcast_to([128, 32, 64]),
                    op1=ALU.mult)
                e.tensor_copy(out=Q8[:, 0:2048], in_=quot)
                xqs = 20480 + (tb % 2) * 2048
                v_a.inc(e.tensor_tensor(
                    out=Bf[:, xqs:xqs + 2048].rearrange("p (j b) -> p j b", b=64),
                    in0=Q8[:, 0:2048].rearrange("p (j b) -> p j b", b=64),
                    in1=SCL[:, 4288:4352].rearrange(
                        "p (o b) -> p o b", o=1).broadcast_to([128, 32, 64]),
                    op=ALU.mult))
              if tb >= 1:
                p5p_evac(e, 2 * (tb - 1))
                p5p_evac(e, 2 * (tb - 1) + 1)
        blk.vector(p5p_vec)

        def p5p_tens(e, fvb=v_b.n):
            e.wait_ge(v_b.h, fvb)
            e.wait_ge(v_c.h, bvc)
            for tb in range(NTB):
                v_a.wait(e, bva + tb + 1)
                for g in range(2):
                    if 2 * tb + g >= 1:
                        v_c.wait(e, bvc + 2 * tb + g)
                    ins = None
                    xqs = 20480 + (tb % 2) * 2048
                    for t in range(8):
                        kc = g * 8 + t
                        ins = e.transpose(
                            out=P16[:, t * 128:(t + 1) * 128],
                            in_=Bf[:, xqs + kc * 128:xqs + (kc + 1) * 128],
                            identity=IDN[:])
                    c_t.inc(ins)
        blk.tensor(p5p_tens)

        def p5p_gp(e):
            for tb in range(NTB):
                v_c.wait(e, bvc + 2 * tb + 2)
                s_p.dma(e, out=d_xq[tb],
                            in_=Bf[:, 16384 + (tb % 2) * 2048:
                                   16384 + (tb % 2) * 2048 + 2048]
                            )
        blk.gpsimd(p5p_gp)

        assert v_a.n == bva + NTB and c_t.n == bct + 2 * NTB
        assert v_c.n == bvc + 2 * NTB and s_p.n == bsp + 16 * NTB

        if STOP == 5:
            _stop_tail()
            return nc

        def pm_wslot(u):
            if u < 32:
                return W[:, (u % 4) * 1024:(u % 4) * 1024 + 1024]
            return W[:, (u % 2) * 2048:(u % 2) * 2048 + 2048]

        def pm_macc(u):
            if u < 16:
                return SCL[:, 0:1024]
            if u < 32:
                return SCL[:, 1024:2048]
            return SCL[:, 2048:4096]

        def pm_tile_scal(e, u, bw):
            s_x.wait(e, bw + 16 * (u + 1))
            c_s.inc(e.activation(out=pm_wslot(u), in_=pm_wslot(u),
                                 func=AF.Abs))

        def pm_tile_vec(e, u, sqcnt):
            c_s.wait(e, sqcnt)
            m = pm_macc(u)
            v_a.inc(e.tensor_tensor(out=m, in0=pm_wslot(u), in1=m,
                                    op=ALU.max))

        def pm_memsets(e):
            e.memset(SCL[:, 0:1024], 0.0)
            e.memset(SCL[:, 1024:2048], 0.0)
            e.memset(SCL[:, 2048:4096], 0.0)

        def pm_folds_max(e):
            for wi in range(2):
                mm = SCL[:, wi * 1024:wi * 1024 + 1024]
                e.tensor_copy(out=SM[0:64, 0:1024], in_=mm[64:128, :])
                e.tensor_tensor(out=mm[0:64, :], in0=SM[0:64, 0:1024],
                                in1=mm[0:64, :], op=ALU.max)
            mm = SCL[:, 2048:4096]
            e.tensor_copy(out=SM[0:64, 0:2048], in_=mm[64:128, :])
            e.tensor_tensor(out=mm[0:64, :], in0=SM[0:64, 0:2048],
                            in1=mm[0:64, :], op=ALU.max)
            e.tensor_copy(out=SM[0:32, 0:2048], in_=mm[32:64, :])
            v_b.inc(e.tensor_tensor(out=mm[0:32, :], in0=SM[0:32, 0:2048],
                                    in1=mm[0:32, :], op=ALU.max))

        def pm_scales(e):
            for wi in range(2):
                mm = SCL[:, wi * 1024:wi * 1024 + 1024]
                rc = SCL[:, 4096 + wi * 1024:4096 + wi * 1024 + 1024]
                e.reciprocal(out=rc[0:64, :], in_=mm[0:64, :])
                e.tensor_scalar(out=rc[0:64, :], in0=rc[0:64, :],
                                scalar1=224.0, scalar2=None, op0=ALU.mult)
                e.tensor_scalar(out=mm[0:64, :], in0=mm[0:64, :],
                                scalar1=1.0 / 224.0, scalar2=None,
                                op0=ALU.mult)
                e.tensor_copy(out=mm[64:128, :], in_=mm[0:64, :])
                e.tensor_copy(out=rc[64:128, :], in_=rc[0:64, :])
            mm = SCL[:, 2048:4096]
            rc = SCL[:, 6144:8192]
            e.reciprocal(out=rc[0:32, :], in_=mm[0:32, :])
            e.tensor_scalar(out=rc[0:32, :], in0=rc[0:32, :],
                            scalar1=224.0, scalar2=None, op0=ALU.mult)
            e.tensor_scalar(out=mm[0:32, :], in0=mm[0:32, :],
                            scalar1=1.0 / 224.0, scalar2=None, op0=ALU.mult)
            e.tensor_copy(out=mm[32:64, :], in_=mm[0:32, :])
            e.tensor_copy(out=mm[64:128, :], in_=mm[0:64, :])
            e.tensor_copy(out=rc[32:64, :], in_=rc[0:32, :])
            e.tensor_copy(out=rc[64:128, :], in_=rc[0:64, :])

        def pm_dma(e, u, sg):
            if u < 32:
                wi, k = divmod(u, 16)
                wnm = "gwT" if wi == 0 else "uwT"
                s_x.dma(e, out=pm_wslot(u),
                        in_=inp[wnm][k * 128:(k + 1) * 128,
                                     sg * SEG:sg * SEG + SEG])
            else:
                k = u - 32
                s_x.dma(e, out=pm_wslot(u),
                        in_=inp["dwT"][(sg * 8 + k) * 128:
                                       (sg * 8 + k + 1) * 128, :])

        V0s = v_a.n
        assert V0s == bva + NTB
        mc_cur = lambda u, V0s=V0s: V0s + u + 1
        bw1_0 = s_x.n

        def p5s_sync0(e, V0s=V0s, mc=mc_cur):
            for u in range(40):
                if u < 2:
                    v_a.wait(e, V0s - 1)
                elif u < 4:
                    v_a.wait(e, V0s)
                elif u < 32:
                    v_a.wait(e, mc(u - 4))
                elif u == 32:
                    v_a.wait(e, mc(29))
                elif u == 33:
                    v_a.wait(e, mc(31))
                else:
                    v_a.wait(e, mc(u - 2))
                pm_dma(e, u, 0)
        blk.sync(p5s_sync0)

        bsq0, bvb0 = c_s.n, v_b.n

        def p5s_scal0(e, bw=bw1_0):
            for u in range(40):
                pm_tile_scal(e, u, bw)
        blk.scalar(p5s_scal0)

        def p5s_vec0(e, bsq0=bsq0):
            pm_memsets(e)
            for u in range(40):
                pm_tile_vec(e, u, bsq0 + u + 1)
            pm_folds_max(e)
            pm_scales(e)
        blk.vector(p5s_vec0)
        assert v_a.n == V0s + 40 and c_s.n == bsq0 + 40
        assert v_b.n == bvb0 + 1

        if STOP == 50:
            _stop_tail()
            return nc

        ct_guA = c_t.n
        ct_dnB = c_t.n
        for sg in range(NSEG):
            bva2, bcs2, bw2 = v_a.n, c_s.n, s_x.n
            mc_p = mc_cur

            def p5q_sync(e, sg=sg, bva2=bva2, mc_p=mc_p):
                for u in range(40):
                    if u < 2:
                        v_a.wait(e, mc_p(38))
                    elif u < 4:
                        v_a.wait(e, mc_p(39))
                    elif u < 32:
                        v_a.wait(e, bva2 + 2 * (u - 4) + 1)
                    elif u <= 33:
                        v_a.wait(e, bva2 + 63)
                    else:
                        v_a.wait(e, bva2 + 2 * (u - 2) + 1)
                    pm_dma(e, u, sg)
            blk.sync(p5q_sync)

            if sg < NSEG - 1:
                bw1n = s_x.n
                V3 = bva2 + 80
                mc_next = lambda u, V3=V3: V3 + 3 * (u // 5 + 1) + u + 1

                def p5s_syncN(e, sg=sg, bva2=bva2, mc=mc_next):
                    for u in range(40):
                        if u < 2:
                            v_a.wait(e, bva2 + 77)
                        elif u < 4:
                            v_a.wait(e, bva2 + 79)
                        elif u < 32:
                            v_a.wait(e, mc(u - 4))
                        elif u == 32:
                            v_a.wait(e, mc(29))
                        elif u == 33:
                            v_a.wait(e, mc(31))
                        else:
                            v_a.wait(e, mc(u - 2))
                        pm_dma(e, u, sg + 1)
                blk.sync(p5s_syncN)
            else:
                bw1n = None
                mc_next = None

            def p5q_vec(e, sg=sg, bw2=bw2, bcs2=bcs2, ct_guA=ct_guA,
                        ct_dnB=ct_dnB):
                for u in range(40):
                    s_x.wait(e, bw2 + 16 * (u + 1))
                    t_ = pm_wslot(u)
                    if u < 32:
                        wi, k = divmod(u, 16)
                        rec = SCL[:, 4096 + wi * 1024:4096 + wi * 1024 + 1024]
                        s2 = SCL[:, wi * 1024:wi * 1024 + 1024]
                        sm = SM[:, (u % 2) * 1024:(u % 2) * 1024 + 1024]
                        q8 = Q8[:, (u % 2) * 1024:(u % 2) * 1024 + 1024]
                        dst = A[:, wi * 16384 + k * 1024:
                                wi * 16384 + (k + 1) * 1024]
                    else:
                        k = u - 32
                        rec = SCL[:, 6144:8192]
                        s2 = SCL[:, 2048:4096]
                        sm = SM[:, 0:2048]
                        q8 = Q8[:, 0:2048]
                        dst = Bf[:, k * 2048:(k + 1) * 2048]
                    if u == 0:
                        c_t.wait(e, ct_guA)
                        c_s.wait(e, bcs2)
                    if u == 32:
                        c_t.wait(e, ct_dnB)
                    if u >= 2:
                        if u < 32:
                            c_s.wait(e, bcs2 + u - 1)
                        elif u == 32:
                            c_s.wait(e, bcs2 + 32)
                        else:
                            c_s.wait(e, bcs2 + u)
                    v_a.inc(e.tensor_tensor(out=sm, in0=t_, in1=rec,
                                            op=ALU.mult))
                    c_s.wait(e, bcs2 + u + 1)
                    v_a.inc(e.tensor_tensor(out=dst, in0=q8, in1=s2,
                                            op=ALU.mult))
                if sg < NSEG - 1:
                    pm_memsets(e)
            blk.vector(p5q_vec)

            def p5q_scal(e, bva2=bva2):
                for u in range(40):
                    v_a.wait(e, bva2 + 2 * u + 1)
                    if u < 32:
                        sm = SM[:, (u % 2) * 1024:(u % 2) * 1024 + 1024]
                        q8 = Q8[:, (u % 2) * 1024:(u % 2) * 1024 + 1024]
                    else:
                        sm = SM[:, 0:2048]
                        q8 = Q8[:, 0:2048]
                    c_s.inc(e.activation(out=q8, in_=sm, func=AF.Copy))
            blk.scalar(p5q_scal)
            assert v_a.n == bva2 + 80 and c_s.n == bcs2 + 40

            bct3, bcs3, bva3, bvb3, bvc3, bsp3 = (c_t.n, c_s.n, v_a.n,
                                                  v_b.n, v_c.n, s_p.n)
            vstep = 8 if sg < NSEG - 1 else 3
            CT = lambda tb, i, b=bct3: b + 4 * tb + i + 1
            CS = lambda tb, i, b=bcs3, vstep=vstep: b + vstep * tb + i + 1
            VA = lambda tb, i, b=bva3, vstep=vstep: b + vstep * tb + i + 1
            VB = lambda tb, b=bvb3: b + tb + 1
            VC = lambda tb, i, b=bvc3: b + 2 * tb + i + 1
            XQC = [2, 4, 6, 10, 14, 18, 22, 26]
            RLC = [7, 11, 15, 19, 23, 27, 29, 31]

            def p5m_gp(e, CT=CT, VC=VC):
                for tb in range(2):
                    s_p.dma(e, out=Bf[:, 18432 + (tb % 2) * 1024:
                                      18432 + (tb % 2) * 1024 + 1024],
                            in_=d_xq[tb][:, 0:1024])
                    s_p.dma(e, out=Bf[:, 20480 + (tb % 2) * 1024:
                                      20480 + (tb % 2) * 1024 + 1024],
                            in_=d_xq[tb][:, 1024:2048])
                for tb in range(NTB):
                    if tb + 2 < NTB:
                        t2 = tb + 2
                        c_t.wait(e, CT(tb, 1))
                        s_p.dma(e, out=Bf[:, 18432 + (t2 % 2) * 1024:
                                          18432 + (t2 % 2) * 1024 + 1024],
                                in_=d_xq[t2][:, 0:1024])
                        s_p.dma(e, out=Bf[:, 20480 + (t2 % 2) * 1024:
                                          20480 + (t2 % 2) * 1024 + 1024],
                                in_=d_xq[t2][:, 1024:2048])
                    if tb >= 2:
                        v_c.wait(e, VC(tb - 2, 1))
                    s_p.dma(e, out=C[:, (tb % 2) * 2048:(tb % 2) * 2048 + 2048],
                            in_=d_acc[tb])
                    v_c.wait(e, VC(tb, 1))
                    s_p.dma(e, out=d_acc[tb],
                            in_=C[:, (tb % 2) * 2048:(tb % 2) * 2048 + 2048])
            blk.gpsimd(p5m_gp)

            def xqsl(tb, k):
                if k < 8:
                    base = 18432 + (tb % 2) * 1024
                    return Bf[:, base + k * 128:base + (k + 1) * 128]
                base = 20480 + (tb % 2) * 1024
                return Bf[:, base + (k - 8) * 128:base + (k - 7) * 128]

            def p5m_tens(e, bva2=bva2, bsp3=bsp3, XQC=XQC, VA=VA, VB=VB,
                         VC=VC, fvb=v_b.n, fvc=v_c.n):
                e.wait_ge(v_b.h, fvb)
                e.wait_ge(v_c.h, fvc)
                for tb in range(NTB):
                    s_p.wait(e, bsp3 + 16 * XQC[tb])
                    for half in range(2):
                        g = 2 * tb + half
                        if g >= 1:
                            v_a.wait(e, VA((g - 1) // 2, (g - 1) % 2))
                        ins = None
                        for k in range(NKC):
                            v_a.wait(e, bva2 + 2 * (16 + k) + 2)
                            lhs = xqsl(tb, k)
                            gt = A[:, k * 1024 + half * 512:
                                   k * 1024 + half * 512 + 512]
                            ut = A[:, 16384 + k * 1024 + half * 512:
                                   16384 + k * 1024 + half * 512 + 512]
                            e.matmul(out=P32[:, 0:512], lhsT=lhs, rhs=gt,
                                     start=(k == 0), stop=(k == NKC - 1))
                            ins = e.matmul(out=P32[:, 512:1024], lhsT=lhs,
                                           rhs=ut, start=(k == 0),
                                           stop=(k == NKC - 1))
                        c_t.inc(ins)
                    v_b.wait(e, VB(tb))
                    if tb >= 1:
                        v_c.wait(e, VC(tb - 1, 0))
                    ins = None
                    for t in range(8):
                        ins = e.transpose(out=P16[:, t * 128:(t + 1) * 128],
                                          in_=Bf[:, 16384 + t * 128:
                                                 16384 + (t + 1) * 128],
                                          identity=IDN[:])
                    c_t.inc(ins)
                    v_c.wait(e, VC(tb, 0))
                    if tb >= 1:
                        v_c.wait(e, VC(tb - 1, 1))
                    ins = None
                    for kq in range(8):
                        v_a.wait(e, bva2 + 2 * (32 + kq) + 2)
                        lhs = Bf[:, 17408 + kq * 128:17408 + (kq + 1) * 128]
                        for t in range(4):
                            rhs = Bf[:, kq * 2048 + t * 512:
                                     kq * 2048 + (t + 1) * 512]
                            ins = e.matmul(out=P32[:, 1024 + t * 512:
                                                  1024 + (t + 1) * 512],
                                           lhsT=lhs, rhs=rhs,
                                           start=(kq == 0), stop=(kq == 7))
                    c_t.inc(ins)
            blk.tensor(p5m_tens)

            def p5m_scal(e, sg=sg, CT=CT, CS=CS, VA=VA, VB=VB, bw1n=bw1n,
                         bvb3=bvb3):
                for tb in range(NTB):
                    for half in range(2):
                        c_t.wait(e, CT(tb, half))
                        if half == 1:
                            v_a.wait(e, VA(tb, 0))
                        c_s.inc(e.activation(out=GS[:, 0:512],
                                             in_=P32[:, 0:512], func=AF.Silu))
                    v_a.wait(e, VA(tb, 2))
                    if tb >= 1:
                        v_b.wait(e, VB(tb - 1))
                    c_s.inc(e.activation(out=Q8[:, 0:1024], in_=SM[:, 0:1024],
                                         func=AF.Copy))
                    if sg < NSEG - 1:
                        for j in range(5):
                            pm_tile_scal(e, 5 * tb + j, bw1n)
            blk.scalar(p5m_scal)

            def p5m_vec(e, sg=sg, CT=CT, CS=CS, VB=VB, VC=VC, bsp3=bsp3,
                        bw1n=bw1n, RLC=RLC):
                for tb in range(NTB):
                    for half in range(2):
                        c_s.wait(e, CS(tb, half))
                        v_a.inc(e.tensor_tensor(
                            out=SM[:, 1024 + half * 512:1536 + half * 512],
                            in0=P32[:, 512:1024], in1=GS[:, 0:512],
                            op=ALU.mult))
                    ht = SM[:, 1024:2048]
                    f_v.cut(e, e.tensor_reduce(
                        out=SCL[:, 4096:4128],
                        in_=ht.rearrange("p (j b) -> p b j", b=32),
                        axis=AX.X, op=ALU.max, apply_absolute_value=True))
                    e.reciprocal(out=SCL[:, 4128:4160], in_=SCL[:, 4096:4128])
                    f_v.cut(e, e.tensor_scalar(out=SCL[:, 4160:4192],
                                               in0=SCL[:, 4096:4128],
                                               scalar1=1.0 / 224.0,
                                               scalar2=None, op0=ALU.mult))
                    v_a.inc(e.scalar_tensor_tensor(
                        out=SM[:, 0:1024].rearrange("p (j b) -> p j b", b=32),
                        in0=ht.rearrange("p (j b) -> p j b", b=32),
                        scalar=224.0, op0=ALU.mult,
                        in1=SCL[:, 4128:4160].rearrange(
                            "p (o b) -> p o b", o=1).broadcast_to([128, 32, 32]),
                        op1=ALU.mult))
                    c_s.wait(e, CS(tb, 2))
                    v_b.inc(e.tensor_tensor(
                        out=Bf[:, 16384:17408].rearrange("p (j b) -> p j b", b=32),
                        in0=Q8[:, 0:1024].rearrange("p (j b) -> p j b", b=32),
                        in1=SCL[:, 4160:4192].rearrange(
                            "p (o b) -> p o b", o=1).broadcast_to([128, 32, 32]),
                        op=ALU.mult))
                    c_t.wait(e, CT(tb, 2))
                    ins = None
                    for t in range(8):
                        ins = e.tensor_copy(out=Bf[:, 17408 + t * 128:
                                                  17408 + (t + 1) * 128],
                                            in_=P16[:, t * 128:(t + 1) * 128])
                    v_c.inc(ins)
                    c_t.wait(e, CT(tb, 3))
                    s_p.wait(e, bsp3 + 16 * RLC[tb])
                    cst = C[:, (tb % 2) * 2048:(tb % 2) * 2048 + 2048]
                    v_c.inc(e.scalar_tensor_tensor(
                        out=cst, in0=P32[:, 1024:3072], scalar=1.0, in1=cst,
                        op0=ALU.mult, op1=ALU.add))
                    if sg < NSEG - 1:
                        for j in range(5):
                            pm_tile_vec(e, 5 * tb + j, CS(tb, 3 + j))
                if sg < NSEG - 1:
                    c_s.wait(e, CS(NTB - 1, 2))
                    pm_folds_max(e)
                    pm_scales(e)
            blk.vector(p5m_vec)

            assert c_t.n == bct3 + 4 * NTB
            assert c_s.n == bcs3 + (64 if sg < NSEG - 1 else 3 * NTB)
            assert v_a.n == bva3 + vstep * NTB
            assert v_b.n == bvb3 + NTB + (1 if sg < NSEG - 1 else 0)
            assert v_c.n == bvc3 + 2 * NTB and s_p.n == bsp3 + 16 * 32
            ct_guA = bct3 + 4 * (NTB - 1) + 2
            ct_dnB = bct3 + 4 * NTB
            mc_cur = mc_next
            if STOP == 60 + sg:
                _stop_tail()
                return nc

        if STOP == 6:
            _stop_tail()
            return nc

        bx6, bsp6, bva6 = s_x.n, s_p.n, v_a.n
        p6_order = []
        for tb in range(NTB):
            p6_order.append(("L", tb))
            if tb >= 2:
                p6_order.append(("S", tb - 2))
        p6_order += [("S", NTB - 2), ("S", NTB - 1)]
        p6_cum = {}
        cnt6 = 0
        for it6 in p6_order:
            cnt6 += 16
            p6_cum[it6] = bx6 + cnt6

        def p6_sync(e, fva=v_a.n):
            e.wait_ge(v_a.h, fva)
            for kind, tb in p6_order:
                if kind == "L":
                    if tb >= 2:
                        v_a.wait(e, bva6 + (tb - 1))
                    s_x.dma(e, out=W[:, (tb % 2) * 2048:(tb % 2) * 2048 + 2048],
                                in_=inp["xf"][tb * 128:(tb + 1) * 128, :]
                                )
                else:
                    v_a.wait(e, bva6 + (tb + 1))
                    s_x.dma(e, out=out_own[tb * 128:(tb + 1) * 128, :],
                                in_=SM[:, 0:2048])
            s_x.wait(e, s_x.n)
        blk.sync(p6_sync)

        def p6_gp(e, fvc=v_c.n):
            e.wait_ge(v_c.h, fvc)
            for tb in range(NTB):
                if tb >= 2:
                    v_a.wait(e, bva6 + (tb - 1))
                s_p.dma(e, out=C[:, (tb % 2) * 2048:(tb % 2) * 2048 + 2048],
                            in_=d_acc[tb])
        blk.gpsimd(p6_gp)

        def p6_vec(e):
            for tb in range(NTB):
                s_x.wait(e, p6_cum[("L", tb)])
                s_p.wait(e, bsp6 + 16 * (tb + 1))
                if tb >= 1:
                    s_x.wait(e, p6_cum[("S", tb - 1)])
                cst = C[:, (tb % 2) * 2048:(tb % 2) * 2048 + 2048]
                xst = W[:, (tb % 2) * 2048:(tb % 2) * 2048 + 2048]
                v_a.inc(e.tensor_tensor(
                    out=SM[:, 0:2048].rearrange("p (b j) -> p b j", j=32),
                    in0=cst.rearrange("p (j b) -> p b j", b=64),
                    in1=xst.rearrange("p (j b) -> p b j", b=64),
                    op=ALU.add))
        blk.vector(p6_vec)

    return nc


def _perms():
    permH = np.empty(D, np.int64)
    for j in range(32):
        for b in range(64):
            permH[j * 64 + b] = 32 * b + j
    rho = np.empty(D, np.int64)
    for h in range(NH):
        for k in range(DK):
            rho[h * DK + k] = h * DK + (2 * k if k < 64 else 2 * (k - 64) + 1)
    permI = np.empty(II, np.int64)
    for sg in range(NSEG):
        for j in range(32):
            for b in range(32):
                permI[sg * SEG + j * 32 + b] = sg * SEG + 32 * b + j
    return permH, rho, permI


def kernel(L0_attn_x, L0_attn_wq, L0_attn_wk, L0_attn_wv, L0_attn_wo,
           L0_ffn_gate_w, L0_ffn_up_w, L0_ffn_down_w):
    x = np.ascontiguousarray(np.asarray(L0_attn_x, np.float32))
    wq = np.asarray(L0_attn_wq, np.float32)
    wk = np.asarray(L0_attn_wk, np.float32)
    wv = np.asarray(L0_attn_wv, np.float32)
    wo = np.asarray(L0_attn_wo, np.float32)
    gw = np.asarray(L0_ffn_gate_w, np.float32)
    uw = np.asarray(L0_ffn_up_w, np.float32)
    dw = np.asarray(L0_ffn_down_w, np.float32)

    permH, rho, permI = _perms()
    wqT = np.ascontiguousarray(wq[rho][:, permH].T)
    wkT = np.ascontiguousarray(wk[rho][:, permH].T)
    wvT = np.ascontiguousarray(wv[:, permH].T)
    woT = np.ascontiguousarray(wo[permH].T)
    gwT = np.ascontiguousarray(gw[permI][:, permH].T)
    uwT = np.ascontiguousarray(uw[permI][:, permH].T)
    dwT = np.ascontiguousarray(dw[permH][:, permI].T)
    tri = np.ascontiguousarray(np.triu(np.ones((128, 128)))).astype(
        ml_dtypes.bfloat16)
    ident = np.eye(128).astype(ml_dtypes.bfloat16)
    inv = (np.float32(10000.0) **
           (-(np.arange(0, DK, 2).astype(np.float32) / np.float32(DK))))

    nc = build_nc()
    in_maps = []
    for c in range(8):
        b, par = divmod(c, 2)
        ordbl = [2 * j + par for j in range(8)] + [2 * j + (1 - par)
                                                  for j in range(8)]
        idx = np.concatenate([np.arange(g * 128, (g + 1) * 128) for g in ordbl])
        xf = np.ascontiguousarray(x[b][idx][:, permH])
        pos = idx.astype(np.float32)
        ang = inv[:, None].astype(np.float32) * pos[None, :].astype(np.float32)
        sn = np.sin(ang).astype(np.float32)
        cs_ = np.cos(ang).astype(np.float32)
        sint = np.ascontiguousarray(np.concatenate([-sn, sn], 0))
        cost = np.ascontiguousarray(np.concatenate([cs_, cs_], 0))
        parf = np.full((128, 1), float(par), np.float32)
        in_maps.append({
            "xf": xf, "wqT": wqT, "wkT": wkT, "wvT": wvT, "woT": woT,
            "gwT": gwT, "uwT": uwT, "dwT": dwT, "sint": sint, "cost": cost,
            "trimask": tri, "parf": parf, "ident": ident,
        })

    res = run_bass_kernel_spmd(nc, in_maps, core_ids=list(range(8)))
    out = np.empty((4, S, D), np.float32)
    for c in range(8):
        b, par = divmod(c, 2)
        o = res.results[c]["out_own"]
        for j in range(8):
            out[b][(2 * j + par) * 128:(2 * j + par + 1) * 128] = \
                o[j * 128:(j + 1) * 128]
    return out


# version 10
# speedup vs baseline: 1.3048x; 1.0782x over previous
import contextlib
import os
import numpy as np
import ml_dtypes
import concourse.bass as bass
import concourse.mybir as mybir
from concourse.bass_utils import run_bass_kernel_spmd

F32 = mybir.dt.float32
BF16 = mybir.dt.bfloat16
FP8 = mybir.dt.float8e4
ALU = mybir.AluOpType
AF = mybir.ActivationFunctionType
AX = mybir.AxisListType

D = 2048
S = 2048
NH = 16
DK = 128
II = 8192
SEG = 1024
NSEG = II // SEG
TOK = 1024
NTB = TOK // 128
NKC = D // 128
NXT = S // 128
NSP = 16
EPS = 1e-5
RSD = float(1.0 / np.sqrt(np.float64(DK)))
INV448H = 0.5 / 448.0


class Sem:
    __slots__ = ("h", "n")

    def __init__(self, nc, es, name):
        self.h = es.enter_context(nc.semaphore(name))
        self.n = 0

    def inc(self, ins, k=1):
        ins.then_inc(self.h, k)
        self.n += k
        return ins

    def wait(self, eng, v):
        eng.wait_ge(self.h, v)


class Fence:
    __slots__ = ("h", "n")

    def __init__(self, nc, es, name):
        self.h = es.enter_context(nc.semaphore(name))
        self.n = 0

    def cut(self, e, ins):
        ins.then_inc(self.h, 1)
        self.n += 1
        e.wait_ge(self.h, self.n)
        return ins


class Stream:

    def __init__(self, nc, es, name, k=3):
        self.sems = [es.enter_context(nc.semaphore(f"{name}{j}"))
                     for j in range(k)]
        self.k = k
        self.i = 0

    @property
    def n(self):
        return 16 * self.i

    def dma(self, e, out, in_):
        i = self.i
        j = i % self.k
        if i >= self.k:
            e.wait_ge(self.sems[j], 16 * (i // self.k))
        ins = e.dma_start(out=out, in_=in_)
        ins.then_inc(self.sems[j], 16)
        self.i += 1
        return ins

    def wait(self, e, v):
        assert v % 16 == 0
        n = v // 16
        for idx in range(max(0, n - self.k), n):
            e.wait_ge(self.sems[idx % self.k], 16 * (idx // self.k + 1))


class B_:
    pass


def build_nc():
    STOP = int(os.environ.get("K_STOP", "99"))
    SUB = os.environ.get("K_SUB", "full")
    z = B_()
    nc = bass.Bass("TRN2", target_bir_lowering=False, debug=False)
    z.nc = nc
    inp = {}
    for nm, shp in [("xf", [S, D]), ("wqT", [D, D]), ("wkT", [D, D]),
                    ("wvT", [D, D]), ("woT", [D, D]), ("gwT", [D, II]),
                    ("uwT", [D, II]), ("dwT", [II, D]),
                    ("sint", [128, S]), ("cost", [128, S])]:
        inp[nm] = nc.dram_tensor(nm, shp, F32, kind="ExternalInput").ap()
    inp["trimask"] = nc.dram_tensor("trimask", [128, 128], BF16, kind="ExternalInput").ap()
    inp["parf"] = nc.dram_tensor("parf", [128, 1], F32, kind="ExternalInput").ap()
    inp["ident"] = nc.dram_tensor("ident", [128, 128], BF16, kind="ExternalInput").ap()
    out_own = nc.dram_tensor("out_own", [TOK, D], F32, kind="ExternalOutput").ap()
    d_k = nc.dram_tensor("d_k", [NH, 128, S], BF16, kind="Internal").ap()
    d_q = nc.dram_tensor("d_q", [NH, 128, TOK], BF16, kind="Internal").ap()
    d_v = nc.dram_tensor("d_v", [S, D], BF16, kind="Internal").ap()
    d_xn = nc.dram_tensor("d_xn", [NTB, 128, D], F32, kind="Internal").ap()
    d_xq = nc.dram_tensor("d_xq", [NTB, 128, NKC * 128], BF16, kind="Internal").ap()
    d_acc = nc.dram_tensor("d_acc", [NTB, 128, D], F32, kind="Internal").ap()

    es = contextlib.ExitStack()
    with es:
        A = es.enter_context(nc.sbuf_tensor("Ar", [128, 32768], BF16))
        Bf = es.enter_context(nc.sbuf_tensor("Br", [128, 24576], BF16))
        C = es.enter_context(nc.sbuf_tensor("Cr", [128, 4096], F32))
        W = es.enter_context(nc.sbuf_tensor("Wr", [128, 4096], F32))
        SM = es.enter_context(nc.sbuf_tensor("SMr", [128, 2048], F32))
        SCL = es.enter_context(nc.sbuf_tensor("SCLr", [128, 8192], F32))
        GS = es.enter_context(nc.sbuf_tensor("GSr", [128, 512], F32))
        Q8 = es.enter_context(nc.sbuf_tensor("Q8r", [128, 2048], FP8))
        TRI = es.enter_context(nc.sbuf_tensor("TRIr", [128, 128], BF16))
        PARF = es.enter_context(nc.sbuf_tensor("PARFr", [128, 1], F32))
        ONE1 = es.enter_context(nc.sbuf_tensor("ONE1r", [128, 1], BF16))
        EPSC = es.enter_context(nc.sbuf_tensor("EPSCr", [128, 1], F32))
        ONER = es.enter_context(nc.sbuf_tensor("ONERr", [1, 128], BF16))
        IDN = es.enter_context(nc.sbuf_tensor("IDNr", [128, 128], BF16))
        P32 = es.enter_context(nc.psum_tensor("P32r", [128, 4096], F32))
        P16 = P32[:, 3584:4096].bitcast(BF16)

        s_x = Stream(nc, es, "sx")
        s_p = Stream(nc, es, "sp")
        c_s = Sem(nc, es, "c_s")
        c_t = Sem(nc, es, "c_t")
        v_a = Sem(nc, es, "v_a")
        v_b = Sem(nc, es, "v_b")
        v_c = Sem(nc, es, "v_c")
        f_v = Fence(nc, es, "f_v")

        blk = es.enter_context(nc.Block())

        def _stop_tail():
            dump = os.environ.get("K_DUMP", "")

            def t_sync(e):
                for sm in (c_s, c_t, v_a, v_b, v_c):
                    e.wait_ge(sm.h, sm.n)
                s_p.wait(e, s_p.n)
                s_x.wait(e, s_x.n)
                if dump == "xn":
                    for j in range(8):
                        s_x.dma(e, out=out_own[j * 128:(j + 1) * 128, :],
                                in_=d_xn[j])
                elif dump == "acc":
                    for j in range(8):
                        s_x.dma(e, out=out_own[j * 128:(j + 1) * 128, :],
                                in_=d_acc[j])
                elif dump == "q":
                    for j in range(8):
                        s_x.dma(e, out=out_own[j * 128:(j + 1) * 128, 0:512]
                                .bitcast(BF16), in_=d_q[2 * j])
                elif dump == "k":
                    for j in range(8):
                        s_x.dma(e, out=out_own[j * 128:(j + 1) * 128, 0:1024]
                                .bitcast(BF16), in_=d_k[2 * j])
                elif dump == "v":
                    for j in range(8):
                        s_x.dma(e, out=out_own[j * 128:(j + 1) * 128, 0:1024]
                                .bitcast(BF16), in_=d_v[j * 128:(j + 1) * 128, :])
                elif dump == "xq":
                    for j in range(8):
                        s_x.dma(e, out=out_own[j * 128:(j + 1) * 128, 0:1024]
                                .bitcast(BF16), in_=d_xq[j])
                else:
                    s_x.dma(e, out=out_own[0:128, :], in_=SM[:, 0:2048])
                s_x.wait(e, s_x.n)
            blk.sync(t_sync)

        def p0_sync(e):
            s_x.dma(e, out=C[:, 0:2048], in_=inp["sint"])
            s_x.dma(e, out=C[:, 2048:4096], in_=inp["cost"])
            s_x.dma(e, out=TRI[:], in_=inp["trimask"])
            s_x.dma(e, out=PARF[:], in_=inp["parf"])
            s_x.dma(e, out=IDN[:], in_=inp["ident"])
        blk.sync(p0_sync)

        def p0_vec(e):
            e.memset(ONE1[:], 1.0)
            e.memset(EPSC[:], EPS)
            v_a.inc(e.memset(ONER[:], 1.0))
        blk.vector(p0_vec)

        if STOP == 0:
            _stop_tail()
            return nc

        bx, bva, bcs, bvb, bct, bvc, bsp = (s_x.n, v_a.n, c_s.n, v_b.n,
                                            c_t.n, v_c.n, s_p.n)

        def p1_sync(e):
            for i in range(NXT):
                if i >= 2:
                    v_b.wait(e, bvb + (i - 1))
                    if SUB not in ("ln", "tp") and i - 2 < NTB:
                        s_p.wait(e, bsp + 16 * (i - 1))
                s_x.dma(e, out=W[:, (i % 2) * 2048:(i % 2) * 2048 + 2048],
                            in_=inp["xf"][i * 128:(i + 1) * 128, :])
        blk.sync(p1_sync)

        def p1_evac(e, G):
            i, g = G // 2, G % 2
            c_t.wait(e, bct + G + 1)
            ins = None
            for t in range(8):
                kc = g * 8 + t
                ins = e.tensor_copy(
                    out=A[:, kc * 2048 + i * 128:kc * 2048 + (i + 1) * 128],
                    in_=P16[:, t * 128:(t + 1) * 128])
            v_c.inc(ins)

        def p1_vec(e):
            for i in range(NXT + 1):
                if i < NXT:
                    xt = W[:, (i % 2) * 2048:(i % 2) * 2048 + 2048]
                    st = (i % 2) * 8
                    s_x.wait(e, bx + 16 * (i + 1))
                    if SUB != "ln" and i >= 2:
                        c_t.wait(e, bct + 2 * (i - 2) + 2)
                    f_v.cut(e, e.tensor_reduce(out=SCL[:, st:st + 1], in_=xt,
                                               axis=AX.X, op=ALU.add))
                    f_v.cut(e, e.tensor_scalar(out=SCL[:, st + 1:st + 2],
                                               in0=SCL[:, st:st + 1],
                                               scalar1=1.0 / 2048.0,
                                               scalar2=None, op0=ALU.mult))
                    e.tensor_scalar(out=xt, in0=xt, scalar1=SCL[:, st + 1:st + 2],
                                    scalar2=None, op0=ALU.subtract)
                    e.tensor_tensor(out=SM[:, 0:2048], in0=xt, in1=xt, op=ALU.mult)
                    v_a.inc(e.tensor_reduce(out=SCL[:, st + 2:st + 3],
                                            in_=SM[:, 0:2048], axis=AX.X, op=ALU.add))
                    c_s.wait(e, bcs + (i + 1))
                    e.tensor_scalar(out=xt, in0=xt, scalar1=SCL[:, st + 4:st + 5],
                                    scalar2=None, op0=ALU.mult)
                    bb = Bf[:, (i % 2) * 2048:(i % 2) * 2048 + 2048]
                    v_b.inc(e.tensor_copy(out=bb, in_=xt))
                if SUB != "ln" and i >= 1:
                    p1_evac(e, 2 * (i - 1))
                    p1_evac(e, 2 * (i - 1) + 1)
        blk.vector(p1_vec)

        if SUB == "ln" and STOP == 1:
            def p1_scal0(e):
                for i in range(NXT):
                    st = (i % 2) * 8
                    v_a.wait(e, bva + (i + 1))
                    e.activation(out=SCL[:, st + 3:st + 4], in_=SCL[:, st + 2:st + 3],
                                 func=AF.Ln, bias=EPSC[:, 0:1], scale=1.0 / 2048.0)
                    c_s.inc(e.activation(out=SCL[:, st + 4:st + 5],
                                         in_=SCL[:, st + 3:st + 4],
                                         func=AF.Exp, scale=-0.5))
            blk.scalar(p1_scal0)
            _stop_tail()
            return nc

        def p1_scal(e):
            for i in range(NXT):
                st = (i % 2) * 8
                v_a.wait(e, bva + (i + 1))
                e.activation(out=SCL[:, st + 3:st + 4], in_=SCL[:, st + 2:st + 3],
                             func=AF.Ln, bias=EPSC[:, 0:1], scale=1.0 / 2048.0)
                c_s.inc(e.activation(out=SCL[:, st + 4:st + 5], in_=SCL[:, st + 3:st + 4],
                                     func=AF.Exp, scale=-0.5))
        blk.scalar(p1_scal)

        def p1_tens(e):
            for i in range(NXT):
                bb = Bf[:, (i % 2) * 2048:(i % 2) * 2048 + 2048]
                v_b.wait(e, bvb + (i + 1))
                for g in range(2):
                    if 2 * i + g >= 1:
                        v_c.wait(e, bvc + 2 * i + g)
                    ins = None
                    for t in range(8):
                        kc = g * 8 + t
                        ins = e.transpose(out=P16[:, t * 128:(t + 1) * 128],
                                          in_=bb[:, kc * 128:(kc + 1) * 128],
                                          identity=IDN[:])
                    c_t.inc(ins)
        blk.tensor(p1_tens)

        def p1_gp(e):
            for i in range(NTB):
                v_b.wait(e, bvb + (i + 1))
                s_p.dma(e, out=d_xn[i],
                            in_=W[:, (i % 2) * 2048:(i % 2) * 2048 + 2048]
                            )
        blk.gpsimd(p1_gp)

        if SUB == "full":
            assert v_a.n == bva + NXT and c_s.n == bcs + NXT
            assert v_b.n == bvb + NXT and c_t.n == bct + 2 * NXT
            assert v_c.n == bvc + 2 * NXT and s_p.n == bsp + 16 * NTB
        if STOP == 1:
            _stop_tail()
            return nc

        for wnm, kind in [("wkT", "K"), ("wvT", "V"), ("wqT", "Q")]:
            for half in range(2):
                bw, bcs, bct, bvb, bsp = s_x.n, c_s.n, c_t.n, v_b.n, s_p.n

                def p2_sync(e, wnm=wnm, half=half, bcs=bcs,
                            fvb=v_b.n, fsp=s_p.n):
                    e.wait_ge(v_b.h, fvb)
                    s_p.wait(e, fsp)
                    e.wait_ge(c_s.h, bcs)
                    for k in range(NKC):
                        if k >= 2:
                            c_s.wait(e, bcs + (k - 1))
                        s_x.dma(e, 
                            out=W[:, (k % 2) * 1024:(k % 2) * 1024 + 1024],
                            in_=inp[wnm][k * 128:(k + 1) * 128,
                                         half * 1024:half * 1024 + 1024]
                            )
                blk.sync(p2_sync)

                def p2_scal(e, bw=bw, fct=c_t.n):
                    e.wait_ge(c_t.h, fct)
                    for k in range(NKC):
                        s_x.wait(e, bw + 16 * (k + 1))
                        c_s.inc(e.activation(
                            out=Bf[:, k * 1024:(k + 1) * 1024],
                            in_=W[:, (k % 2) * 1024:(k % 2) * 1024 + 1024],
                            func=AF.Copy))
                blk.scalar(p2_scal)

                if kind in ("K", "Q"):
                    ntc = (S if kind == "K" else TOK) // 512

                    def p2_tens(e, ntc=ntc, bcs=bcs, bvb=bvb):
                        e.wait_ge(v_b.h, bvb)
                        for hh in range(8):
                            if hh >= 1:
                                v_b.wait(e, bvb + hh)
                            ins = None
                            for k in range(NKC):
                                c_s.wait(e, bcs + (k + 1))
                                lhs = Bf[:, k * 1024 + hh * 128:k * 1024 + (hh + 1) * 128]
                                for t in range(ntc):
                                    rhs = A[:, k * 2048 + t * 512:k * 2048 + (t + 1) * 512]
                                    ins = e.matmul(out=P32[:, t * 512:(t + 1) * 512],
                                                   lhsT=lhs, rhs=rhs,
                                                   start=(k == 0), stop=(k == NKC - 1))
                            c_t.inc(ins)
                    blk.tensor(p2_tens)

                    def p2_vec(e, ntc=ntc, bct=bct, bsp=bsp):
                        sin = C[:, 0:2048]
                        cos = C[:, 2048:4096]
                        for hh in range(8):
                            c_t.wait(e, bct + hh + 1)
                            if hh >= 2:
                                s_p.wait(e, bsp + 16 * (hh - 1))
                            stg = Bf[:, 16384 + (hh % 2) * 2048:
                                     16384 + (hh % 2) * 2048 + ntc * 512]
                            ins = None
                            for t in range(ntc):
                                ps = P32[:, t * 512:(t + 1) * 512]
                                tok0 = t * 512
                                s1 = SM[:, 0:512]
                                t2 = SM[:, 512:1024]
                                e.tensor_tensor(out=s1[0:64, :], in0=ps[64:128, :],
                                                in1=sin[0:64, tok0:tok0 + 512], op=ALU.mult)
                                e.tensor_tensor(out=s1[64:128, :], in0=ps[0:64, :],
                                                in1=sin[64:128, tok0:tok0 + 512], op=ALU.mult)
                                e.tensor_tensor(out=t2, in0=ps,
                                                in1=cos[:, tok0:tok0 + 512], op=ALU.mult)
                                ins = e.tensor_tensor(out=stg[:, t * 512:(t + 1) * 512],
                                                      in0=t2, in1=s1, op=ALU.add)
                            v_b.inc(ins)
                    blk.vector(p2_vec)

                    def p2_gp(e, kind=kind, half=half, ntc=ntc, bvb=bvb):
                        dst = d_k if kind == "K" else d_q
                        for hh in range(8):
                            v_b.wait(e, bvb + hh + 1)
                            stg = Bf[:, 16384 + (hh % 2) * 2048:
                                     16384 + (hh % 2) * 2048 + ntc * 512]
                            s_p.dma(e, out=dst[half * 8 + hh], in_=stg
                                        )
                    blk.gpsimd(p2_gp)
                    nu = 8
                else:
                    def p2_tens(e, bcs=bcs, bvb=bvb):
                        e.wait_ge(v_b.h, bvb)
                        for tb in range(NXT):
                            if tb >= 2:
                                v_b.wait(e, bvb + tb - 1)
                            pc = (tb % 2) * 1024
                            ins = None
                            for k in range(NKC):
                                c_s.wait(e, bcs + (k + 1))
                                lhs = A[:, k * 2048 + tb * 128:k * 2048 + (tb + 1) * 128]
                                for t in range(2):
                                    rhs = Bf[:, k * 1024 + t * 512:k * 1024 + (t + 1) * 512]
                                    ins = e.matmul(
                                        out=P32[:, pc + t * 512:pc + (t + 1) * 512],
                                        lhsT=lhs, rhs=rhs,
                                        start=(k == 0), stop=(k == NKC - 1))
                            c_t.inc(ins)
                    blk.tensor(p2_tens)

                    def p2_scal2(e, bct=bct, bsp=bsp):
                        for tb in range(NXT):
                            c_t.wait(e, bct + tb + 1)
                            if tb >= 2:
                                s_p.wait(e, bsp + 16 * (tb - 1))
                            pc = (tb % 2) * 1024
                            stg = Bf[:, 20480 + (tb % 2) * 1024:
                                     20480 + (tb % 2) * 1024 + 1024]
                            v_b.inc(e.activation(out=stg, in_=P32[:, pc:pc + 1024],
                                                 func=AF.Copy))
                    blk.scalar(p2_scal2)

                    def p2_gp(e, half=half, bvb=bvb):
                        for tb in range(NXT):
                            v_b.wait(e, bvb + tb + 1)
                            stg = Bf[:, 20480 + (tb % 2) * 1024:
                                     20480 + (tb % 2) * 1024 + 1024]
                            s_p.dma(e, out=d_v[tb * 128:(tb + 1) * 128,
                                                half * 1024:half * 1024 + 1024],
                                        in_=stg)
                    blk.gpsimd(p2_gp)
                    nu = NXT

                assert c_t.n == bct + nu and v_b.n == bvb + nu
                assert s_p.n == bsp + 16 * nu

        if STOP == 2:
            _stop_tail()
            return nc

        bsr, bct, bcs, bva, bvb, bvc = s_x.n, c_t.n, c_s.n, v_a.n, v_b.n, v_c.n
        sp_qb0 = [sp % 8 for sp in range(NSP)]
        sp_nq = [8 - q for q in sp_qb0]

        ct = {}
        n = bct
        for h in range(NH):
            n += 1; ct[("rows", h)] = n
            n += 1; ct[("bbc", h)] = n
            for sp in range(NSP):
                n += 1; ct[("sc", h, sp)] = n
                n += 1; ct[("av", h, sp)] = n
            n += 1; ct[("rbc", h)] = n
        ct_end = n
        cs = {}
        n = bcs
        for h in range(NH):
            n += 1; cs[("sq", h)] = n
            n += 1; cs[("bnd", h)] = n
            for sp in range(NSP):
                n += 1; cs[("exp", h, sp)] = n
        cs_end = n
        va = {}
        n = bva
        for h in range(NH):
            n += 1; va[("mkk", h)] = n
            n += 1; va[("bnd", h)] = n
            for sp in range(NSP):
                n += 1; va[("t", h, sp)] = n
                n += 1; va[("avadd", h, sp)] = n
        va_end = n
        vb = {}
        n = bvb
        for h in range(NH):
            n += 1; vb[("brow", h)] = n
            for sp in range(NSP):
                n += 1; vb[("P", h, sp)] = n
        vb_end = n
        vc = {}
        n = bvc
        for h in range(NH):
            n += 1; vc[("rec", h)] = n
            n += 1; vc[("ao", h)] = n
        vc_end = n

        def kcol(h):
            return (h % 2) * 2048

        def vvcol(h):
            return 4096 + (h % 2) * 2048

        def qcol(h):
            return 8192 + (h % 2) * 1024

        AACC = SCL[:, 4096:5120]
        DACC = SCL[0:1, 5120:6144]

        def p3_sync(e, fsp=s_p.n, fct=c_t.n, fcs=c_s.n):
            s_p.wait(e, fsp)
            e.wait_ge(c_t.h, fct)
            e.wait_ge(c_s.h, fcs)
            for h in range(NH):
                if h >= 2:
                    c_t.wait(e, ct[("av", h - 2, NSP - 1)])
                    c_s.wait(e, cs[("sq", h - 1)])
                s_x.dma(e, out=Bf[:, kcol(h):kcol(h) + 2048], in_=d_k[h])
                s_x.dma(e, out=Bf[:, vvcol(h):vvcol(h) + 2048].rearrange(
                    "p (sb j) -> p sb j", j=128),
                    in_=d_v[:, h * 128:(h + 1) * 128].rearrange(
                        "(sb p) j -> p sb j", p=128))
                s_x.dma(e, out=Bf[:, qcol(h):qcol(h) + 1024], in_=d_q[h])
        blk.sync(p3_sync)

        def p3_scal(e):
            for h in range(NH):
                s_x.wait(e, bsr + 48 * (h + 1))
                e.activation(out=Bf[:, 12288:14336],
                             in_=Bf[:, kcol(h):kcol(h) + 2048], func=AF.Square)
                c_s.inc(e.activation(out=Bf[:, 14336:15360],
                                     in_=Bf[:, qcol(h):qcol(h) + 1024],
                                     func=AF.Square))
                v_a.wait(e, va[("mkk", h)])
                c_s.inc(e.activation(out=SCL[0:1, 3072:4096],
                                     in_=SCL[0:1, 2048:3072], func=AF.Sqrt))
                for sp in range(NSP):
                    nq = sp_nq[sp]
                    v_a.wait(e, va[("t", h, sp)])
                    if sp >= 2:
                        c_t.wait(e, ct[("av", h, sp - 2)])
                    elif h >= 1:
                        c_t.wait(e, ct[("av", h - 1, NSP - 2 + sp)])
                    tst = SM[:, (sp % 2) * 1024:(sp % 2) * 1024 + nq * 128]
                    pst = Bf[:, 10240 + (sp % 2) * 1024:
                             10240 + (sp % 2) * 1024 + nq * 128]
                    c_s.inc(e.activation(out=pst, in_=tst, func=AF.Exp))
        blk.scalar(p3_scal)

        def p3_tens(e, fvb=v_b.n, fvc=v_c.n):
            e.wait_ge(v_b.h, fvb)
            e.wait_ge(v_c.h, fvc)
            for h in range(NH):
                c_s.wait(e, cs[("sq", h)])
                if h >= 1:
                    v_c.wait(e, vc[("ao", h - 1)])
                ins = None
                for t in range(4):
                    ins = e.matmul(out=P32[0:1, t * 512:(t + 1) * 512], lhsT=ONE1[:],
                                   rhs=Bf[:, 12288 + t * 512:12288 + (t + 1) * 512],
                                   start=True, stop=True)
                for t in range(2):
                    ins = e.matmul(out=P32[0:1, 2048 + t * 512:2048 + (t + 1) * 512],
                                   lhsT=ONE1[:],
                                   rhs=Bf[:, 14336 + t * 512:14336 + (t + 1) * 512],
                                   start=True, stop=True)
                c_t.inc(ins)
                v_b.wait(e, vb[("brow", h)])
                ins = None
                for t in range(2):
                    ins = e.matmul(out=P32[:, t * 512:(t + 1) * 512], lhsT=ONER[:],
                                   rhs=Bf[0:1, 15360 + t * 512:15360 + (t + 1) * 512],
                                   start=True, stop=True)
                c_t.inc(ins)
                for sp in range(NSP):
                    q0, nq = sp_qb0[sp], sp_nq[sp]
                    if sp < 2:
                        v_a.wait(e, va[("bnd", h)])
                    else:
                        v_a.wait(e, va[("avadd", h, sp - 2)])
                    ksl = Bf[:, kcol(h) + sp * 128:kcol(h) + (sp + 1) * 128]
                    pcol = (sp % 2) * 1024
                    ins = None
                    nmm = (nq * 128 + 511) // 512
                    for t in range(nmm):
                        w0 = t * 512
                        w1 = min(nq * 128, w0 + 512)
                        rhs = Bf[:, qcol(h) + q0 * 128 + w0:qcol(h) + q0 * 128 + w1]
                        ins = e.matmul(out=P32[:, pcol + w0:pcol + w1],
                                       lhsT=ksl, rhs=rhs, start=True, stop=True)
                    c_t.inc(ins)
                    v_b.wait(e, vb[("P", h, sp)])
                    vsl = Bf[:, vvcol(h) + sp * 128:vvcol(h) + (sp + 1) * 128]
                    pb = Bf[:, 10240 + (sp % 2) * 1024:
                            10240 + (sp % 2) * 1024 + nq * 128]
                    vcol0 = 2048 + (sp % 2) * 1024
                    ins = None
                    for t in range(nmm):
                        w0 = t * 512
                        w1 = min(nq * 128, w0 + 512)
                        ins = e.matmul(out=P32[:, vcol0 + w0:vcol0 + w1],
                                       lhsT=vsl, rhs=pb[:, w0:w1],
                                       start=True, stop=True)
                    for t in range(nmm):
                        w0 = t * 512
                        w1 = min(nq * 128, w0 + 512)
                        ins = e.matmul(out=P32[0:1, pcol + w0:pcol + w1],
                                       lhsT=ONE1[:], rhs=pb[:, w0:w1],
                                       start=True, stop=True)
                    c_t.inc(ins)
                v_c.wait(e, vc[("rec", h)])
                ins = None
                for t in range(2):
                    ins = e.matmul(out=P32[:, t * 512:(t + 1) * 512], lhsT=ONER[:],
                                   rhs=Bf[0:1, 16384 + t * 512:16384 + (t + 1) * 512],
                                   start=True, stop=True)
                c_t.inc(ins)
        blk.tensor(p3_tens)

        def p3_vec(e):
            for h in range(NH):
                e.memset(AACC[:, :], 0.0)
                e.memset(DACC[:, :], 0.0)
                c_t.wait(e, ct[("rows", h)])
                f_v.cut(e, e.tensor_reduce(out=SCL[0:1, 1024:1025],
                                           in_=P32[0:1, 0:2048],
                                           axis=AX.X, op=ALU.max))
                v_a.inc(e.tensor_scalar(out=SCL[0:1, 2048:3072],
                                        in0=P32[0:1, 2048:3072],
                                        scalar1=SCL[0:1, 1024:1025],
                                        scalar2=None, op0=ALU.mult))
                c_s.wait(e, cs[("bnd", h)])
                v_b.inc(e.tensor_scalar(out=Bf[0:1, 15360:16384],
                                        in0=SCL[0:1, 3072:4096], scalar1=RSD,
                                        scalar2=-40.0, op0=ALU.mult,
                                        op1=ALU.add))
                c_t.wait(e, ct[("bbc", h)])
                v_a.inc(e.tensor_scalar(out=SCL[:, 0:1024], in0=P32[:, 0:1024],
                                        scalar1=-1.0, scalar2=None, op0=ALU.mult))

                def avadd(sp):
                    q0n, nqn = sp_qb0[sp], sp_nq[sp]
                    c_t.wait(e, ct[("av", h, sp)])
                    vcol0 = 2048 + (sp % 2) * 1024
                    pcol = (sp % 2) * 1024
                    e.scalar_tensor_tensor(
                        out=AACC[:, q0n * 128:1024],
                        in0=P32[:, vcol0:vcol0 + nqn * 128], scalar=1.0,
                        in1=AACC[:, q0n * 128:1024],
                        op0=ALU.mult, op1=ALU.add)
                    v_a.inc(e.scalar_tensor_tensor(
                        out=DACC[0:1, q0n * 128:1024],
                        in0=P32[0:1, pcol:pcol + nqn * 128], scalar=1.0,
                        in1=DACC[0:1, q0n * 128:1024],
                        op0=ALU.mult, op1=ALU.add))

                for sp in range(NSP):
                    q0, nq = sp_qb0[sp], sp_nq[sp]
                    c_t.wait(e, ct[("sc", h, sp)])
                    pcol = (sp % 2) * 1024
                    tst = SM[:, (sp % 2) * 1024:(sp % 2) * 1024 + nq * 128]
                    v_a.inc(e.scalar_tensor_tensor(
                        out=tst, in0=P32[:, pcol:pcol + nq * 128], scalar=RSD,
                        in1=SCL[:, q0 * 128:q0 * 128 + nq * 128],
                        op0=ALU.mult, op1=ALU.add))
                    c_s.wait(e, cs[("exp", h, sp)])
                    pb = Bf[:, 10240 + (sp % 2) * 1024:
                            10240 + (sp % 2) * 1024 + nq * 128]
                    if sp < 8:
                        ins = e.tensor_tensor(out=pb[:, 0:128], in0=pb[:, 0:128],
                                              in1=TRI[:], op=ALU.mult)
                    else:
                        ins = e.tensor_scalar(out=pb[:, 0:128], in0=pb[:, 0:128],
                                              scalar1=PARF[:, 0:1], scalar2=None,
                                              op0=ALU.mult)
                    v_b.inc(ins)
                    avadd(sp)
                f_v.cut(e, e.engine_nop())
                e.reciprocal(out=DACC[:, :], in_=DACC[:, :])
                v_c.inc(e.tensor_copy(out=Bf[0:1, 16384:17408], in_=DACC[:, :]))
                c_t.wait(e, ct[("rbc", h)])
                e.tensor_copy(out=SM[:, 0:1024], in_=P32[:, 0:1024])
                v_c.inc(e.tensor_tensor(out=A[:, h * 1024:(h + 1) * 1024],
                                        in0=AACC[:, :], in1=SM[:, 0:1024],
                                        op=ALU.mult))
        blk.vector(p3_vec)

        assert c_t.n == ct_end and c_s.n == cs_end
        assert v_a.n == va_end and v_b.n == vb_end and v_c.n == vc_end

        if STOP == 3:
            _stop_tail()
            return nc

        for half in range(2):
            bw, bcs, bct, bvb, bsp = s_x.n, c_s.n, c_t.n, v_b.n, s_p.n

            def p4_sync(e, half=half, bcs=bcs, fcs=c_s.n, fvc=v_c.n):
                e.wait_ge(c_s.h, fcs)
                e.wait_ge(v_c.h, fvc)
                for k in range(8):
                    kk = half * 8 + k
                    if k >= 2:
                        c_s.wait(e, bcs + (k - 1))
                    s_x.dma(e, out=W[:, (k % 2) * 2048:(k % 2) * 2048 + 2048],
                                in_=inp["woT"][kk * 128:(kk + 1) * 128, :]
                                )
            blk.sync(p4_sync)

            def p4_scal(e, bw=bw, fct=c_t.n):
                e.wait_ge(c_t.h, fct)
                for k in range(8):
                    s_x.wait(e, bw + 16 * (k + 1))
                    c_s.inc(e.activation(out=Bf[:, k * 2048:(k + 1) * 2048],
                                         in_=W[:, (k % 2) * 2048:(k % 2) * 2048 + 2048],
                                         func=AF.Copy))
            blk.scalar(p4_scal)

            def p4_tens(e, half=half, bcs=bcs, bvb=bvb, fvc=v_c.n,
                        fva=v_a.n):
                e.wait_ge(v_c.h, fvc)
                e.wait_ge(v_a.h, fva)
                e.wait_ge(v_b.h, bvb)
                for tb in range(NTB):
                    if tb >= 1:
                        v_b.wait(e, bvb + tb)
                    ins = None
                    for k in range(8):
                        c_s.wait(e, bcs + (k + 1))
                        hh = half * 8 + k
                        lhs = A[:, hh * 1024 + tb * 128:hh * 1024 + (tb + 1) * 128]
                        for t in range(4):
                            rhs = Bf[:, k * 2048 + t * 512:k * 2048 + (t + 1) * 512]
                            ins = e.matmul(out=P32[:, t * 512:(t + 1) * 512],
                                           lhsT=lhs, rhs=rhs,
                                           start=(k == 0), stop=(k == 7))
                    c_t.inc(ins)
            blk.tensor(p4_tens)

            def p4_gp(e, half=half, bvb=bvb, bsp=bsp):
                for tb in range(NTB):
                    if half == 1:
                        s_p.dma(e, out=C[:, (tb % 2) * 2048:(tb % 2) * 2048 + 2048],
                                    in_=d_acc[tb])
                    v_b.wait(e, bvb + tb + 1)
                    s_p.dma(e, out=d_acc[tb],
                                in_=C[:, (tb % 2) * 2048:(tb % 2) * 2048 + 2048]
                                )
            blk.gpsimd(p4_gp)

            def p4_vec(e, half=half, bct=bct, bsp=bsp):
                for tb in range(NTB):
                    c_t.wait(e, bct + tb + 1)
                    cst = C[:, (tb % 2) * 2048:(tb % 2) * 2048 + 2048]
                    if half == 0:
                        if tb >= 2:
                            s_p.wait(e, bsp + 16 * (tb - 1))
                        v_b.inc(e.tensor_copy(out=cst, in_=P32[:, 0:2048]))
                    else:
                        s_p.wait(e, bsp + 32 * tb + 16)
                        if tb >= 2:
                            s_p.wait(e, bsp + 32 * (tb - 1))
                        v_b.inc(e.scalar_tensor_tensor(
                            out=cst, in0=P32[:, 0:2048], scalar=1.0, in1=cst,
                            op0=ALU.mult, op1=ALU.add))
            blk.vector(p4_vec)

            assert c_t.n == bct + NTB and v_b.n == bvb + NTB

        if STOP == 4:
            _stop_tail()
            return nc

        bxn, bva, bct, bvc, bsp = s_x.n, v_a.n, c_t.n, v_c.n, s_p.n

        def p5p_sync(e, fcs=c_s.n):
            e.wait_ge(c_s.h, fcs)
            for tb in range(NTB):
                if tb >= 2:
                    v_a.wait(e, bva + (tb - 1))
                s_x.dma(e, out=W[:, (tb % 2) * 2048:(tb % 2) * 2048 + 2048],
                            in_=d_xn[tb])
        blk.sync(p5p_sync)

        def p5p_evac(e, G):
            tb, g = G // 2, G % 2
            c_t.wait(e, bct + G + 1)
            if tb >= 2 and g == 0:
                s_p.wait(e, bsp + 16 * (tb - 1))
            ins = None
            for t in range(8):
                kc = g * 8 + t
                ins = e.tensor_copy(
                    out=Bf[:, 16384 + (tb % 2) * 2048 + kc * 128:
                           16384 + (tb % 2) * 2048 + (kc + 1) * 128],
                    in_=P16[:, t * 128:(t + 1) * 128])
            v_c.inc(ins)

        def p5p_vec(e):
            for tb in range(NTB + 1):
              if tb < NTB:
                xt = W[:, (tb % 2) * 2048:(tb % 2) * 2048 + 2048]
                s_x.wait(e, bxn + 16 * (tb + 1))
                if tb >= 2:
                    c_t.wait(e, bct + 2 * (tb - 2) + 2)
                f_v.cut(e, e.tensor_reduce(
                    out=SCL[:, 4160:4224],
                    in_=xt.rearrange("p (j b) -> p b j", b=64),
                    axis=AX.X, op=ALU.max, apply_absolute_value=True))
                e.reciprocal(out=SCL[:, 4224:4288], in_=SCL[:, 4160:4224])
                f_v.cut(e, e.tensor_scalar(out=SCL[:, 4288:4352],
                                           in0=SCL[:, 4160:4224],
                                           scalar1=1.0 / 224.0, scalar2=None,
                                           op0=ALU.mult))
                quot = SM[:, 0:2048]
                e.scalar_tensor_tensor(
                    out=quot.rearrange("p (j b) -> p j b", b=64),
                    in0=xt.rearrange("p (j b) -> p j b", b=64),
                    scalar=224.0, op0=ALU.mult,
                    in1=SCL[:, 4224:4288].rearrange(
                        "p (o b) -> p o b", o=1).broadcast_to([128, 32, 64]),
                    op1=ALU.mult)
                e.tensor_copy(out=Q8[:, 0:2048], in_=quot)
                xqs = 20480 + (tb % 2) * 2048
                v_a.inc(e.tensor_tensor(
                    out=Bf[:, xqs:xqs + 2048].rearrange("p (j b) -> p j b", b=64),
                    in0=Q8[:, 0:2048].rearrange("p (j b) -> p j b", b=64),
                    in1=SCL[:, 4288:4352].rearrange(
                        "p (o b) -> p o b", o=1).broadcast_to([128, 32, 64]),
                    op=ALU.mult))
              if tb >= 1:
                p5p_evac(e, 2 * (tb - 1))
                p5p_evac(e, 2 * (tb - 1) + 1)
        blk.vector(p5p_vec)

        def p5p_tens(e, fvb=v_b.n):
            e.wait_ge(v_b.h, fvb)
            e.wait_ge(v_c.h, bvc)
            for tb in range(NTB):
                v_a.wait(e, bva + tb + 1)
                for g in range(2):
                    if 2 * tb + g >= 1:
                        v_c.wait(e, bvc + 2 * tb + g)
                    ins = None
                    xqs = 20480 + (tb % 2) * 2048
                    for t in range(8):
                        kc = g * 8 + t
                        ins = e.transpose(
                            out=P16[:, t * 128:(t + 1) * 128],
                            in_=Bf[:, xqs + kc * 128:xqs + (kc + 1) * 128],
                            identity=IDN[:])
                    c_t.inc(ins)
        blk.tensor(p5p_tens)

        def p5p_gp(e):
            for tb in range(NTB):
                v_c.wait(e, bvc + 2 * tb + 2)
                s_p.dma(e, out=d_xq[tb],
                            in_=Bf[:, 16384 + (tb % 2) * 2048:
                                   16384 + (tb % 2) * 2048 + 2048]
                            )
        blk.gpsimd(p5p_gp)

        assert v_a.n == bva + NTB and c_t.n == bct + 2 * NTB
        assert v_c.n == bvc + 2 * NTB and s_p.n == bsp + 16 * NTB

        if STOP == 5:
            _stop_tail()
            return nc

        def pm_wslot(u):
            if u < 32:
                return W[:, (u % 4) * 1024:(u % 4) * 1024 + 1024]
            return W[:, (u % 2) * 2048:(u % 2) * 2048 + 2048]

        def pm_macc(u):
            if u < 16:
                return SCL[:, 0:1024]
            if u < 32:
                return SCL[:, 1024:2048]
            return SCL[:, 2048:4096]

        def pm_tile_scal(e, u, bw):
            s_x.wait(e, bw + 16 * (u + 1))
            c_s.inc(e.activation(out=pm_wslot(u), in_=pm_wslot(u),
                                 func=AF.Abs))

        def pm_tile_vec(e, u, sqcnt):
            c_s.wait(e, sqcnt)
            m = pm_macc(u)
            v_a.inc(e.tensor_tensor(out=m, in0=pm_wslot(u), in1=m,
                                    op=ALU.max))

        def pm_memsets(e):
            e.memset(SCL[:, 0:1024], 0.0)
            e.memset(SCL[:, 1024:2048], 0.0)
            e.memset(SCL[:, 2048:4096], 0.0)

        def pm_folds_max(e):
            for wi in range(2):
                mm = SCL[:, wi * 1024:wi * 1024 + 1024]
                e.tensor_copy(out=SM[0:64, 0:1024], in_=mm[64:128, :])
                e.tensor_tensor(out=mm[0:64, :], in0=SM[0:64, 0:1024],
                                in1=mm[0:64, :], op=ALU.max)
            mm = SCL[:, 2048:4096]
            e.tensor_copy(out=SM[0:64, 0:2048], in_=mm[64:128, :])
            e.tensor_tensor(out=mm[0:64, :], in0=SM[0:64, 0:2048],
                            in1=mm[0:64, :], op=ALU.max)
            e.tensor_copy(out=SM[0:32, 0:2048], in_=mm[32:64, :])
            v_b.inc(e.tensor_tensor(out=mm[0:32, :], in0=SM[0:32, 0:2048],
                                    in1=mm[0:32, :], op=ALU.max))

        def pm_scales(e):
            for wi in range(2):
                mm = SCL[:, wi * 1024:wi * 1024 + 1024]
                rc = SCL[:, 4096 + wi * 1024:4096 + wi * 1024 + 1024]
                e.reciprocal(out=rc[0:64, :], in_=mm[0:64, :])
                e.tensor_scalar(out=rc[0:64, :], in0=rc[0:64, :],
                                scalar1=224.0, scalar2=None, op0=ALU.mult)
                e.tensor_scalar(out=mm[0:64, :], in0=mm[0:64, :],
                                scalar1=1.0 / 224.0, scalar2=None,
                                op0=ALU.mult)
                e.tensor_copy(out=mm[64:128, :], in_=mm[0:64, :])
                e.tensor_copy(out=rc[64:128, :], in_=rc[0:64, :])
            mm = SCL[:, 2048:4096]
            rc = SCL[:, 6144:8192]
            e.reciprocal(out=rc[0:32, :], in_=mm[0:32, :])
            e.tensor_scalar(out=rc[0:32, :], in0=rc[0:32, :],
                            scalar1=224.0, scalar2=None, op0=ALU.mult)
            e.tensor_scalar(out=mm[0:32, :], in0=mm[0:32, :],
                            scalar1=1.0 / 224.0, scalar2=None, op0=ALU.mult)
            e.tensor_copy(out=mm[32:64, :], in_=mm[0:32, :])
            e.tensor_copy(out=mm[64:128, :], in_=mm[0:64, :])
            e.tensor_copy(out=rc[32:64, :], in_=rc[0:32, :])
            e.tensor_copy(out=rc[64:128, :], in_=rc[0:64, :])

        def pm_dma(e, u, sg):
            if u < 32:
                wi, k = divmod(u, 16)
                wnm = "gwT" if wi == 0 else "uwT"
                s_x.dma(e, out=pm_wslot(u),
                        in_=inp[wnm][k * 128:(k + 1) * 128,
                                     sg * SEG:sg * SEG + SEG])
            else:
                k = u - 32
                s_x.dma(e, out=pm_wslot(u),
                        in_=inp["dwT"][(sg * 8 + k) * 128:
                                       (sg * 8 + k + 1) * 128, :])

        V0s = v_a.n
        assert V0s == bva + NTB
        mc_cur = lambda u, V0s=V0s: V0s + u + 1
        bw1_0 = s_x.n

        def p5s_sync0(e, V0s=V0s, mc=mc_cur):
            for u in range(40):
                if u < 2:
                    v_a.wait(e, V0s - 1)
                elif u < 4:
                    v_a.wait(e, V0s)
                elif u < 32:
                    v_a.wait(e, mc(u - 4))
                elif u == 32:
                    v_a.wait(e, mc(29))
                elif u == 33:
                    v_a.wait(e, mc(31))
                else:
                    v_a.wait(e, mc(u - 2))
                pm_dma(e, u, 0)
        blk.sync(p5s_sync0)

        bsq0, bvb0 = c_s.n, v_b.n

        def p5s_scal0(e, bw=bw1_0):
            for u in range(40):
                pm_tile_scal(e, u, bw)
        blk.scalar(p5s_scal0)

        def p5s_vec0(e, bsq0=bsq0):
            pm_memsets(e)
            for u in range(40):
                pm_tile_vec(e, u, bsq0 + u + 1)
            pm_folds_max(e)
            pm_scales(e)
        blk.vector(p5s_vec0)
        assert v_a.n == V0s + 40 and c_s.n == bsq0 + 40
        assert v_b.n == bvb0 + 1

        if STOP == 50:
            _stop_tail()
            return nc

        ct_guA = c_t.n
        ct_dnB = c_t.n
        for sg in range(NSEG):
            bva2, bcs2, bw2 = v_a.n, c_s.n, s_x.n
            mc_p = mc_cur

            def p5q_sync(e, sg=sg, bva2=bva2, mc_p=mc_p):
                for u in range(40):
                    if u < 2:
                        v_a.wait(e, mc_p(38))
                    elif u < 4:
                        v_a.wait(e, mc_p(39))
                    elif u < 32:
                        v_a.wait(e, bva2 + 2 * (u - 4) + 1)
                    elif u <= 33:
                        v_a.wait(e, bva2 + 63)
                    else:
                        v_a.wait(e, bva2 + 2 * (u - 2) + 1)
                    pm_dma(e, u, sg)
            blk.sync(p5q_sync)

            if sg < NSEG - 1:
                bw1n = s_x.n
                V3 = bva2 + 80
                mc_next = lambda u, V3=V3: V3 + 3 * (u // 5 + 1) + u + 1

                def p5s_syncN(e, sg=sg, bva2=bva2, mc=mc_next):
                    for u in range(40):
                        if u < 2:
                            v_a.wait(e, bva2 + 77)
                        elif u < 4:
                            v_a.wait(e, bva2 + 79)
                        elif u < 32:
                            v_a.wait(e, mc(u - 4))
                        elif u == 32:
                            v_a.wait(e, mc(29))
                        elif u == 33:
                            v_a.wait(e, mc(31))
                        else:
                            v_a.wait(e, mc(u - 2))
                        pm_dma(e, u, sg + 1)
                blk.sync(p5s_syncN)
            else:
                bw1n = None
                mc_next = None

            def p5q_vec(e, sg=sg, bw2=bw2, bcs2=bcs2, ct_guA=ct_guA,
                        ct_dnB=ct_dnB):
                for u in range(40):
                    s_x.wait(e, bw2 + 16 * (u + 1))
                    t_ = pm_wslot(u)
                    if u < 32:
                        wi, k = divmod(u, 16)
                        rec = SCL[:, 4096 + wi * 1024:4096 + wi * 1024 + 1024]
                        s2 = SCL[:, wi * 1024:wi * 1024 + 1024]
                        q8 = Q8[:, (u % 2) * 1024:(u % 2) * 1024 + 1024]
                        dst = A[:, wi * 16384 + k * 1024:
                                wi * 16384 + (k + 1) * 1024]
                    else:
                        k = u - 32
                        rec = SCL[:, 6144:8192]
                        s2 = SCL[:, 2048:4096]
                        q8 = Q8[:, 0:2048]
                        dst = Bf[:, k * 2048:(k + 1) * 2048]
                    if u == 0:
                        c_t.wait(e, ct_guA)
                    if u == 32:
                        c_t.wait(e, ct_dnB)
                    v_a.inc(e.tensor_tensor(out=q8, in0=t_, in1=rec,
                                            op=ALU.mult))
                    v_a.inc(e.tensor_tensor(out=dst, in0=q8, in1=s2,
                                            op=ALU.mult))
                if sg < NSEG - 1:
                    pm_memsets(e)
            blk.vector(p5q_vec)
            assert v_a.n == bva2 + 80 and c_s.n == bcs2

            bct3, bcs3, bva3, bvb3, bvc3, bsp3 = (c_t.n, c_s.n, v_a.n,
                                                  v_b.n, v_c.n, s_p.n)
            vstep = 8 if sg < NSEG - 1 else 3
            cstep = 7 if sg < NSEG - 1 else 2
            CT = lambda tb, i, b=bct3: b + 4 * tb + i + 1
            CS = lambda tb, i, b=bcs3, cstep=cstep: b + cstep * tb + i + 1
            VA = lambda tb, i, b=bva3, vstep=vstep: b + vstep * tb + i + 1
            VB = lambda tb, b=bvb3: b + tb + 1
            VC = lambda tb, i, b=bvc3: b + 2 * tb + i + 1
            XQC = [2, 4, 6, 10, 14, 18, 22, 26]
            RLC = [7, 11, 15, 19, 23, 27, 29, 31]

            def p5m_gp(e, CT=CT, VC=VC):
                for tb in range(2):
                    s_p.dma(e, out=Bf[:, 18432 + (tb % 2) * 1024:
                                      18432 + (tb % 2) * 1024 + 1024],
                            in_=d_xq[tb][:, 0:1024])
                    s_p.dma(e, out=Bf[:, 20480 + (tb % 2) * 1024:
                                      20480 + (tb % 2) * 1024 + 1024],
                            in_=d_xq[tb][:, 1024:2048])
                for tb in range(NTB):
                    if tb + 2 < NTB:
                        t2 = tb + 2
                        c_t.wait(e, CT(tb, 1))
                        s_p.dma(e, out=Bf[:, 18432 + (t2 % 2) * 1024:
                                          18432 + (t2 % 2) * 1024 + 1024],
                                in_=d_xq[t2][:, 0:1024])
                        s_p.dma(e, out=Bf[:, 20480 + (t2 % 2) * 1024:
                                          20480 + (t2 % 2) * 1024 + 1024],
                                in_=d_xq[t2][:, 1024:2048])
                    if tb >= 2:
                        v_c.wait(e, VC(tb - 2, 1))
                    s_p.dma(e, out=C[:, (tb % 2) * 2048:(tb % 2) * 2048 + 2048],
                            in_=d_acc[tb])
                    v_c.wait(e, VC(tb, 1))
                    s_p.dma(e, out=d_acc[tb],
                            in_=C[:, (tb % 2) * 2048:(tb % 2) * 2048 + 2048])
            blk.gpsimd(p5m_gp)

            def xqsl(tb, k):
                if k < 8:
                    base = 18432 + (tb % 2) * 1024
                    return Bf[:, base + k * 128:base + (k + 1) * 128]
                base = 20480 + (tb % 2) * 1024
                return Bf[:, base + (k - 8) * 128:base + (k - 7) * 128]

            def p5m_tens(e, bva2=bva2, bsp3=bsp3, XQC=XQC, VA=VA, VB=VB,
                         VC=VC, fvb=v_b.n, fvc=v_c.n):
                e.wait_ge(v_b.h, fvb)
                e.wait_ge(v_c.h, fvc)
                for tb in range(NTB):
                    s_p.wait(e, bsp3 + 16 * XQC[tb])
                    for half in range(2):
                        g = 2 * tb + half
                        if g >= 1:
                            v_a.wait(e, VA((g - 1) // 2, (g - 1) % 2))
                        ins = None
                        for k in range(NKC):
                            v_a.wait(e, bva2 + 2 * (16 + k) + 2)
                            lhs = xqsl(tb, k)
                            gt = A[:, k * 1024 + half * 512:
                                   k * 1024 + half * 512 + 512]
                            ut = A[:, 16384 + k * 1024 + half * 512:
                                   16384 + k * 1024 + half * 512 + 512]
                            e.matmul(out=P32[:, 0:512], lhsT=lhs, rhs=gt,
                                     start=(k == 0), stop=(k == NKC - 1))
                            ins = e.matmul(out=P32[:, 512:1024], lhsT=lhs,
                                           rhs=ut, start=(k == 0),
                                           stop=(k == NKC - 1))
                        c_t.inc(ins)
                    v_b.wait(e, VB(tb))
                    if tb >= 1:
                        v_c.wait(e, VC(tb - 1, 0))
                    ins = None
                    for t in range(8):
                        ins = e.transpose(out=P16[:, t * 128:(t + 1) * 128],
                                          in_=Bf[:, 16384 + t * 128:
                                                 16384 + (t + 1) * 128],
                                          identity=IDN[:])
                    c_t.inc(ins)
                    v_c.wait(e, VC(tb, 0))
                    if tb >= 1:
                        v_c.wait(e, VC(tb - 1, 1))
                    ins = None
                    for kq in range(8):
                        v_a.wait(e, bva2 + 2 * (32 + kq) + 2)
                        lhs = Bf[:, 17408 + kq * 128:17408 + (kq + 1) * 128]
                        for t in range(4):
                            rhs = Bf[:, kq * 2048 + t * 512:
                                     kq * 2048 + (t + 1) * 512]
                            ins = e.matmul(out=P32[:, 1024 + t * 512:
                                                  1024 + (t + 1) * 512],
                                           lhsT=lhs, rhs=rhs,
                                           start=(kq == 0), stop=(kq == 7))
                    c_t.inc(ins)
            blk.tensor(p5m_tens)

            def p5m_scal(e, sg=sg, CT=CT, CS=CS, VA=VA, VB=VB, bw1n=bw1n,
                         bvb3=bvb3):
                for tb in range(NTB):
                    for half in range(2):
                        c_t.wait(e, CT(tb, half))
                        if half == 1:
                            v_a.wait(e, VA(tb, 0))
                        c_s.inc(e.activation(out=GS[:, 0:512],
                                             in_=P32[:, 0:512], func=AF.Silu))
                    if sg < NSEG - 1:
                        for j in range(5):
                            pm_tile_scal(e, 5 * tb + j, bw1n)
            blk.scalar(p5m_scal)

            def p5m_vec(e, sg=sg, CT=CT, CS=CS, VB=VB, VC=VC, bsp3=bsp3,
                        bw1n=bw1n, RLC=RLC):
                for tb in range(NTB):
                    for half in range(2):
                        c_s.wait(e, CS(tb, half))
                        v_a.inc(e.tensor_tensor(
                            out=SM[:, 1024 + half * 512:1536 + half * 512],
                            in0=P32[:, 512:1024], in1=GS[:, 0:512],
                            op=ALU.mult))
                    ht = SM[:, 1024:2048]
                    f_v.cut(e, e.tensor_reduce(
                        out=SCL[:, 4096:4128],
                        in_=ht.rearrange("p (j b) -> p b j", b=32),
                        axis=AX.X, op=ALU.max, apply_absolute_value=True))
                    e.reciprocal(out=SCL[:, 4128:4160], in_=SCL[:, 4096:4128])
                    f_v.cut(e, e.tensor_scalar(out=SCL[:, 4160:4192],
                                               in0=SCL[:, 4096:4128],
                                               scalar1=1.0 / 224.0,
                                               scalar2=None, op0=ALU.mult))
                    v_a.inc(e.scalar_tensor_tensor(
                        out=Q8[:, 0:1024].rearrange("p (j b) -> p j b", b=32),
                        in0=ht.rearrange("p (j b) -> p j b", b=32),
                        scalar=224.0, op0=ALU.mult,
                        in1=SCL[:, 4128:4160].rearrange(
                            "p (o b) -> p o b", o=1).broadcast_to([128, 32, 32]),
                        op1=ALU.mult))
                    v_b.inc(e.tensor_tensor(
                        out=Bf[:, 16384:17408].rearrange("p (j b) -> p j b", b=32),
                        in0=Q8[:, 0:1024].rearrange("p (j b) -> p j b", b=32),
                        in1=SCL[:, 4160:4192].rearrange(
                            "p (o b) -> p o b", o=1).broadcast_to([128, 32, 32]),
                        op=ALU.mult))
                    c_t.wait(e, CT(tb, 2))
                    ins = None
                    for t in range(8):
                        ins = e.tensor_copy(out=Bf[:, 17408 + t * 128:
                                                  17408 + (t + 1) * 128],
                                            in_=P16[:, t * 128:(t + 1) * 128])
                    v_c.inc(ins)
                    c_t.wait(e, CT(tb, 3))
                    s_p.wait(e, bsp3 + 16 * RLC[tb])
                    cst = C[:, (tb % 2) * 2048:(tb % 2) * 2048 + 2048]
                    v_c.inc(e.scalar_tensor_tensor(
                        out=cst, in0=P32[:, 1024:3072], scalar=1.0, in1=cst,
                        op0=ALU.mult, op1=ALU.add))
                    if sg < NSEG - 1:
                        for j in range(5):
                            pm_tile_vec(e, 5 * tb + j, CS(tb, 2 + j))
                if sg < NSEG - 1:
                    pm_folds_max(e)
                    pm_scales(e)
            blk.vector(p5m_vec)

            assert c_t.n == bct3 + 4 * NTB
            assert c_s.n == bcs3 + (56 if sg < NSEG - 1 else 2 * NTB)
            assert v_a.n == bva3 + vstep * NTB
            assert v_b.n == bvb3 + NTB + (1 if sg < NSEG - 1 else 0)
            assert v_c.n == bvc3 + 2 * NTB and s_p.n == bsp3 + 16 * 32
            ct_guA = bct3 + 4 * (NTB - 1) + 2
            ct_dnB = bct3 + 4 * NTB
            mc_cur = mc_next
            if STOP == 60 + sg:
                _stop_tail()
                return nc

        if STOP == 6:
            _stop_tail()
            return nc

        bx6, bsp6, bva6 = s_x.n, s_p.n, v_a.n
        p6_order = []
        for tb in range(NTB):
            p6_order.append(("L", tb))
            if tb >= 2:
                p6_order.append(("S", tb - 2))
        p6_order += [("S", NTB - 2), ("S", NTB - 1)]
        p6_cum = {}
        cnt6 = 0
        for it6 in p6_order:
            cnt6 += 16
            p6_cum[it6] = bx6 + cnt6

        def p6_sync(e, fva=v_a.n):
            e.wait_ge(v_a.h, fva)
            for kind, tb in p6_order:
                if kind == "L":
                    if tb >= 2:
                        v_a.wait(e, bva6 + (tb - 1))
                    s_x.dma(e, out=W[:, (tb % 2) * 2048:(tb % 2) * 2048 + 2048],
                                in_=inp["xf"][tb * 128:(tb + 1) * 128, :]
                                )
                else:
                    v_a.wait(e, bva6 + (tb + 1))
                    s_x.dma(e, out=out_own[tb * 128:(tb + 1) * 128, :],
                                in_=SM[:, 0:2048])
            s_x.wait(e, s_x.n)
        blk.sync(p6_sync)

        def p6_gp(e, fvc=v_c.n):
            e.wait_ge(v_c.h, fvc)
            for tb in range(NTB):
                if tb >= 2:
                    v_a.wait(e, bva6 + (tb - 1))
                s_p.dma(e, out=C[:, (tb % 2) * 2048:(tb % 2) * 2048 + 2048],
                            in_=d_acc[tb])
        blk.gpsimd(p6_gp)

        def p6_vec(e):
            for tb in range(NTB):
                s_x.wait(e, p6_cum[("L", tb)])
                s_p.wait(e, bsp6 + 16 * (tb + 1))
                if tb >= 1:
                    s_x.wait(e, p6_cum[("S", tb - 1)])
                cst = C[:, (tb % 2) * 2048:(tb % 2) * 2048 + 2048]
                xst = W[:, (tb % 2) * 2048:(tb % 2) * 2048 + 2048]
                v_a.inc(e.tensor_tensor(
                    out=SM[:, 0:2048].rearrange("p (b j) -> p b j", j=32),
                    in0=cst.rearrange("p (j b) -> p b j", b=64),
                    in1=xst.rearrange("p (j b) -> p b j", b=64),
                    op=ALU.add))
        blk.vector(p6_vec)

    return nc


def _perms():
    permH = np.empty(D, np.int64)
    for j in range(32):
        for b in range(64):
            permH[j * 64 + b] = 32 * b + j
    rho = np.empty(D, np.int64)
    for h in range(NH):
        for k in range(DK):
            rho[h * DK + k] = h * DK + (2 * k if k < 64 else 2 * (k - 64) + 1)
    permI = np.empty(II, np.int64)
    for sg in range(NSEG):
        for j in range(32):
            for b in range(32):
                permI[sg * SEG + j * 32 + b] = sg * SEG + 32 * b + j
    return permH, rho, permI


def kernel(L0_attn_x, L0_attn_wq, L0_attn_wk, L0_attn_wv, L0_attn_wo,
           L0_ffn_gate_w, L0_ffn_up_w, L0_ffn_down_w):
    x = np.ascontiguousarray(np.asarray(L0_attn_x, np.float32))
    wq = np.asarray(L0_attn_wq, np.float32)
    wk = np.asarray(L0_attn_wk, np.float32)
    wv = np.asarray(L0_attn_wv, np.float32)
    wo = np.asarray(L0_attn_wo, np.float32)
    gw = np.asarray(L0_ffn_gate_w, np.float32)
    uw = np.asarray(L0_ffn_up_w, np.float32)
    dw = np.asarray(L0_ffn_down_w, np.float32)

    permH, rho, permI = _perms()
    wqT = np.ascontiguousarray(wq[rho][:, permH].T)
    wkT = np.ascontiguousarray(wk[rho][:, permH].T)
    wvT = np.ascontiguousarray(wv[:, permH].T)
    woT = np.ascontiguousarray(wo[permH].T)
    gwT = np.ascontiguousarray(gw[permI][:, permH].T)
    uwT = np.ascontiguousarray(uw[permI][:, permH].T)
    dwT = np.ascontiguousarray(dw[permH][:, permI].T)
    tri = np.ascontiguousarray(np.triu(np.ones((128, 128)))).astype(
        ml_dtypes.bfloat16)
    ident = np.eye(128).astype(ml_dtypes.bfloat16)
    inv = (np.float32(10000.0) **
           (-(np.arange(0, DK, 2).astype(np.float32) / np.float32(DK))))

    nc = build_nc()
    in_maps = []
    for c in range(8):
        b, par = divmod(c, 2)
        ordbl = [2 * j + par for j in range(8)] + [2 * j + (1 - par)
                                                  for j in range(8)]
        idx = np.concatenate([np.arange(g * 128, (g + 1) * 128) for g in ordbl])
        xf = np.ascontiguousarray(x[b][idx][:, permH])
        pos = idx.astype(np.float32)
        ang = inv[:, None].astype(np.float32) * pos[None, :].astype(np.float32)
        sn = np.sin(ang).astype(np.float32)
        cs_ = np.cos(ang).astype(np.float32)
        sint = np.ascontiguousarray(np.concatenate([-sn, sn], 0))
        cost = np.ascontiguousarray(np.concatenate([cs_, cs_], 0))
        parf = np.full((128, 1), float(par), np.float32)
        in_maps.append({
            "xf": xf, "wqT": wqT, "wkT": wkT, "wvT": wvT, "woT": woT,
            "gwT": gwT, "uwT": uwT, "dwT": dwT, "sint": sint, "cost": cost,
            "trimask": tri, "parf": parf, "ident": ident,
        })

    res = run_bass_kernel_spmd(nc, in_maps, core_ids=list(range(8)))
    out = np.empty((4, S, D), np.float32)
    for c in range(8):
        b, par = divmod(c, 2)
        o = res.results[c]["out_own"]
        for j in range(8):
            out[b][(2 * j + par) * 128:(2 * j + par + 1) * 128] = \
                o[j * 128:(j + 1) * 128]
    return out


# version 11 (final)
# speedup vs baseline: 1.3298x; 1.0192x over previous
import contextlib
import os
import numpy as np
import ml_dtypes
import concourse.bass as bass
import concourse.mybir as mybir
from concourse.bass_utils import run_bass_kernel_spmd

F32 = mybir.dt.float32
BF16 = mybir.dt.bfloat16
FP8 = mybir.dt.float8e4
ALU = mybir.AluOpType
AF = mybir.ActivationFunctionType
AX = mybir.AxisListType

D = 2048
S = 2048
NH = 16
DK = 128
II = 8192
SEG = 1024
NSEG = II // SEG
TOK = 1024
NTB = TOK // 128
NKC = D // 128
NXT = S // 128
NSP = 16
EPS = 1e-5
RSD = float(1.0 / np.sqrt(np.float64(DK)))
INV448H = 0.5 / 448.0


class Sem:
    __slots__ = ("h", "n")

    def __init__(self, nc, es, name):
        self.h = es.enter_context(nc.semaphore(name))
        self.n = 0

    def inc(self, ins, k=1):
        ins.then_inc(self.h, k)
        self.n += k
        return ins

    def wait(self, eng, v):
        eng.wait_ge(self.h, v)


class Fence:
    __slots__ = ("h", "n")

    def __init__(self, nc, es, name):
        self.h = es.enter_context(nc.semaphore(name))
        self.n = 0

    def cut(self, e, ins):
        ins.then_inc(self.h, 1)
        self.n += 1
        e.wait_ge(self.h, self.n)
        return ins


class Stream:

    def __init__(self, nc, es, name, k=3):
        self.sems = [es.enter_context(nc.semaphore(f"{name}{j}"))
                     for j in range(k)]
        self.k = k
        self.i = 0

    @property
    def n(self):
        return 16 * self.i

    def dma(self, e, out, in_):
        i = self.i
        j = i % self.k
        if i >= self.k:
            e.wait_ge(self.sems[j], 16 * (i // self.k))
        ins = e.dma_start(out=out, in_=in_)
        ins.then_inc(self.sems[j], 16)
        self.i += 1
        return ins

    def wait(self, e, v):
        assert v % 16 == 0
        n = v // 16
        for idx in range(max(0, n - self.k), n):
            e.wait_ge(self.sems[idx % self.k], 16 * (idx // self.k + 1))


class B_:
    pass


def build_nc():
    STOP = int(os.environ.get("K_STOP", "99"))
    SUB = os.environ.get("K_SUB", "full")
    z = B_()
    nc = bass.Bass("TRN2", target_bir_lowering=False, debug=False)
    z.nc = nc
    inp = {}
    for nm, shp in [("xf", [S, D]), ("wqT", [D, D]), ("wkT", [D, D]),
                    ("wvT", [D, D]), ("woT", [D, D]), ("gwT", [D, II]),
                    ("uwT", [D, II]), ("dwT", [II, D]),
                    ("sint", [128, S]), ("cost", [128, S])]:
        inp[nm] = nc.dram_tensor(nm, shp, F32, kind="ExternalInput").ap()
    inp["trimask"] = nc.dram_tensor("trimask", [128, 128], BF16, kind="ExternalInput").ap()
    inp["parf"] = nc.dram_tensor("parf", [128, 1], F32, kind="ExternalInput").ap()
    inp["ident"] = nc.dram_tensor("ident", [128, 128], BF16, kind="ExternalInput").ap()
    out_own = nc.dram_tensor("out_own", [TOK, D], F32, kind="ExternalOutput").ap()
    d_k = nc.dram_tensor("d_k", [NH, 128, S], BF16, kind="Internal").ap()
    d_q = nc.dram_tensor("d_q", [NH, 128, TOK], BF16, kind="Internal").ap()
    d_v = nc.dram_tensor("d_v", [S, D], BF16, kind="Internal").ap()
    d_xn = nc.dram_tensor("d_xn", [NTB, 128, D], F32, kind="Internal").ap()
    d_xq = nc.dram_tensor("d_xq", [NTB, 128, NKC * 128], BF16, kind="Internal").ap()
    d_acc = nc.dram_tensor("d_acc", [NTB, 128, D], F32, kind="Internal").ap()

    es = contextlib.ExitStack()
    with es:
        A = es.enter_context(nc.sbuf_tensor("Ar", [128, 32768], BF16))
        Bf = es.enter_context(nc.sbuf_tensor("Br", [128, 24576], BF16))
        C = es.enter_context(nc.sbuf_tensor("Cr", [128, 4096], F32))
        W = es.enter_context(nc.sbuf_tensor("Wr", [128, 4096], F32))
        SM = es.enter_context(nc.sbuf_tensor("SMr", [128, 2048], F32))
        SCL = es.enter_context(nc.sbuf_tensor("SCLr", [128, 8192], F32))
        GS = es.enter_context(nc.sbuf_tensor("GSr", [128, 512], F32))
        Q8 = es.enter_context(nc.sbuf_tensor("Q8r", [128, 2048], FP8))
        TRI = es.enter_context(nc.sbuf_tensor("TRIr", [128, 128], BF16))
        PARF = es.enter_context(nc.sbuf_tensor("PARFr", [128, 1], F32))
        ONE1 = es.enter_context(nc.sbuf_tensor("ONE1r", [128, 1], BF16))
        EPSC = es.enter_context(nc.sbuf_tensor("EPSCr", [128, 1], F32))
        ONER = es.enter_context(nc.sbuf_tensor("ONERr", [1, 128], BF16))
        IDN = es.enter_context(nc.sbuf_tensor("IDNr", [128, 128], BF16))
        P32 = es.enter_context(nc.psum_tensor("P32r", [128, 4096], F32))
        P16 = P32[:, 3584:4096].bitcast(BF16)

        s_x = Stream(nc, es, "sx")
        s_p = Stream(nc, es, "sp")
        c_s = Sem(nc, es, "c_s")
        c_t = Sem(nc, es, "c_t")
        v_a = Sem(nc, es, "v_a")
        v_b = Sem(nc, es, "v_b")
        v_c = Sem(nc, es, "v_c")
        f_v = Fence(nc, es, "f_v")

        blk = es.enter_context(nc.Block())

        def _stop_tail():
            dump = os.environ.get("K_DUMP", "")

            def t_sync(e):
                for sm in (c_s, c_t, v_a, v_b, v_c):
                    e.wait_ge(sm.h, sm.n)
                s_p.wait(e, s_p.n)
                s_x.wait(e, s_x.n)
                if dump == "xn":
                    for j in range(8):
                        s_x.dma(e, out=out_own[j * 128:(j + 1) * 128, :],
                                in_=d_xn[j])
                elif dump == "acc":
                    for j in range(8):
                        s_x.dma(e, out=out_own[j * 128:(j + 1) * 128, :],
                                in_=d_acc[j])
                elif dump == "q":
                    for j in range(8):
                        s_x.dma(e, out=out_own[j * 128:(j + 1) * 128, 0:512]
                                .bitcast(BF16), in_=d_q[2 * j])
                elif dump == "k":
                    for j in range(8):
                        s_x.dma(e, out=out_own[j * 128:(j + 1) * 128, 0:1024]
                                .bitcast(BF16), in_=d_k[2 * j])
                elif dump == "v":
                    for j in range(8):
                        s_x.dma(e, out=out_own[j * 128:(j + 1) * 128, 0:1024]
                                .bitcast(BF16), in_=d_v[j * 128:(j + 1) * 128, :])
                elif dump == "xq":
                    for j in range(8):
                        s_x.dma(e, out=out_own[j * 128:(j + 1) * 128, 0:1024]
                                .bitcast(BF16), in_=d_xq[j])
                else:
                    s_x.dma(e, out=out_own[0:128, :], in_=SM[:, 0:2048])
                s_x.wait(e, s_x.n)
            blk.sync(t_sync)

        def p0_sync(e):
            s_x.dma(e, out=C[:, 0:2048], in_=inp["sint"])
            s_x.dma(e, out=C[:, 2048:4096], in_=inp["cost"])
            s_x.dma(e, out=TRI[:], in_=inp["trimask"])
            s_x.dma(e, out=PARF[:], in_=inp["parf"])
            s_x.dma(e, out=IDN[:], in_=inp["ident"])
        blk.sync(p0_sync)

        def p0_vec(e):
            e.memset(ONE1[:], 1.0)
            e.memset(EPSC[:], EPS)
            v_a.inc(e.memset(ONER[:], 1.0))
        blk.vector(p0_vec)

        if STOP == 0:
            _stop_tail()
            return nc

        bx, bva, bcs, bvb, bct, bvc, bsp = (s_x.n, v_a.n, c_s.n, v_b.n,
                                            c_t.n, v_c.n, s_p.n)

        def p1_sync(e):
            for i in range(NXT):
                if i >= 2:
                    v_b.wait(e, bvb + (i - 1))
                    if SUB not in ("ln", "tp") and i - 2 < NTB:
                        s_p.wait(e, bsp + 16 * (i - 1))
                s_x.dma(e, out=W[:, (i % 2) * 2048:(i % 2) * 2048 + 2048],
                            in_=inp["xf"][i * 128:(i + 1) * 128, :])
        blk.sync(p1_sync)

        def p1_evac(e, G):
            i, g = G // 2, G % 2
            c_t.wait(e, bct + G + 1)
            ins = None
            for t in range(8):
                kc = g * 8 + t
                ins = e.tensor_copy(
                    out=A[:, kc * 2048 + i * 128:kc * 2048 + (i + 1) * 128],
                    in_=P16[:, t * 128:(t + 1) * 128])
            v_c.inc(ins)

        def p1_vec(e):
            for i in range(NXT + 1):
                if i < NXT:
                    xt = W[:, (i % 2) * 2048:(i % 2) * 2048 + 2048]
                    st = (i % 2) * 8
                    s_x.wait(e, bx + 16 * (i + 1))
                    if SUB != "ln" and i >= 2:
                        c_t.wait(e, bct + 2 * (i - 2) + 2)
                    f_v.cut(e, e.tensor_reduce(out=SCL[:, st:st + 1], in_=xt,
                                               axis=AX.X, op=ALU.add))
                    f_v.cut(e, e.tensor_scalar(out=SCL[:, st + 1:st + 2],
                                               in0=SCL[:, st:st + 1],
                                               scalar1=1.0 / 2048.0,
                                               scalar2=None, op0=ALU.mult))
                    e.tensor_scalar(out=xt, in0=xt, scalar1=SCL[:, st + 1:st + 2],
                                    scalar2=None, op0=ALU.subtract)
                    e.tensor_tensor(out=SM[:, 0:2048], in0=xt, in1=xt, op=ALU.mult)
                    v_a.inc(e.tensor_reduce(out=SCL[:, st + 2:st + 3],
                                            in_=SM[:, 0:2048], axis=AX.X, op=ALU.add))
                    c_s.wait(e, bcs + (i + 1))
                    e.tensor_scalar(out=xt, in0=xt, scalar1=SCL[:, st + 4:st + 5],
                                    scalar2=None, op0=ALU.mult)
                    bb = Bf[:, (i % 2) * 2048:(i % 2) * 2048 + 2048]
                    v_b.inc(e.tensor_copy(out=bb, in_=xt))
                if SUB != "ln" and i >= 1:
                    p1_evac(e, 2 * (i - 1))
                    p1_evac(e, 2 * (i - 1) + 1)
        blk.vector(p1_vec)

        if SUB == "ln" and STOP == 1:
            def p1_scal0(e):
                for i in range(NXT):
                    st = (i % 2) * 8
                    v_a.wait(e, bva + (i + 1))
                    e.activation(out=SCL[:, st + 3:st + 4], in_=SCL[:, st + 2:st + 3],
                                 func=AF.Ln, bias=EPSC[:, 0:1], scale=1.0 / 2048.0)
                    c_s.inc(e.activation(out=SCL[:, st + 4:st + 5],
                                         in_=SCL[:, st + 3:st + 4],
                                         func=AF.Exp, scale=-0.5))
            blk.scalar(p1_scal0)
            _stop_tail()
            return nc

        def p1_scal(e):
            for i in range(NXT):
                st = (i % 2) * 8
                v_a.wait(e, bva + (i + 1))
                e.activation(out=SCL[:, st + 3:st + 4], in_=SCL[:, st + 2:st + 3],
                             func=AF.Ln, bias=EPSC[:, 0:1], scale=1.0 / 2048.0)
                c_s.inc(e.activation(out=SCL[:, st + 4:st + 5], in_=SCL[:, st + 3:st + 4],
                                     func=AF.Exp, scale=-0.5))
        blk.scalar(p1_scal)

        def p1_tens(e):
            for i in range(NXT):
                bb = Bf[:, (i % 2) * 2048:(i % 2) * 2048 + 2048]
                v_b.wait(e, bvb + (i + 1))
                for g in range(2):
                    if 2 * i + g >= 1:
                        v_c.wait(e, bvc + 2 * i + g)
                    ins = None
                    for t in range(8):
                        kc = g * 8 + t
                        ins = e.transpose(out=P16[:, t * 128:(t + 1) * 128],
                                          in_=bb[:, kc * 128:(kc + 1) * 128],
                                          identity=IDN[:])
                    c_t.inc(ins)
        blk.tensor(p1_tens)

        def p1_gp(e):
            for i in range(NTB):
                v_b.wait(e, bvb + (i + 1))
                s_p.dma(e, out=d_xn[i],
                            in_=W[:, (i % 2) * 2048:(i % 2) * 2048 + 2048]
                            )
        blk.gpsimd(p1_gp)

        if SUB == "full":
            assert v_a.n == bva + NXT and c_s.n == bcs + NXT
            assert v_b.n == bvb + NXT and c_t.n == bct + 2 * NXT
            assert v_c.n == bvc + 2 * NXT and s_p.n == bsp + 16 * NTB
        if STOP == 1:
            _stop_tail()
            return nc

        for wnm, kind in [("wkT", "K"), ("wvT", "V"), ("wqT", "Q")]:
            for half in range(2):
                bw, bcs, bct, bvb, bsp = s_x.n, c_s.n, c_t.n, v_b.n, s_p.n

                def p2_sync(e, wnm=wnm, half=half, bcs=bcs,
                            fvb=v_b.n, fsp=s_p.n):
                    e.wait_ge(v_b.h, fvb)
                    s_p.wait(e, fsp)
                    e.wait_ge(c_s.h, bcs)
                    for k in range(NKC):
                        if k >= 4:
                            c_s.wait(e, bcs + (k - 3))
                        s_x.dma(e, 
                            out=W[:, (k % 4) * 1024:(k % 4) * 1024 + 1024],
                            in_=inp[wnm][k * 128:(k + 1) * 128,
                                         half * 1024:half * 1024 + 1024]
                            )
                blk.sync(p2_sync)

                def p2_scal(e, bw=bw, fct=c_t.n):
                    e.wait_ge(c_t.h, fct)
                    for k in range(NKC):
                        s_x.wait(e, bw + 16 * (k + 1))
                        c_s.inc(e.activation(
                            out=Bf[:, k * 1024:(k + 1) * 1024],
                            in_=W[:, (k % 4) * 1024:(k % 4) * 1024 + 1024],
                            func=AF.Copy))
                blk.scalar(p2_scal)

                if kind in ("K", "Q"):
                    ntc = (S if kind == "K" else TOK) // 512

                    def p2_tens(e, ntc=ntc, bcs=bcs, bvb=bvb):
                        e.wait_ge(v_b.h, bvb)
                        for hh in range(8):
                            if hh >= 1:
                                v_b.wait(e, bvb + hh)
                            ins = None
                            for k in range(NKC):
                                c_s.wait(e, bcs + (k + 1))
                                lhs = Bf[:, k * 1024 + hh * 128:k * 1024 + (hh + 1) * 128]
                                for t in range(ntc):
                                    rhs = A[:, k * 2048 + t * 512:k * 2048 + (t + 1) * 512]
                                    ins = e.matmul(out=P32[:, t * 512:(t + 1) * 512],
                                                   lhsT=lhs, rhs=rhs,
                                                   start=(k == 0), stop=(k == NKC - 1))
                            c_t.inc(ins)
                    blk.tensor(p2_tens)

                    def p2_vec(e, ntc=ntc, bct=bct, bsp=bsp):
                        sin = C[:, 0:2048]
                        cos = C[:, 2048:4096]
                        for hh in range(8):
                            c_t.wait(e, bct + hh + 1)
                            if hh >= 2:
                                s_p.wait(e, bsp + 16 * (hh - 1))
                            stg = Bf[:, 16384 + (hh % 2) * 2048:
                                     16384 + (hh % 2) * 2048 + ntc * 512]
                            ins = None
                            for t in range(ntc):
                                ps = P32[:, t * 512:(t + 1) * 512]
                                tok0 = t * 512
                                s1 = SM[:, 0:512]
                                t2 = SM[:, 512:1024]
                                e.tensor_tensor(out=s1[0:64, :], in0=ps[64:128, :],
                                                in1=sin[0:64, tok0:tok0 + 512], op=ALU.mult)
                                e.tensor_tensor(out=s1[64:128, :], in0=ps[0:64, :],
                                                in1=sin[64:128, tok0:tok0 + 512], op=ALU.mult)
                                e.tensor_tensor(out=t2, in0=ps,
                                                in1=cos[:, tok0:tok0 + 512], op=ALU.mult)
                                ins = e.tensor_tensor(out=stg[:, t * 512:(t + 1) * 512],
                                                      in0=t2, in1=s1, op=ALU.add)
                            v_b.inc(ins)
                    blk.vector(p2_vec)

                    def p2_gp(e, kind=kind, half=half, ntc=ntc, bvb=bvb):
                        dst = d_k if kind == "K" else d_q
                        for hh in range(8):
                            v_b.wait(e, bvb + hh + 1)
                            stg = Bf[:, 16384 + (hh % 2) * 2048:
                                     16384 + (hh % 2) * 2048 + ntc * 512]
                            s_p.dma(e, out=dst[half * 8 + hh], in_=stg
                                        )
                    blk.gpsimd(p2_gp)
                    nu = 8
                else:
                    def p2_tens(e, bcs=bcs, bvb=bvb):
                        e.wait_ge(v_b.h, bvb)
                        for tb in range(NXT):
                            if tb >= 2:
                                v_b.wait(e, bvb + tb - 1)
                            pc = (tb % 2) * 1024
                            ins = None
                            for k in range(NKC):
                                c_s.wait(e, bcs + (k + 1))
                                lhs = A[:, k * 2048 + tb * 128:k * 2048 + (tb + 1) * 128]
                                for t in range(2):
                                    rhs = Bf[:, k * 1024 + t * 512:k * 1024 + (t + 1) * 512]
                                    ins = e.matmul(
                                        out=P32[:, pc + t * 512:pc + (t + 1) * 512],
                                        lhsT=lhs, rhs=rhs,
                                        start=(k == 0), stop=(k == NKC - 1))
                            c_t.inc(ins)
                    blk.tensor(p2_tens)

                    def p2_scal2(e, bct=bct, bsp=bsp):
                        for tb in range(NXT):
                            c_t.wait(e, bct + tb + 1)
                            if tb >= 2:
                                s_p.wait(e, bsp + 16 * (tb - 1))
                            pc = (tb % 2) * 1024
                            stg = Bf[:, 20480 + (tb % 2) * 1024:
                                     20480 + (tb % 2) * 1024 + 1024]
                            v_b.inc(e.activation(out=stg, in_=P32[:, pc:pc + 1024],
                                                 func=AF.Copy))
                    blk.scalar(p2_scal2)

                    def p2_gp(e, half=half, bvb=bvb):
                        for tb in range(NXT):
                            v_b.wait(e, bvb + tb + 1)
                            stg = Bf[:, 20480 + (tb % 2) * 1024:
                                     20480 + (tb % 2) * 1024 + 1024]
                            s_p.dma(e, out=d_v[tb * 128:(tb + 1) * 128,
                                                half * 1024:half * 1024 + 1024],
                                        in_=stg)
                    blk.gpsimd(p2_gp)
                    nu = NXT

                assert c_t.n == bct + nu and v_b.n == bvb + nu
                assert s_p.n == bsp + 16 * nu

        if STOP == 2:
            _stop_tail()
            return nc

        bsr, bct, bcs, bva, bvb, bvc = s_x.n, c_t.n, c_s.n, v_a.n, v_b.n, v_c.n
        sp_qb0 = [sp % 8 for sp in range(NSP)]
        sp_nq = [8 - q for q in sp_qb0]

        ct = {}
        n = bct
        for h in range(NH):
            n += 1; ct[("rows", h)] = n
            n += 1; ct[("bbc", h)] = n
            for sp in range(NSP):
                n += 1; ct[("sc", h, sp)] = n
                n += 1; ct[("av", h, sp)] = n
            n += 1; ct[("rbc", h)] = n
        ct_end = n
        cs = {}
        n = bcs
        for h in range(NH):
            n += 1; cs[("sq", h)] = n
            n += 1; cs[("bnd", h)] = n
            for sp in range(NSP):
                n += 1; cs[("exp", h, sp)] = n
        cs_end = n
        va = {}
        n = bva
        for h in range(NH):
            n += 1; va[("mkk", h)] = n
            n += 1; va[("bnd", h)] = n
            for sp in range(NSP):
                n += 1; va[("t", h, sp)] = n
                n += 1; va[("avadd", h, sp)] = n
        va_end = n
        vb = {}
        n = bvb
        for h in range(NH):
            n += 1; vb[("brow", h)] = n
            for sp in range(NSP):
                n += 1; vb[("P", h, sp)] = n
        vb_end = n
        vc = {}
        n = bvc
        for h in range(NH):
            n += 1; vc[("rec", h)] = n
            n += 1; vc[("ao", h)] = n
        vc_end = n

        def kcol(h):
            return (h % 2) * 2048

        def vvcol(h):
            return 4096 + (h % 2) * 2048

        def qcol(h):
            return 8192 + (h % 2) * 1024

        AACC = SCL[:, 4096:5120]
        DACC = SCL[0:1, 5120:6144]

        def p3_sync(e, fsp=s_p.n, fct=c_t.n, fcs=c_s.n):
            s_p.wait(e, fsp)
            e.wait_ge(c_t.h, fct)
            e.wait_ge(c_s.h, fcs)
            for h in range(NH):
                if h >= 2:
                    c_t.wait(e, ct[("av", h - 2, NSP - 1)])
                    c_s.wait(e, cs[("sq", h - 1)])
                s_x.dma(e, out=Bf[:, kcol(h):kcol(h) + 2048], in_=d_k[h])
                s_x.dma(e, out=Bf[:, vvcol(h):vvcol(h) + 2048].rearrange(
                    "p (sb j) -> p sb j", j=128),
                    in_=d_v[:, h * 128:(h + 1) * 128].rearrange(
                        "(sb p) j -> p sb j", p=128))
                s_x.dma(e, out=Bf[:, qcol(h):qcol(h) + 1024], in_=d_q[h])
        blk.sync(p3_sync)

        def p3_scal(e):
            for h in range(NH):
                s_x.wait(e, bsr + 48 * (h + 1))
                e.activation(out=Bf[:, 12288:14336],
                             in_=Bf[:, kcol(h):kcol(h) + 2048], func=AF.Square)
                c_s.inc(e.activation(out=Bf[:, 14336:15360],
                                     in_=Bf[:, qcol(h):qcol(h) + 1024],
                                     func=AF.Square))
                v_a.wait(e, va[("mkk", h)])
                c_s.inc(e.activation(out=SCL[0:1, 3072:4096],
                                     in_=SCL[0:1, 2048:3072], func=AF.Sqrt))
                for sp in range(NSP):
                    nq = sp_nq[sp]
                    v_a.wait(e, va[("t", h, sp)])
                    if sp >= 2:
                        c_t.wait(e, ct[("av", h, sp - 2)])
                    elif h >= 1:
                        c_t.wait(e, ct[("av", h - 1, NSP - 2 + sp)])
                    tst = SM[:, (sp % 2) * 1024:(sp % 2) * 1024 + nq * 128]
                    pst = Bf[:, 10240 + (sp % 2) * 1024:
                             10240 + (sp % 2) * 1024 + nq * 128]
                    c_s.inc(e.activation(out=pst, in_=tst, func=AF.Exp))
        blk.scalar(p3_scal)

        def p3_tens(e, fvb=v_b.n, fvc=v_c.n):
            e.wait_ge(v_b.h, fvb)
            e.wait_ge(v_c.h, fvc)
            for h in range(NH):
                c_s.wait(e, cs[("sq", h)])
                if h >= 1:
                    v_c.wait(e, vc[("ao", h - 1)])
                ins = None
                for t in range(4):
                    ins = e.matmul(out=P32[0:1, t * 512:(t + 1) * 512], lhsT=ONE1[:],
                                   rhs=Bf[:, 12288 + t * 512:12288 + (t + 1) * 512],
                                   start=True, stop=True)
                for t in range(2):
                    ins = e.matmul(out=P32[0:1, 2048 + t * 512:2048 + (t + 1) * 512],
                                   lhsT=ONE1[:],
                                   rhs=Bf[:, 14336 + t * 512:14336 + (t + 1) * 512],
                                   start=True, stop=True)
                c_t.inc(ins)
                v_b.wait(e, vb[("brow", h)])
                ins = None
                for t in range(2):
                    ins = e.matmul(out=P32[:, t * 512:(t + 1) * 512], lhsT=ONER[:],
                                   rhs=Bf[0:1, 15360 + t * 512:15360 + (t + 1) * 512],
                                   start=True, stop=True)
                c_t.inc(ins)
                for sp in range(NSP):
                    q0, nq = sp_qb0[sp], sp_nq[sp]
                    if sp < 2:
                        v_a.wait(e, va[("bnd", h)])
                    else:
                        v_a.wait(e, va[("avadd", h, sp - 2)])
                    ksl = Bf[:, kcol(h) + sp * 128:kcol(h) + (sp + 1) * 128]
                    pcol = (sp % 2) * 1024
                    ins = None
                    nmm = (nq * 128 + 511) // 512
                    for t in range(nmm):
                        w0 = t * 512
                        w1 = min(nq * 128, w0 + 512)
                        rhs = Bf[:, qcol(h) + q0 * 128 + w0:qcol(h) + q0 * 128 + w1]
                        ins = e.matmul(out=P32[:, pcol + w0:pcol + w1],
                                       lhsT=ksl, rhs=rhs, start=True, stop=True)
                    c_t.inc(ins)
                    v_b.wait(e, vb[("P", h, sp)])
                    vsl = Bf[:, vvcol(h) + sp * 128:vvcol(h) + (sp + 1) * 128]
                    pb = Bf[:, 10240 + (sp % 2) * 1024:
                            10240 + (sp % 2) * 1024 + nq * 128]
                    vcol0 = 2048 + (sp % 2) * 1024
                    ins = None
                    for t in range(nmm):
                        w0 = t * 512
                        w1 = min(nq * 128, w0 + 512)
                        ins = e.matmul(out=P32[:, vcol0 + w0:vcol0 + w1],
                                       lhsT=vsl, rhs=pb[:, w0:w1],
                                       start=True, stop=True)
                    for t in range(nmm):
                        w0 = t * 512
                        w1 = min(nq * 128, w0 + 512)
                        ins = e.matmul(out=P32[0:1, pcol + w0:pcol + w1],
                                       lhsT=ONE1[:], rhs=pb[:, w0:w1],
                                       start=True, stop=True)
                    c_t.inc(ins)
                v_c.wait(e, vc[("rec", h)])
                ins = None
                for t in range(2):
                    ins = e.matmul(out=P32[:, t * 512:(t + 1) * 512], lhsT=ONER[:],
                                   rhs=Bf[0:1, 16384 + t * 512:16384 + (t + 1) * 512],
                                   start=True, stop=True)
                c_t.inc(ins)
        blk.tensor(p3_tens)

        def p3_vec(e):
            for h in range(NH):
                e.memset(AACC[:, :], 0.0)
                e.memset(DACC[:, :], 0.0)
                c_t.wait(e, ct[("rows", h)])
                f_v.cut(e, e.tensor_reduce(out=SCL[0:1, 1024:1025],
                                           in_=P32[0:1, 0:2048],
                                           axis=AX.X, op=ALU.max))
                v_a.inc(e.tensor_scalar(out=SCL[0:1, 2048:3072],
                                        in0=P32[0:1, 2048:3072],
                                        scalar1=SCL[0:1, 1024:1025],
                                        scalar2=None, op0=ALU.mult))
                c_s.wait(e, cs[("bnd", h)])
                v_b.inc(e.tensor_scalar(out=Bf[0:1, 15360:16384],
                                        in0=SCL[0:1, 3072:4096], scalar1=RSD,
                                        scalar2=-40.0, op0=ALU.mult,
                                        op1=ALU.add))
                c_t.wait(e, ct[("bbc", h)])
                v_a.inc(e.tensor_scalar(out=SCL[:, 0:1024], in0=P32[:, 0:1024],
                                        scalar1=-1.0, scalar2=None, op0=ALU.mult))

                def avadd(sp):
                    q0n, nqn = sp_qb0[sp], sp_nq[sp]
                    c_t.wait(e, ct[("av", h, sp)])
                    vcol0 = 2048 + (sp % 2) * 1024
                    pcol = (sp % 2) * 1024
                    e.scalar_tensor_tensor(
                        out=AACC[:, q0n * 128:1024],
                        in0=P32[:, vcol0:vcol0 + nqn * 128], scalar=1.0,
                        in1=AACC[:, q0n * 128:1024],
                        op0=ALU.mult, op1=ALU.add)
                    v_a.inc(e.scalar_tensor_tensor(
                        out=DACC[0:1, q0n * 128:1024],
                        in0=P32[0:1, pcol:pcol + nqn * 128], scalar=1.0,
                        in1=DACC[0:1, q0n * 128:1024],
                        op0=ALU.mult, op1=ALU.add))

                for sp in range(NSP):
                    q0, nq = sp_qb0[sp], sp_nq[sp]
                    c_t.wait(e, ct[("sc", h, sp)])
                    pcol = (sp % 2) * 1024
                    tst = SM[:, (sp % 2) * 1024:(sp % 2) * 1024 + nq * 128]
                    v_a.inc(e.scalar_tensor_tensor(
                        out=tst, in0=P32[:, pcol:pcol + nq * 128], scalar=RSD,
                        in1=SCL[:, q0 * 128:q0 * 128 + nq * 128],
                        op0=ALU.mult, op1=ALU.add))
                    c_s.wait(e, cs[("exp", h, sp)])
                    pb = Bf[:, 10240 + (sp % 2) * 1024:
                            10240 + (sp % 2) * 1024 + nq * 128]
                    if sp < 8:
                        ins = e.tensor_tensor(out=pb[:, 0:128], in0=pb[:, 0:128],
                                              in1=TRI[:], op=ALU.mult)
                    else:
                        ins = e.tensor_scalar(out=pb[:, 0:128], in0=pb[:, 0:128],
                                              scalar1=PARF[:, 0:1], scalar2=None,
                                              op0=ALU.mult)
                    v_b.inc(ins)
                    avadd(sp)
                f_v.cut(e, e.engine_nop())
                e.reciprocal(out=DACC[:, :], in_=DACC[:, :])
                v_c.inc(e.tensor_copy(out=Bf[0:1, 16384:17408], in_=DACC[:, :]))
                c_t.wait(e, ct[("rbc", h)])
                e.tensor_copy(out=SM[:, 0:1024], in_=P32[:, 0:1024])
                v_c.inc(e.tensor_tensor(out=A[:, h * 1024:(h + 1) * 1024],
                                        in0=AACC[:, :], in1=SM[:, 0:1024],
                                        op=ALU.mult))
        blk.vector(p3_vec)

        assert c_t.n == ct_end and c_s.n == cs_end
        assert v_a.n == va_end and v_b.n == vb_end and v_c.n == vc_end

        if STOP == 3:
            _stop_tail()
            return nc

        for half in range(2):
            bw, bcs, bct, bvb, bsp = s_x.n, c_s.n, c_t.n, v_b.n, s_p.n

            def p4_sync(e, half=half, bcs=bcs, fcs=c_s.n, fvc=v_c.n):
                e.wait_ge(c_s.h, fcs)
                e.wait_ge(v_c.h, fvc)
                for k in range(8):
                    kk = half * 8 + k
                    if k >= 2:
                        c_s.wait(e, bcs + (k - 1))
                    s_x.dma(e, out=W[:, (k % 2) * 2048:(k % 2) * 2048 + 2048],
                                in_=inp["woT"][kk * 128:(kk + 1) * 128, :]
                                )
            blk.sync(p4_sync)

            def p4_scal(e, bw=bw, fct=c_t.n):
                e.wait_ge(c_t.h, fct)
                for k in range(8):
                    s_x.wait(e, bw + 16 * (k + 1))
                    c_s.inc(e.activation(out=Bf[:, k * 2048:(k + 1) * 2048],
                                         in_=W[:, (k % 2) * 2048:(k % 2) * 2048 + 2048],
                                         func=AF.Copy))
            blk.scalar(p4_scal)

            def p4_tens(e, half=half, bcs=bcs, bvb=bvb, fvc=v_c.n,
                        fva=v_a.n):
                e.wait_ge(v_c.h, fvc)
                e.wait_ge(v_a.h, fva)
                e.wait_ge(v_b.h, bvb)
                for tb in range(NTB):
                    if tb >= 1:
                        v_b.wait(e, bvb + tb)
                    ins = None
                    for k in range(8):
                        c_s.wait(e, bcs + (k + 1))
                        hh = half * 8 + k
                        lhs = A[:, hh * 1024 + tb * 128:hh * 1024 + (tb + 1) * 128]
                        for t in range(4):
                            rhs = Bf[:, k * 2048 + t * 512:k * 2048 + (t + 1) * 512]
                            ins = e.matmul(out=P32[:, t * 512:(t + 1) * 512],
                                           lhsT=lhs, rhs=rhs,
                                           start=(k == 0), stop=(k == 7))
                    c_t.inc(ins)
            blk.tensor(p4_tens)

            def p4_gp(e, half=half, bvb=bvb, bsp=bsp):
                for tb in range(NTB):
                    if half == 1:
                        s_p.dma(e, out=C[:, (tb % 2) * 2048:(tb % 2) * 2048 + 2048],
                                    in_=d_acc[tb])
                    v_b.wait(e, bvb + tb + 1)
                    s_p.dma(e, out=d_acc[tb],
                                in_=C[:, (tb % 2) * 2048:(tb % 2) * 2048 + 2048]
                                )
            blk.gpsimd(p4_gp)

            def p4_vec(e, half=half, bct=bct, bsp=bsp):
                for tb in range(NTB):
                    c_t.wait(e, bct + tb + 1)
                    cst = C[:, (tb % 2) * 2048:(tb % 2) * 2048 + 2048]
                    if half == 0:
                        if tb >= 2:
                            s_p.wait(e, bsp + 16 * (tb - 1))
                        v_b.inc(e.tensor_copy(out=cst, in_=P32[:, 0:2048]))
                    else:
                        s_p.wait(e, bsp + 32 * tb + 16)
                        if tb >= 2:
                            s_p.wait(e, bsp + 32 * (tb - 1))
                        v_b.inc(e.scalar_tensor_tensor(
                            out=cst, in0=P32[:, 0:2048], scalar=1.0, in1=cst,
                            op0=ALU.mult, op1=ALU.add))
            blk.vector(p4_vec)

            assert c_t.n == bct + NTB and v_b.n == bvb + NTB

        if STOP == 4:
            _stop_tail()
            return nc

        bxn, bva, bct, bvc, bsp = s_x.n, v_a.n, c_t.n, v_c.n, s_p.n

        def p5p_sync(e, fcs=c_s.n):
            e.wait_ge(c_s.h, fcs)
            for tb in range(NTB):
                if tb >= 2:
                    v_a.wait(e, bva + (tb - 1))
                s_x.dma(e, out=W[:, (tb % 2) * 2048:(tb % 2) * 2048 + 2048],
                            in_=d_xn[tb])
        blk.sync(p5p_sync)

        def p5p_evac(e, G):
            tb, g = G // 2, G % 2
            c_t.wait(e, bct + G + 1)
            if tb >= 2 and g == 0:
                s_p.wait(e, bsp + 16 * (tb - 1))
            ins = None
            for t in range(8):
                kc = g * 8 + t
                ins = e.tensor_copy(
                    out=Bf[:, 16384 + (tb % 2) * 2048 + kc * 128:
                           16384 + (tb % 2) * 2048 + (kc + 1) * 128],
                    in_=P16[:, t * 128:(t + 1) * 128])
            v_c.inc(ins)

        def p5p_vec(e):
            for tb in range(NTB + 1):
              if tb < NTB:
                xt = W[:, (tb % 2) * 2048:(tb % 2) * 2048 + 2048]
                s_x.wait(e, bxn + 16 * (tb + 1))
                if tb >= 2:
                    c_t.wait(e, bct + 2 * (tb - 2) + 2)
                f_v.cut(e, e.tensor_reduce(
                    out=SCL[:, 4160:4224],
                    in_=xt.rearrange("p (j b) -> p b j", b=64),
                    axis=AX.X, op=ALU.max, apply_absolute_value=True))
                e.reciprocal(out=SCL[:, 4224:4288], in_=SCL[:, 4160:4224])
                f_v.cut(e, e.tensor_scalar(out=SCL[:, 4288:4352],
                                           in0=SCL[:, 4160:4224],
                                           scalar1=1.0 / 224.0, scalar2=None,
                                           op0=ALU.mult))
                quot = SM[:, 0:2048]
                e.scalar_tensor_tensor(
                    out=quot.rearrange("p (j b) -> p j b", b=64),
                    in0=xt.rearrange("p (j b) -> p j b", b=64),
                    scalar=224.0, op0=ALU.mult,
                    in1=SCL[:, 4224:4288].rearrange(
                        "p (o b) -> p o b", o=1).broadcast_to([128, 32, 64]),
                    op1=ALU.mult)
                e.tensor_copy(out=Q8[:, 0:2048], in_=quot)
                xqs = 20480 + (tb % 2) * 2048
                v_a.inc(e.tensor_tensor(
                    out=Bf[:, xqs:xqs + 2048].rearrange("p (j b) -> p j b", b=64),
                    in0=Q8[:, 0:2048].rearrange("p (j b) -> p j b", b=64),
                    in1=SCL[:, 4288:4352].rearrange(
                        "p (o b) -> p o b", o=1).broadcast_to([128, 32, 64]),
                    op=ALU.mult))
              if tb >= 1:
                p5p_evac(e, 2 * (tb - 1))
                p5p_evac(e, 2 * (tb - 1) + 1)
        blk.vector(p5p_vec)

        def p5p_tens(e, fvb=v_b.n):
            e.wait_ge(v_b.h, fvb)
            e.wait_ge(v_c.h, bvc)
            for tb in range(NTB):
                v_a.wait(e, bva + tb + 1)
                for g in range(2):
                    if 2 * tb + g >= 1:
                        v_c.wait(e, bvc + 2 * tb + g)
                    ins = None
                    xqs = 20480 + (tb % 2) * 2048
                    for t in range(8):
                        kc = g * 8 + t
                        ins = e.transpose(
                            out=P16[:, t * 128:(t + 1) * 128],
                            in_=Bf[:, xqs + kc * 128:xqs + (kc + 1) * 128],
                            identity=IDN[:])
                    c_t.inc(ins)
        blk.tensor(p5p_tens)

        def p5p_gp(e):
            for tb in range(NTB):
                v_c.wait(e, bvc + 2 * tb + 2)
                s_p.dma(e, out=d_xq[tb],
                            in_=Bf[:, 16384 + (tb % 2) * 2048:
                                   16384 + (tb % 2) * 2048 + 2048]
                            )
        blk.gpsimd(p5p_gp)

        assert v_a.n == bva + NTB and c_t.n == bct + 2 * NTB
        assert v_c.n == bvc + 2 * NTB and s_p.n == bsp + 16 * NTB

        if STOP == 5:
            _stop_tail()
            return nc

        def pm_wslot(u):
            if u < 32:
                return W[:, (u % 4) * 1024:(u % 4) * 1024 + 1024]
            return W[:, (u % 2) * 2048:(u % 2) * 2048 + 2048]

        def pm_macc(u):
            if u < 16:
                return SCL[:, 0:1024]
            if u < 32:
                return SCL[:, 1024:2048]
            return SCL[:, 2048:4096]

        def pm_tile_scal(e, u, bw):
            s_x.wait(e, bw + 16 * (u + 1))
            c_s.inc(e.activation(out=pm_wslot(u), in_=pm_wslot(u),
                                 func=AF.Abs))

        def pm_tile_vec(e, u, sqcnt):
            c_s.wait(e, sqcnt)
            m = pm_macc(u)
            v_a.inc(e.tensor_tensor(out=m, in0=pm_wslot(u), in1=m,
                                    op=ALU.max))

        def pm_memsets(e):
            e.memset(SCL[:, 0:1024], 0.0)
            e.memset(SCL[:, 1024:2048], 0.0)
            e.memset(SCL[:, 2048:4096], 0.0)

        def pm_folds_max(e):
            for wi in range(2):
                mm = SCL[:, wi * 1024:wi * 1024 + 1024]
                e.tensor_copy(out=SM[0:64, 0:1024], in_=mm[64:128, :])
                e.tensor_tensor(out=mm[0:64, :], in0=SM[0:64, 0:1024],
                                in1=mm[0:64, :], op=ALU.max)
            mm = SCL[:, 2048:4096]
            e.tensor_copy(out=SM[0:64, 0:2048], in_=mm[64:128, :])
            e.tensor_tensor(out=mm[0:64, :], in0=SM[0:64, 0:2048],
                            in1=mm[0:64, :], op=ALU.max)
            e.tensor_copy(out=SM[0:32, 0:2048], in_=mm[32:64, :])
            v_b.inc(e.tensor_tensor(out=mm[0:32, :], in0=SM[0:32, 0:2048],
                                    in1=mm[0:32, :], op=ALU.max))

        def pm_scales(e):
            for wi in range(2):
                mm = SCL[:, wi * 1024:wi * 1024 + 1024]
                rc = SCL[:, 4096 + wi * 1024:4096 + wi * 1024 + 1024]
                e.reciprocal(out=rc[0:64, :], in_=mm[0:64, :])
                e.tensor_scalar(out=rc[0:64, :], in0=rc[0:64, :],
                                scalar1=224.0, scalar2=None, op0=ALU.mult)
                e.tensor_scalar(out=mm[0:64, :], in0=mm[0:64, :],
                                scalar1=1.0 / 224.0, scalar2=None,
                                op0=ALU.mult)
                e.tensor_copy(out=mm[64:128, :], in_=mm[0:64, :])
                e.tensor_copy(out=rc[64:128, :], in_=rc[0:64, :])
            mm = SCL[:, 2048:4096]
            rc = SCL[:, 6144:8192]
            e.reciprocal(out=rc[0:32, :], in_=mm[0:32, :])
            e.tensor_scalar(out=rc[0:32, :], in0=rc[0:32, :],
                            scalar1=224.0, scalar2=None, op0=ALU.mult)
            e.tensor_scalar(out=mm[0:32, :], in0=mm[0:32, :],
                            scalar1=1.0 / 224.0, scalar2=None, op0=ALU.mult)
            e.tensor_copy(out=mm[32:64, :], in_=mm[0:32, :])
            e.tensor_copy(out=mm[64:128, :], in_=mm[0:64, :])
            e.tensor_copy(out=rc[32:64, :], in_=rc[0:32, :])
            e.tensor_copy(out=rc[64:128, :], in_=rc[0:64, :])

        def pm_dma(e, u, sg):
            if u < 32:
                wi, k = divmod(u, 16)
                wnm = "gwT" if wi == 0 else "uwT"
                s_x.dma(e, out=pm_wslot(u),
                        in_=inp[wnm][k * 128:(k + 1) * 128,
                                     sg * SEG:sg * SEG + SEG])
            else:
                k = u - 32
                s_x.dma(e, out=pm_wslot(u),
                        in_=inp["dwT"][(sg * 8 + k) * 128:
                                       (sg * 8 + k + 1) * 128, :])

        V0s = v_a.n
        assert V0s == bva + NTB
        mc_cur = lambda u, V0s=V0s: V0s + u + 1
        bw1_0 = s_x.n

        def p5s_sync0(e, V0s=V0s, mc=mc_cur):
            for u in range(40):
                if u < 2:
                    v_a.wait(e, V0s - 1)
                elif u < 4:
                    v_a.wait(e, V0s)
                elif u < 32:
                    v_a.wait(e, mc(u - 4))
                elif u == 32:
                    v_a.wait(e, mc(29))
                elif u == 33:
                    v_a.wait(e, mc(31))
                else:
                    v_a.wait(e, mc(u - 2))
                pm_dma(e, u, 0)
        blk.sync(p5s_sync0)

        bsq0, bvb0 = c_s.n, v_b.n

        def p5s_scal0(e, bw=bw1_0):
            for u in range(40):
                pm_tile_scal(e, u, bw)
        blk.scalar(p5s_scal0)

        def p5s_vec0(e, bsq0=bsq0):
            pm_memsets(e)
            for u in range(40):
                pm_tile_vec(e, u, bsq0 + u + 1)
            pm_folds_max(e)
            pm_scales(e)
        blk.vector(p5s_vec0)
        assert v_a.n == V0s + 40 and c_s.n == bsq0 + 40
        assert v_b.n == bvb0 + 1

        if STOP == 50:
            _stop_tail()
            return nc

        ct_guA = c_t.n
        ct_dnB = c_t.n
        for sg in range(NSEG):
            bva2, bcs2, bw2 = v_a.n, c_s.n, s_x.n
            mc_p = mc_cur

            def p5q_sync(e, sg=sg, bva2=bva2, mc_p=mc_p):
                for u in range(40):
                    if u < 2:
                        v_a.wait(e, mc_p(38))
                    elif u < 4:
                        v_a.wait(e, mc_p(39))
                    elif u < 32:
                        v_a.wait(e, bva2 + 2 * (u - 4) + 1)
                    elif u <= 33:
                        v_a.wait(e, bva2 + 63)
                    else:
                        v_a.wait(e, bva2 + 2 * (u - 2) + 1)
                    pm_dma(e, u, sg)
            blk.sync(p5q_sync)

            if sg < NSEG - 1:
                bw1n = s_x.n
                V3 = bva2 + 80
                mc_next = lambda u, V3=V3: V3 + 3 * (u // 5 + 1) + u + 1

                def p5s_syncN(e, sg=sg, bva2=bva2, mc=mc_next):
                    for u in range(40):
                        if u < 2:
                            v_a.wait(e, bva2 + 77)
                        elif u < 4:
                            v_a.wait(e, bva2 + 79)
                        elif u < 32:
                            v_a.wait(e, mc(u - 4))
                        elif u == 32:
                            v_a.wait(e, mc(29))
                        elif u == 33:
                            v_a.wait(e, mc(31))
                        else:
                            v_a.wait(e, mc(u - 2))
                        pm_dma(e, u, sg + 1)
                blk.sync(p5s_syncN)
            else:
                bw1n = None
                mc_next = None

            def p5q_vec(e, sg=sg, bw2=bw2, bcs2=bcs2, ct_guA=ct_guA,
                        ct_dnB=ct_dnB):
                for u in range(40):
                    s_x.wait(e, bw2 + 16 * (u + 1))
                    t_ = pm_wslot(u)
                    if u < 32:
                        wi, k = divmod(u, 16)
                        rec = SCL[:, 4096 + wi * 1024:4096 + wi * 1024 + 1024]
                        s2 = SCL[:, wi * 1024:wi * 1024 + 1024]
                        q8 = Q8[:, (u % 2) * 1024:(u % 2) * 1024 + 1024]
                        dst = A[:, wi * 16384 + k * 1024:
                                wi * 16384 + (k + 1) * 1024]
                    else:
                        k = u - 32
                        rec = SCL[:, 6144:8192]
                        s2 = SCL[:, 2048:4096]
                        q8 = Q8[:, 0:2048]
                        dst = Bf[:, k * 2048:(k + 1) * 2048]
                    if u == 0:
                        c_t.wait(e, ct_guA)
                    if u == 32:
                        c_t.wait(e, ct_dnB)
                    v_a.inc(e.tensor_tensor(out=q8, in0=t_, in1=rec,
                                            op=ALU.mult))
                    v_a.inc(e.tensor_tensor(out=dst, in0=q8, in1=s2,
                                            op=ALU.mult))
                if sg < NSEG - 1:
                    pm_memsets(e)
            blk.vector(p5q_vec)
            assert v_a.n == bva2 + 80 and c_s.n == bcs2

            bct3, bcs3, bva3, bvb3, bvc3, bsp3 = (c_t.n, c_s.n, v_a.n,
                                                  v_b.n, v_c.n, s_p.n)
            vstep = 8 if sg < NSEG - 1 else 3
            cstep = 7 if sg < NSEG - 1 else 2
            CT = lambda tb, i, b=bct3: b + 4 * tb + i + 1
            CS = lambda tb, i, b=bcs3, cstep=cstep: b + cstep * tb + i + 1
            VA = lambda tb, i, b=bva3, vstep=vstep: b + vstep * tb + i + 1
            VB = lambda tb, b=bvb3: b + tb + 1
            VC = lambda tb, i, b=bvc3: b + 2 * tb + i + 1
            XQC = [2, 4, 6, 10, 14, 18, 22, 26]
            RLC = [7, 11, 15, 19, 23, 27, 29, 31]

            def p5m_gp(e, CT=CT, VC=VC):
                for tb in range(2):
                    s_p.dma(e, out=Bf[:, 18432 + (tb % 2) * 1024:
                                      18432 + (tb % 2) * 1024 + 1024],
                            in_=d_xq[tb][:, 0:1024])
                    s_p.dma(e, out=Bf[:, 20480 + (tb % 2) * 1024:
                                      20480 + (tb % 2) * 1024 + 1024],
                            in_=d_xq[tb][:, 1024:2048])
                for tb in range(NTB):
                    if tb + 2 < NTB:
                        t2 = tb + 2
                        c_t.wait(e, CT(tb, 1))
                        s_p.dma(e, out=Bf[:, 18432 + (t2 % 2) * 1024:
                                          18432 + (t2 % 2) * 1024 + 1024],
                                in_=d_xq[t2][:, 0:1024])
                        s_p.dma(e, out=Bf[:, 20480 + (t2 % 2) * 1024:
                                          20480 + (t2 % 2) * 1024 + 1024],
                                in_=d_xq[t2][:, 1024:2048])
                    if tb >= 2:
                        v_c.wait(e, VC(tb - 2, 1))
                    s_p.dma(e, out=C[:, (tb % 2) * 2048:(tb % 2) * 2048 + 2048],
                            in_=d_acc[tb])
                    v_c.wait(e, VC(tb, 1))
                    s_p.dma(e, out=d_acc[tb],
                            in_=C[:, (tb % 2) * 2048:(tb % 2) * 2048 + 2048])
            blk.gpsimd(p5m_gp)

            def xqsl(tb, k):
                if k < 8:
                    base = 18432 + (tb % 2) * 1024
                    return Bf[:, base + k * 128:base + (k + 1) * 128]
                base = 20480 + (tb % 2) * 1024
                return Bf[:, base + (k - 8) * 128:base + (k - 7) * 128]

            def p5m_tens(e, bva2=bva2, bsp3=bsp3, XQC=XQC, VA=VA, VB=VB,
                         VC=VC, fvb=v_b.n, fvc=v_c.n):
                e.wait_ge(v_b.h, fvb)
                e.wait_ge(v_c.h, fvc)
                for tb in range(NTB):
                    s_p.wait(e, bsp3 + 16 * XQC[tb])
                    for half in range(2):
                        g = 2 * tb + half
                        if g >= 1:
                            v_a.wait(e, VA((g - 1) // 2, (g - 1) % 2))
                        ins = None
                        for k in range(NKC):
                            v_a.wait(e, bva2 + 2 * (16 + k) + 2)
                            lhs = xqsl(tb, k)
                            gt = A[:, k * 1024 + half * 512:
                                   k * 1024 + half * 512 + 512]
                            ut = A[:, 16384 + k * 1024 + half * 512:
                                   16384 + k * 1024 + half * 512 + 512]
                            e.matmul(out=P32[:, 0:512], lhsT=lhs, rhs=gt,
                                     start=(k == 0), stop=(k == NKC - 1))
                            ins = e.matmul(out=P32[:, 512:1024], lhsT=lhs,
                                           rhs=ut, start=(k == 0),
                                           stop=(k == NKC - 1))
                        c_t.inc(ins)
                    v_b.wait(e, VB(tb))
                    if tb >= 1:
                        v_c.wait(e, VC(tb - 1, 0))
                    ins = None
                    for t in range(8):
                        ins = e.transpose(out=P16[:, t * 128:(t + 1) * 128],
                                          in_=Bf[:, 16384 + t * 128:
                                                 16384 + (t + 1) * 128],
                                          identity=IDN[:])
                    c_t.inc(ins)
                    v_c.wait(e, VC(tb, 0))
                    if tb >= 1:
                        v_c.wait(e, VC(tb - 1, 1))
                    ins = None
                    for kq in range(8):
                        v_a.wait(e, bva2 + 2 * (32 + kq) + 2)
                        lhs = Bf[:, 17408 + kq * 128:17408 + (kq + 1) * 128]
                        for t in range(4):
                            rhs = Bf[:, kq * 2048 + t * 512:
                                     kq * 2048 + (t + 1) * 512]
                            ins = e.matmul(out=P32[:, 1024 + t * 512:
                                                  1024 + (t + 1) * 512],
                                           lhsT=lhs, rhs=rhs,
                                           start=(kq == 0), stop=(kq == 7))
                    c_t.inc(ins)
            blk.tensor(p5m_tens)

            def p5m_scal(e, sg=sg, CT=CT, CS=CS, VA=VA, VB=VB, bw1n=bw1n,
                         bvb3=bvb3):
                for tb in range(NTB):
                    for half in range(2):
                        c_t.wait(e, CT(tb, half))
                        if half == 1:
                            v_a.wait(e, VA(tb, 0))
                        c_s.inc(e.activation(out=GS[:, 0:512],
                                             in_=P32[:, 0:512], func=AF.Silu))
                    if sg < NSEG - 1:
                        for j in range(5):
                            pm_tile_scal(e, 5 * tb + j, bw1n)
            blk.scalar(p5m_scal)

            def p5m_vec(e, sg=sg, CT=CT, CS=CS, VB=VB, VC=VC, bsp3=bsp3,
                        bw1n=bw1n, RLC=RLC):
                for tb in range(NTB):
                    for half in range(2):
                        c_s.wait(e, CS(tb, half))
                        v_a.inc(e.tensor_tensor(
                            out=SM[:, 1024 + half * 512:1536 + half * 512],
                            in0=P32[:, 512:1024], in1=GS[:, 0:512],
                            op=ALU.mult))
                    ht = SM[:, 1024:2048]
                    f_v.cut(e, e.tensor_reduce(
                        out=SCL[:, 4096:4128],
                        in_=ht.rearrange("p (j b) -> p b j", b=32),
                        axis=AX.X, op=ALU.max, apply_absolute_value=True))
                    e.reciprocal(out=SCL[:, 4128:4160], in_=SCL[:, 4096:4128])
                    f_v.cut(e, e.tensor_scalar(out=SCL[:, 4160:4192],
                                               in0=SCL[:, 4096:4128],
                                               scalar1=1.0 / 224.0,
                                               scalar2=None, op0=ALU.mult))
                    v_a.inc(e.scalar_tensor_tensor(
                        out=Q8[:, 0:1024].rearrange("p (j b) -> p j b", b=32),
                        in0=ht.rearrange("p (j b) -> p j b", b=32),
                        scalar=224.0, op0=ALU.mult,
                        in1=SCL[:, 4128:4160].rearrange(
                            "p (o b) -> p o b", o=1).broadcast_to([128, 32, 32]),
                        op1=ALU.mult))
                    v_b.inc(e.tensor_tensor(
                        out=Bf[:, 16384:17408].rearrange("p (j b) -> p j b", b=32),
                        in0=Q8[:, 0:1024].rearrange("p (j b) -> p j b", b=32),
                        in1=SCL[:, 4160:4192].rearrange(
                            "p (o b) -> p o b", o=1).broadcast_to([128, 32, 32]),
                        op=ALU.mult))
                    c_t.wait(e, CT(tb, 2))
                    ins = None
                    for t in range(8):
                        ins = e.tensor_copy(out=Bf[:, 17408 + t * 128:
                                                  17408 + (t + 1) * 128],
                                            in_=P16[:, t * 128:(t + 1) * 128])
                    v_c.inc(ins)
                    c_t.wait(e, CT(tb, 3))
                    s_p.wait(e, bsp3 + 16 * RLC[tb])
                    cst = C[:, (tb % 2) * 2048:(tb % 2) * 2048 + 2048]
                    v_c.inc(e.scalar_tensor_tensor(
                        out=cst, in0=P32[:, 1024:3072], scalar=1.0, in1=cst,
                        op0=ALU.mult, op1=ALU.add))
                    if sg < NSEG - 1:
                        for j in range(5):
                            pm_tile_vec(e, 5 * tb + j, CS(tb, 2 + j))
                if sg < NSEG - 1:
                    pm_folds_max(e)
                    pm_scales(e)
            blk.vector(p5m_vec)

            assert c_t.n == bct3 + 4 * NTB
            assert c_s.n == bcs3 + (56 if sg < NSEG - 1 else 2 * NTB)
            assert v_a.n == bva3 + vstep * NTB
            assert v_b.n == bvb3 + NTB + (1 if sg < NSEG - 1 else 0)
            assert v_c.n == bvc3 + 2 * NTB and s_p.n == bsp3 + 16 * 32
            ct_guA = bct3 + 4 * (NTB - 1) + 2
            ct_dnB = bct3 + 4 * NTB
            mc_cur = mc_next
            if STOP == 60 + sg:
                _stop_tail()
                return nc

        if STOP == 6:
            _stop_tail()
            return nc

        bx6, bsp6, bva6 = s_x.n, s_p.n, v_a.n
        p6_order = []
        for tb in range(NTB):
            p6_order.append(("L", tb))
            if tb >= 2:
                p6_order.append(("S", tb - 2))
        p6_order += [("S", NTB - 2), ("S", NTB - 1)]
        p6_cum = {}
        cnt6 = 0
        for it6 in p6_order:
            cnt6 += 16
            p6_cum[it6] = bx6 + cnt6

        def p6_sync(e, fva=v_a.n):
            e.wait_ge(v_a.h, fva)
            for kind, tb in p6_order:
                if kind == "L":
                    if tb >= 2:
                        v_a.wait(e, bva6 + (tb - 1))
                    s_x.dma(e, out=W[:, (tb % 2) * 2048:(tb % 2) * 2048 + 2048],
                                in_=inp["xf"][tb * 128:(tb + 1) * 128, :]
                                )
                else:
                    v_a.wait(e, bva6 + (tb + 1))
                    s_x.dma(e, out=out_own[tb * 128:(tb + 1) * 128, :],
                                in_=SM[:, 0:2048])
            s_x.wait(e, s_x.n)
        blk.sync(p6_sync)

        def p6_gp(e, fvc=v_c.n):
            e.wait_ge(v_c.h, fvc)
            for tb in range(NTB):
                if tb >= 2:
                    v_a.wait(e, bva6 + (tb - 1))
                s_p.dma(e, out=C[:, (tb % 2) * 2048:(tb % 2) * 2048 + 2048],
                            in_=d_acc[tb])
        blk.gpsimd(p6_gp)

        def p6_vec(e):
            for tb in range(NTB):
                s_x.wait(e, p6_cum[("L", tb)])
                s_p.wait(e, bsp6 + 16 * (tb + 1))
                if tb >= 1:
                    s_x.wait(e, p6_cum[("S", tb - 1)])
                cst = C[:, (tb % 2) * 2048:(tb % 2) * 2048 + 2048]
                xst = W[:, (tb % 2) * 2048:(tb % 2) * 2048 + 2048]
                v_a.inc(e.tensor_tensor(
                    out=SM[:, 0:2048].rearrange("p (b j) -> p b j", j=32),
                    in0=cst.rearrange("p (j b) -> p b j", b=64),
                    in1=xst.rearrange("p (j b) -> p b j", b=64),
                    op=ALU.add))
        blk.vector(p6_vec)

    return nc


def _perms():
    permH = np.empty(D, np.int64)
    for j in range(32):
        for b in range(64):
            permH[j * 64 + b] = 32 * b + j
    rho = np.empty(D, np.int64)
    for h in range(NH):
        for k in range(DK):
            rho[h * DK + k] = h * DK + (2 * k if k < 64 else 2 * (k - 64) + 1)
    permI = np.empty(II, np.int64)
    for sg in range(NSEG):
        for j in range(32):
            for b in range(32):
                permI[sg * SEG + j * 32 + b] = sg * SEG + 32 * b + j
    return permH, rho, permI


def kernel(L0_attn_x, L0_attn_wq, L0_attn_wk, L0_attn_wv, L0_attn_wo,
           L0_ffn_gate_w, L0_ffn_up_w, L0_ffn_down_w):
    x = np.ascontiguousarray(np.asarray(L0_attn_x, np.float32))
    wq = np.asarray(L0_attn_wq, np.float32)
    wk = np.asarray(L0_attn_wk, np.float32)
    wv = np.asarray(L0_attn_wv, np.float32)
    wo = np.asarray(L0_attn_wo, np.float32)
    gw = np.asarray(L0_ffn_gate_w, np.float32)
    uw = np.asarray(L0_ffn_up_w, np.float32)
    dw = np.asarray(L0_ffn_down_w, np.float32)

    permH, rho, permI = _perms()
    wqT = np.ascontiguousarray(wq[rho][:, permH].T)
    wkT = np.ascontiguousarray(wk[rho][:, permH].T)
    wvT = np.ascontiguousarray(wv[:, permH].T)
    woT = np.ascontiguousarray(wo[permH].T)
    gwT = np.ascontiguousarray(gw[permI][:, permH].T)
    uwT = np.ascontiguousarray(uw[permI][:, permH].T)
    dwT = np.ascontiguousarray(dw[permH][:, permI].T)
    tri = np.ascontiguousarray(np.triu(np.ones((128, 128)))).astype(
        ml_dtypes.bfloat16)
    ident = np.eye(128).astype(ml_dtypes.bfloat16)
    inv = (np.float32(10000.0) **
           (-(np.arange(0, DK, 2).astype(np.float32) / np.float32(DK))))

    nc = build_nc()
    in_maps = []
    for c in range(8):
        b, par = divmod(c, 2)
        ordbl = [2 * j + par for j in range(8)] + [2 * j + (1 - par)
                                                  for j in range(8)]
        idx = np.concatenate([np.arange(g * 128, (g + 1) * 128) for g in ordbl])
        xf = np.ascontiguousarray(x[b][idx][:, permH])
        pos = idx.astype(np.float32)
        ang = inv[:, None].astype(np.float32) * pos[None, :].astype(np.float32)
        sn = np.sin(ang).astype(np.float32)
        cs_ = np.cos(ang).astype(np.float32)
        sint = np.ascontiguousarray(np.concatenate([-sn, sn], 0))
        cost = np.ascontiguousarray(np.concatenate([cs_, cs_], 0))
        parf = np.full((128, 1), float(par), np.float32)
        in_maps.append({
            "xf": xf, "wqT": wqT, "wkT": wkT, "wvT": wvT, "woT": woT,
            "gwT": gwT, "uwT": uwT, "dwT": dwT, "sint": sint, "cost": cost,
            "trimask": tri, "parf": parf, "ident": ident,
        })

    res = run_bass_kernel_spmd(nc, in_maps, core_ids=list(range(8)))
    out = np.empty((4, S, D), np.float32)
    for c in range(8):
        b, par = divmod(c, 2)
        o = res.results[c]["out_own"]
        for j in range(8):
            out[b][(2 * j + par) * 128:(2 * j + par + 1) * 128] = \
                o[j * 128:(j + 1) * 128]
    return out
